# version 14
# speedup vs baseline: 1.0293x; 1.0028x over previous
import contextlib

import numpy as np
import ml_dtypes

import concourse.bass as bass
import concourse.mybir as mybir
from concourse.bass_utils import run_bass_kernel_spmd

F32 = mybir.dt.float32
BF16 = mybir.dt.bfloat16
AF = mybir.ActivationFunctionType
ALU = mybir.AluOpType
AX = mybir.AxisListType

BF = ml_dtypes.bfloat16

B, L, D = 4, 4096, 1024
T = 2048
NCORES = 8
NH, HD = 4, 256
CC = 64
NCH = T // CC
HEADS, DH = 8, 64
WIN = 512
HID = 2730
NEG = -1.0e30

VCOLS = 8 * 65
PAY = 4 * 512 + 4 * VCOLS + 8 + 8

DEBUG = 0


def host_tables():
    half = np.arange(0, DH, 2, dtype=np.float64)
    inv_freq = 1.0 / 10000.0 ** (half / DH)
    base = (half + 0.4 * DH) / (1.4 * DH)

    def xpos(positions):
        t = positions.astype(np.float64)
        freqs = t[:, None] * inv_freq[None, :]
        freqs = np.concatenate([freqs, freqs], axis=-1)
        power = (t - L // 2) / float(WIN // 2)
        scale = base[None, :] ** power[:, None]
        scale = np.concatenate([scale, scale], axis=-1)
        return np.cos(freqs), np.sin(freqs), scale

    R = np.zeros((DH, DH), np.float32)
    for dd in range(32):
        R[dd, dd + 32] = -1.0
        R[dd + 32, dd] = 1.0
    R2 = np.zeros((128, 128), np.float32)
    R2[:64, :64] = R
    R2[64:, 64:] = R
    R2_lhsT = np.ascontiguousarray(R2.T)

    a = np.arange(128)
    m_diag = np.where(a[:, None] >= a[None, :], 0.0, NEG).astype(np.float32)
    m_far = np.where(a[:, None] <= a[None, :], 0.0, NEG).astype(np.float32)
    zeros128 = np.zeros((128, 128), np.float32)
    full128 = np.full((128, 128), NEG, np.float32)

    selq = np.zeros((4, 128, 16), np.float32)
    selk = np.zeros((4, 128, 16), np.float32)
    for c in range(4):
        selq[c, 0:64, 2 * c] = 1.0
        selq[c, 64:128, 2 * c + 1] = 1.0
        selk[c, 0:64, 8 + 2 * c] = 1.0
        selk[c, 64:128, 8 + 2 * c + 1] = 1.0
    dselq = np.zeros((8, 128, 8), np.float32)
    dselk = np.zeros((8, 128, 8), np.float32)
    for c in range(8):
        dselq[c, :, c // 2] = 1.0
        dselk[c, :, 4 + c // 2] = 1.0
    ones128 = np.ones((128, 1), np.float32)

    eye64 = np.eye(64, dtype=np.float32)
    lmask64 = np.tril(np.ones((64, 64), np.float32), -1)
    umask64 = np.triu(np.ones((64, 64), np.float32), 1)

    per_core = []
    for c in range(NCORES):
        bb, hh = c // 2, c % 2
        s = hh * T
        cs, sn, sc = xpos(np.arange(s, s + T))

        def fm(tab):
            t2 = tab.T.astype(np.float32)
            return np.concatenate([t2, t2], axis=0)

        cm64 = (np.arange(CC) <= bb).astype(np.float32)
        cmask2 = np.broadcast_to(cm64[None, :], (128, CC)).copy()
        per_core.append(dict(
            t_qc=fm(cs * sc).astype(BF), t_qs=fm(sn * sc).astype(BF),
            t_kc=fm(cs / sc).astype(BF), t_ks=fm(sn / sc).astype(BF),
            m_diag=m_diag.astype(BF), m_far=m_far.astype(BF),
            m_pfar0=(full128 if hh == 0 else m_far).astype(BF),
            m_pmid0=(full128 if hh == 0 else zeros128).astype(BF),
            c_r2=R2_lhsT.astype(BF),
            c_selq=selq.transpose(1, 0, 2).astype(BF),
            c_selk=selk.transpose(1, 0, 2).astype(BF),
            c_dselq=dselq.transpose(1, 0, 2).astype(BF),
            c_dselk=dselk.transpose(1, 0, 2).astype(BF),
            c_ones=ones128.astype(BF),
            c_eye64=eye64.astype(BF),
            c_idbf=np.eye(128, dtype=np.float32).astype(BF),
            c_idf32=np.eye(128, dtype=np.float32),
            c_lmask64=lmask64,
            c_cmask2=cmask2.astype(BF),
            c_oddmask=np.full((128, 1), float(hh), np.float32),
        ))
    return per_core


def permute_w1(w1, b1):
    order = []
    m = 0
    while m * 128 < HID:
        lo, hi = m * 128, min((m + 1) * 128, HID)
        order.extend(range(lo, hi))
        order.extend(range(HID + lo, HID + hi))
        m += 1
    order = np.array(order)
    return w1[:, :, order], b1[:, order]


class Sem:
    def __init__(self, handle, name):
        self.h = handle
        self.name = name
        self.count = 0


class Builder:
    ENGS = ("sync", "gpsimd", "vector", "scalar", "tensor")

    def __init__(self):
        self.nc = bass.Bass("TRN2", target_bir_lowering=False)
        self.ctx = contextlib.ExitStack()
        self.streams = {k: [] for k in self.ENGS}
        self._semno = 0
        self.d = {}
        self.sb = {}
        self.dbg = {}

    def sem(self, name=None):
        self._semno += 1
        nm = name or f"s{self._semno}"
        h = self.ctx.enter_context(self.nc.semaphore(nm))
        return Sem(h, nm)

    def on(self, eng, fn):
        self.streams[eng].append(fn)

    def raw(self, eng, call):
        self.on(eng, call)

    def _names(self, aps):
        out = set()
        for ap in aps:
            if ap is None or isinstance(ap, (int, float)):
                continue
            try:
                out.add(ap.tensor.name)
            except AttributeError:
                pass
        return out

    _NOFENCE = ("EventSemaphore", "NoOp", "Drain", "DMACopy",
                "CollectiveCompute")

    def _track(self, eng, ret):
        li = ret[-1] if isinstance(ret, tuple) and ret else ret
        if li is None or not hasattr(li, "concise"):
            self._lastok[eng] = False
            return
        try:
            op = li.concise().split()[1]
        except Exception:
            self._lastok[eng] = False
            return
        if op == "EventSemaphore":
            return
        if op in self._NOFENCE:
            self._lastok[eng] = False
            return
        self._last[eng] = li
        self._lastok[eng] = True

    def _stat(self, key, rode):
        st = self._relstat.setdefault(key, [0, 0, 0])
        st[0 if rode else 1] += 1

    def _stat2(self, key):
        st = self._relstat.setdefault(key, [0, 0, 0])
        st[2] += 1

    def _relinc(self, e, eng, semh, k, nm="?", v=None):
        li = self._last.get(eng)
        if self._lastok.get(eng) and li is not None:
            prev = self._upd.get(id(li), ())
            if prev == ():
                li.then_inc(semh, k)
                self._upd[id(li)] = (li, semh, v) if v is not None else (li,)
                self._stat((eng, nm), True)
                return li
            if len(prev) == 3:
                e.wait_ge(prev[1], prev[2])
                e.nop().then_inc(semh, k)
                self._stat2((eng, nm))
                return li
        self._stat((eng, nm), False)
        e.drain()
        e.nop().then_inc(semh, k)
        return None

    def _fence(self, e, eng, ln=0):
        hz = self._hz[eng]
        li = self._last.get(eng)
        if self._lastok.get(eng) and li is not None:
            prev = self._upd.get(id(li), ())
            if prev == ():
                li.then_inc(hz.h, 1)
                hz.count += 1
                self._upd[id(li)] = (li, hz.h, hz.count)
                e.wait_ge(hz.h, hz.count)
                self._stat((eng, f"<fence>@{ln}"), True)
                return
            if len(prev) == 3:
                e.wait_ge(prev[1], prev[2])
                self._stat2((eng, f"<fence>@{ln}"))
                return
        self._stat((eng, f"<fence>@{ln}"), False)
        e.drain()

    def release(self, eng, sem, k=1):
        import sys
        fr = sys._getframe(1)
        while fr.f_back is not None and fr.f_code.co_name in (
                "release", "produced", "mark", "release_from"):
            fr = fr.f_back
        nm = f"{sem.name}@{fr.f_lineno}"
        sem.count += k
        v = sem.count
        self.on(eng, lambda e, eng=eng, s=sem.h, k=k, nm=nm, v=v:
                self._relinc(e, eng, s, k, nm, v))
        return v

    def fence_after(self, e, eng, inst):
        hz = self._hz[eng]
        hz.count += 1
        inst.then_inc(hz.h, 1)
        self._upd[id(inst)] = (inst, hz.h, hz.count)
        e.wait_ge(hz.h, hz.count)

    def ve(self, call, reads=(), writes=(), eng="vector"):
        import sys
        ln = sys._getframe(1).f_lineno
        r = self._names(reads) | self._names(writes)
        w = self._names(writes)

        def f(e, r=r, w=w, eng=eng, call=call, ln=ln):
            if (r & self._dirty[eng]) or (w & self._reads[eng]):
                self._fence(e, eng, ln)
                self._dirty[eng] = set()
                self._reads[eng] = set()
            ret = call(e)
            self._dirty[eng] |= w
            self._reads[eng] |= r
            return ret
        self.on(eng, f)

    def se(self, call, reads=(), writes=()):
        self.ve(call, reads, writes, eng="scalar")

    def barrier(self, engs=None):
        engs = engs or self.ENGS
        if not hasattr(self, "_barsem"):
            self._barsem = self.sem("bar")
        s = self._barsem
        s.count += len(engs)
        target = s.count

        def mk(eng):
            def f(e):
                self._relinc(e, eng, s.h, 1, "<barrier>", None)
                e.wait_ge(s.h, target)
                self._dirty[eng] = set()
                self._reads[eng] = set()
            return f
        for eng in engs:
            self.on(eng, mk(eng))

    def replay(self):
        self._dirty = {k: set() for k in self.ENGS}
        self._reads = {k: set() for k in self.ENGS}
        self._last = {k: None for k in self.ENGS}
        self._lastok = {k: False for k in self.ENGS}
        self._upd = {}
        self._relstat = {}
        self._hz = {k: self.sem("hz_" + k) for k in
                    ("vector", "scalar", "tensor", "gpsimd")}
        blk = self.ctx.enter_context(self.nc.Block())
        st = self.streams

        def run(e, eng):
            for f in st[eng]:
                self._track(eng, f(e))

        @blk.sync
        def _(e):
            run(e, "sync")

        @blk.gpsimd
        def _(e):
            run(e, "gpsimd")

        @blk.vector
        def _(e):
            run(e, "vector")

        @blk.scalar
        def _(e):
            run(e, "scalar")

        @blk.tensor
        def _(e):
            run(e, "tensor")

    def din(self, name, shape, dtype=F32):
        ap = self.nc.dram_tensor(name, list(shape), dtype, kind="ExternalInput").ap()
        self.d[name] = ap
        return ap

    def dout(self, name, shape, dtype=F32):
        ap = self.nc.dram_tensor(name, list(shape), dtype, kind="ExternalOutput").ap()
        self.d[name] = ap
        return ap

    def dram(self, name, shape, dtype=F32):
        ap = self.nc.dram_tensor(name, list(shape), dtype).ap()
        self.d[name] = ap
        return ap

    def sbuf(self, name, shape, dtype=BF16):
        t = self.ctx.enter_context(self.nc.sbuf_tensor(name, list(shape), dtype))
        self.sb[name] = t
        return t

    def psum(self, name, shape, dtype=F32):
        return self.ctx.enter_context(self.nc.psum_tensor(name, list(shape), dtype))

    def qdma(self, eng, sem, out_ap, in_ap, **kw):
        prior = sem.count
        sem.count += 16
        v = sem.count

        def f(e, out_ap=out_ap, in_ap=in_ap, s=sem.h, prior=prior, kw=kw):
            if prior:
                e.wait_ge(s, prior)
            e.dma_start(out=out_ap, in_=in_ap, **kw).then_inc(s, 16)
        self.on(eng, f)
        return v

    def dbg_out(self, name, src_ap, shape, dtype=F32, wait_sems=()):
        if not DEBUG:
            return
        o = self.dout("dbg_" + name, shape, dtype)
        s = self.sem()

        def f(e, o=o, src=src_ap, s=s):
            e.dma_start(out=o, in_=src).then_inc(s.h, 16)
            e.wait_ge(s.h, 16)
        self.on("sync", f)
        s.count += 16


def build_program():
    bd = Builder()
    nc, d = bd.nc, bd.d

    bd.din("x_loc", (128, 8, T + 3))
    bd.din("Wq", (D, D)); bd.din("Wk", (D, D)); bd.din("Wv", (D, D))
    bd.din("Wb", (D, NH))
    bd.din("convs", (3, D, 4))
    bd.din("g_col", (128, 2))
    bd.din("Wo", (D, D))
    bd.din("wqkv", (2, D, 3 * WIN))
    bd.din("qs_col", (128, 2, 1)); bd.din("ks_col", (128, 2, 1))
    bd.din("wo_f", (2, WIN, D))
    bd.din("ln_g_col", (128, 2, 8)); bd.din("ln_b_col", (128, 2, 8))
    bd.din("w1p", (2, D, 2 * HID)); bd.din("b1p_col", (128, 2, 44))
    bd.din("w2", (2, HID, D)); bd.din("b2_col", (128, 2, 8))
    for nm, shp, dt in [
        ("t_qc", (128, T), BF16), ("t_qs", (128, T), BF16),
        ("t_kc", (128, T), BF16), ("t_ks", (128, T), BF16),
        ("m_diag", (128, 128), BF16), ("m_far", (128, 128), BF16),
        ("m_pfar0", (128, 128), BF16), ("m_pmid0", (128, 128), BF16),
        ("c_r2", (128, 128), BF16),
        ("c_selq", (128, 4, 16), BF16), ("c_selk", (128, 4, 16), BF16),
        ("c_dselq", (128, 8, 8), BF16), ("c_dselk", (128, 8, 8), BF16),
        ("c_ones", (128, 1), BF16), ("c_eye64", (64, 64), BF16),
        ("c_idbf", (128, 128), BF16),
        ("c_lmask64", (64, 64), F32),
        ("c_cmask2", (128, 64), BF16), ("c_oddmask", (128, 1), F32),
    ]:
        bd.din(nm, shp, dt)

    out = bd.dout("out", (T, D), F32)

    h_stat = bd.dram("h_stat", (16, T))
    h_sd = bd.dram("h_sd", (128, 8 * NCH))
    h_x0 = bd.dram("h_x0", (128, 8 * T), BF16)
    e_in = bd.dram("e_in", (128, PAY), BF16)
    e_out = bd.dram("e_out", (2, 128, PAY), BF16)
    d0_in = bd.dram("d0_in", (128, 8))
    d0_out = bd.dram("d0_out", (2, 128, 8))

    FREE_AR = 16424
    arA = bd.sbuf("arA", (128, FREE_AR))
    arB = bd.sbuf("arB", (128, FREE_AR))
    arC = bd.sbuf("arC", (128, FREE_AR))
    arD = bd.sbuf("arD", (128, FREE_AR))
    wgt = bd.sbuf("wgt", (128, 16416))
    tab4 = bd.sbuf("tab4", (128, 4, 2048))
    st16 = bd.sbuf("st16", (128, 2048), F32)
    sttm = bd.sbuf("sttm", (128, 16, 16), F32)
    sd_sb = bd.sbuf("sd_sb", (128, 8, NCH + 1), F32)
    a_sb = bd.sbuf("a_sb", (128, 8, NCH), F32)
    b_sb = bd.sbuf("b_sb", (128, 8, NCH), F32)
    tt_sb = bd.sbuf("tt_sb", (128, NCH // 2, CC))
    nsc = bd.sbuf("nsc", (128, 12, 64))
    small = {}
    for nm, shp, dt in [
        ("m_diag", (128, 128), BF16), ("m_far", (128, 128), BF16),
        ("m_pfar0", (128, 128), BF16), ("m_pmid0", (128, 128), BF16),
        ("c_r2", (128, 128), BF16),
        ("c_selq", (128, 4, 16), BF16), ("c_selk", (128, 4, 16), BF16),
        ("c_dselq", (128, 8, 8), BF16), ("c_dselk", (128, 8, 8), BF16),
        ("c_ones", (128, 1), BF16), ("c_eye64", (64, 64), BF16),
        ("c_idbf", (128, 128), BF16),
        ("c_lmask64", (64, 64), F32),
        ("c_cmask2", (128, 64), BF16),
        ("c_oddmask", (128, 1), F32),
        ("g_col", (128, 2), F32),
        ("qs_col", (128, 2, 1), F32), ("ks_col", (128, 2, 1), F32),
        ("ln_g_col", (128, 2, 8), F32), ("ln_b_col", (128, 2, 8), F32),
        ("b1p_col", (128, 2, 44), F32), ("b2_col", (128, 2, 8), F32),
        ("stcol", (128, 8), F32),
    ]:
        small[nm] = bd.sbuf("s_" + nm, shp, dt)
    bd.small = small

    from types import SimpleNamespace
    S = SimpleNamespace(**{nm: bd.sem(nm) for nm in
         ("xin", "wg0", "wg1", "tload", "stW", "stB", "prj", "cnv", "act",
          "ten", "vec", "mm2", "ev1", "ev2", "coll", "gdma", "sdma",
          "tp0", "tp1", "outs", "out2", "bx0", "brd")})
    bd.S = S

    cst = bd.sbuf("cst", (128, 1), F32)
    nc.const_aps.register(F32, 1e-5, cst[:, 0:1])
    bd.on("vector", lambda e: e.memset(cst[:, 0:1], 1e-5))
    bd.release("vector", S.vec, 1)

    emit_load_consts(bd)
    bd.raw("sync", lambda e, v=S.tload.count: e.wait_ge(S.tload.h, v))
    bd.barrier()
    emit_deltanet(bd, arA, arB, arC, arD, wgt, tab4)
    emit_ffn(bd, arA, arB, arC, arD, wgt, tab4)
    emit_final(bd, arA, arB, arC, arD)

    bd.replay()
    return bd


def vw2(ar, off, n0, n1):
    return ar[:, off:off + n0 * n1].rearrange("p (a b) -> p a b", a=n0)


class PsumPipe:
    def __init__(self, bd, name, shape, n=2, dtype=F32, scope=None,
                 reg_key=None):
        alloc = scope.enter_context if scope is not None else bd.ctx.enter_context
        self.ts = [alloc(bd.nc.psum_tensor(f"{name}{i}", list(shape), dtype))
                   for i in range(n)]
        self.bd = bd
        self.n = n
        if not hasattr(bd, "pipereg"):
            bd.pipereg = {}
        key = reg_key or name
        if key in bd.pipereg:
            self.prod, self.cons, self.st = bd.pipereg[key]
            assert self.st["n"] == n, (key, n, self.st)
        else:
            self.prod = bd.sem(key + "_p")
            self.cons = bd.sem(key + "_c")
            self.st = {"i": self.prod_i0 if False else 0, "consumed": 0, "n": n}
            bd.pipereg[key] = (self.prod, self.cons, self.st)

    @property
    def i(self):
        return self.st["i"]

    @i.setter
    def i(self, v):
        self.st["i"] = v

    @property
    def consumed(self):
        return self.st["consumed"]

    @consumed.setter
    def consumed(self, v):
        self.st["consumed"] = v

    def begin(self):
        t = self.ts[self.i % self.n]
        if self.i >= self.n:
            need = self.i - self.n + 1
            self.bd.raw("tensor", lambda e, s=self.cons.h, v=need: e.wait_ge(s, v))
        self.i += 1
        return t

    def produced(self):
        self.bd.release("tensor", self.prod, 1)

    def consume(self, eng, fn, reads=(), writes=(), release=True):
        self.consumed += 1
        v = self.consumed
        t = self.ts[(self.consumed - 1) % self.n]
        s = self.prod
        c = self.cons
        if release:
            c.count += 1

        cv = c.count

        def f(e, v=v, t=t, fn=fn, release=release, cv=cv):
            e.wait_ge(s.h, v)
            ret = fn(e, t)
            if release:
                self.bd._track(eng, ret)
                self.bd._relinc(e, eng, c.h, 1, "<cons>", cv)
            return ret
        rr = self.bd._names(list(reads) + [t[:, :]]) | self.bd._names(writes)
        ww = self.bd._names(writes)

        def g(e, rr=rr, ww=ww, eng=eng, f=f):
            if (rr & self.bd._dirty[eng]) or (ww & self.bd._reads[eng]):
                self.bd._fence(e, eng)
                self.bd._dirty[eng] = set()
                self.bd._reads[eng] = set()
            ret = f(e)
            self.bd._dirty[eng] |= ww
            self.bd._reads[eng] |= rr
            return ret
        self.bd.on(eng, g)

    def release_from(self, eng):
        self.bd.release(eng, self.cons, 1)

    def peek(self):
        return self.ts[self.consumed % self.n]

    def wait_in(self, eng):
        self.consumed += 1
        v = self.consumed
        t = self.ts[(self.consumed - 1) % self.n]
        self.bd.on(eng, lambda e, s=self.prod.h, v=v: e.wait_ge(s, v))
        return t


PSTRIPS = [(0, 512), (512, 512), (1024, 512), (1536, 512), (2048, 3)]
STRIPS = [(0, 512), (512, 512), (1024, 512), (1536, 512)]


def emit_load_consts(bd):
    d, S, sm = bd.d, bd.S, bd.small
    def ld(name, dst_ap=None):
        src = d[name]
        dst = (sm[name][...] if dst_ap is None else dst_ap)
        prior = S.tload.count
        S.tload.count += 16

        def f(e, dst=dst, src=src, prior=prior):
            if prior:
                e.wait_ge(S.tload.h, prior)
            e.dma_start(out=dst, in_=src).then_inc(S.tload.h, 16)
        bd.on("sync", f)

    for nm in ("m_diag", "m_far", "m_pfar0", "m_pmid0", "c_r2",
               "c_selq", "c_selk", "c_dselq", "c_dselk", "c_ones",
               "c_eye64", "c_idbf", "c_lmask64", "c_cmask2", "c_oddmask",
               "g_col", "qs_col", "ks_col", "ln_g_col", "ln_b_col",
               "b1p_col", "b2_col"):
        ld(nm)
    convs = bd.sbuf("convs_sb", (128, 3, 8, 4), F32)
    bd.convs_sb = convs
    src = d["convs"].rearrange("w (k p) t -> p w k t", p=128)
    bd.qdma("sync", S.tload, convs[...], src)
    idf = bd.sbuf("idf32", (128, 128), F32)
    bd.idf32 = idf
    eye = bd.din("c_idf32", (128, 128))
    bd.qdma("sync", S.tload, idf[...], eye)


def wload(bd, name, dst_view, src_ap, sem):
    prior = sem.count
    sem.count += 16
    v = sem.count

    def f(e, dst=dst_view, src=src_ap, s=sem.h, prior=prior):
        if prior:
            e.wait_ge(s, prior)
        e.dma_start(out=dst, in_=src).then_inc(s, 16)
    bd.on("gpsimd", f)
    return v


def emit_deltanet(bd, arA, arB, arC, arD, wgt, tab4):
    import dataclasses
    nc, d, S, sm = bd.nc, bd.d, bd.S, bd.small
    ps = contextlib.ExitStack()

    xv = vw2(arA, 0, 8, 2051)
    prj = vw2(arB, 0, 8, 2051)
    qh = vw2(arC, 0, 8, 2048)
    kr = vw2(arD, 0, 8, 2048)
    vv = vw2(arA, 0, 8, 2048)
    vbt = vw2(arB, 0, 16, 1024)
    kbt = vw2(arA, 0, 16, 1024)
    wA = vw2(wgt, 0, 8, 1024)
    wB = vw2(wgt, 8 * 1024, 8, 1024)
    wb_sb = vw2(wgt, 2 * 8 * 1024, 8, 4)

    stB = bd.sb["st16"]
    sttm = bd.sb["sttm"]
    sd = bd.sb["sd_sb"]
    a_sb, b_sb = bd.sb["a_sb"], bd.sb["b_sb"]
    tts = bd.sb["tt_sb"]
    nscr = bd.sb["nsc"]

    def wsrc(nm):
        return d[nm].rearrange("(k p) j -> p k j", p=128)
    v_wb = wload(bd, "Wb", wb_sb[:, :, :], wsrc("Wb"), S.wg1)
    v_xa = bd.qdma("gpsimd", S.xin, xv[:, 0:4, :], d["x_loc"][:, 0:4, :])
    v_xb = bd.qdma("gpsimd", S.xin, xv[:, 4:8, :], d["x_loc"][:, 4:8, :])
    v_wq = wload(bd, "Wq", wA[:, :, :], wsrc("Wq"), S.wg0)
    v_wk = wload(bd, "Wk", wB[:, :, :], wsrc("Wk"), S.wg1)

    def mark(eng, sem):
        return bd.release(eng, sem, 1)

    def wait(eng, sem, v):
        bd.raw(eng, lambda e, s=sem.h, v=v: e.wait_ge(s, v))

    pp_b = PsumPipe(bd, "psb", (64, 512), 2, scope=ps)
    wait("tensor", S.wg1, v_wb)
    wait("tensor", S.xin, v_xa)
    for (t0, tn) in STRIPS:
        pt = pp_b.begin()
        for k in range(8):
            if k == 4:
                wait("tensor", S.xin, v_xb)
            bd.raw("tensor", lambda e, pt=pt, k=k, t0=t0, tn=tn:
                   e.matmul(pt[32:36, 0:tn], wb_sb[:, k, :],
                            xv[:, k, 3 + t0:3 + t0 + tn],
                            start=(k == 0), stop=(k == 7),
                            tile_position=(0, 32)))
        pp_b.produced()
        pp_b.consume("scalar", lambda e, pt, t0=t0, tn=tn: e.activation(
            stB[32:36, t0:t0 + tn], pt[32:36, 0:tn], AF.Sigmoid),
            writes=[stB[:, :]])

    pp_p = PsumPipe(bd, "psp", (128, 512), 2, scope=ps)
    convs = bd.convs_sb

    def do_proj(widx, wbuf, wv, wsem, dst):
        wait("tensor", wsem, wv)
        for f in range(8):
            for (t0, tn) in PSTRIPS:
                pt = pp_p.begin()
                for k in range(8):
                    bd.raw("tensor", lambda e, pt=pt, f=f, k=k, t0=t0, tn=tn,
                           wbuf=wbuf:
                           e.matmul(pt[:, 0:tn],
                                    wbuf[:, k, f * 128:(f + 1) * 128],
                                    xv[:, k, t0:t0 + tn],
                                    start=(k == 0), stop=(k == 7)))
                pp_p.produced()
                pp_p.consume("scalar", lambda e, pt, f=f, t0=t0, tn=tn:
                             e.activation(prj[:, f, t0:t0 + tn], pt[:, 0:tn],
                                          AF.Copy),
                             writes=[prj[:, :, :]])
        av = mark("scalar", S.act)
        wait("vector", S.act, av)
        for cc in range(8):
            w0 = convs[:, widx, cc, 0:1]
            bd.ve(lambda e, cc=cc, w0=w0, dst=dst: e.tensor_scalar(
                dst[:, cc, :], prj[:, cc, 0:2048], w0, None, ALU.mult),
                reads=[prj[:, 0, 0:1]], writes=[dst[:, 0, 0:1]])
            for tau in range(1, 4):
                wc = convs[:, widx, cc, tau:tau + 1]
                bd.ve(lambda e, cc=cc, wc=wc, tau=tau, dst=dst:
                      e.scalar_tensor_tensor(
                          dst[:, cc, :], prj[:, cc, tau:tau + 2048], wc,
                          dst[:, cc, :], ALU.mult, ALU.add),
                      reads=[prj[:, 0, 0:1], dst[:, 0, 0:1]],
                      writes=[dst[:, 0, 0:1]])
        vv_ = mark("vector", S.vec)
        wait("scalar", S.vec, vv_)
        bd.se(lambda e, dst=dst: e.activation(dst[:, :, :], dst[:, :, :], AF.Silu),
              reads=[dst[:, 0, 0:1]], writes=[dst[:, 0, 0:1]])
        return mark("scalar", S.act)

    q_silu = do_proj(0, wA, v_wq, S.wg0, qh)
    tq = mark("tensor", S.ten)
    wait("gpsimd", S.ten, tq)
    v_wv = wload(bd, "Wv", wA[:, :, :], wsrc("Wv"), S.wg0)

    ps_ss4 = [ps.enter_context(nc.psum_tensor(f"ss4_{i}", [8, 512], F32))
              for i in range(4)]

    def do_square(src):
        bd.se(lambda e, src=src: e.activation(prj[:, :, 0:2048], src[:, :, :],
                                              AF.Square),
              reads=[src[:, 0, 0:1]], writes=[prj[:, 0, 0:1]])
        return mark("scalar", S.act)

    def sumsq_mms(sel, start):
        for si, (t0, tn) in enumerate(STRIPS):
            for cc in range(8):
                bd.raw("tensor", lambda e, si=si, cc=cc, t0=t0, tn=tn,
                       sel=sel, st=(start and cc == 0),
                       sp=((not start) and cc == 7):
                       e.matmul(ps_ss4[si][:, 0:tn], sel[:, cc, 0:8],
                                prj[:, cc, t0:t0 + tn], start=st, stop=sp))

    sq_q = do_square(qh)
    wait("tensor", S.act, sq_q)
    sumsq_mms(sm["c_dselq"], True)
    t_sq = mark("tensor", S.ten)

    wait("scalar", S.ten, t_sq)
    k_silu = do_proj(1, wB, v_wk, S.wg1, kr)
    tk = mark("tensor", S.ten)
    wait("gpsimd", S.ten, tk)
    v_wo = wload(bd, "Wo", wB[:, :, :], wsrc("Wo"), S.wg1)

    sq_k = do_square(kr)
    wait("tensor", S.act, sq_k)
    sumsq_mms(sm["c_dselk"], False)
    t_ssk = mark("tensor", S.ten)

    wait("scalar", S.ten, t_ssk)
    v_silu = do_proj(2, wA, v_wv, S.wg0, vv)

    wait("vector", S.ten, t_ssk)
    psB = contextlib.ExitStack()
    for si, (t0, tn) in enumerate(STRIPS):
        bd.ve(lambda e, si=si, t0=t0, tn=tn: e.tensor_copy(
            stB[0:8, t0:t0 + tn], ps_ss4[si][:, 0:tn]),
            writes=[stB[:, :]])
    bd.ve(lambda e: e.reciprocal(stB[0:8, :], stB[0:8, :]),
          reads=[stB[:, :]], writes=[stB[:, :]])
    v_rec = mark("vector", S.vec)
    wait("scalar", S.vec, v_rec)
    bd.se(lambda e: e.activation(stB[0:8, :], stB[0:8, :], AF.Sqrt),
          reads=[stB[:, :]], writes=[stB[:, :]])
    v_inv = mark("scalar", S.act)

    bd.barrier()
    ps.close()
    pp_t = PsumPipe(bd, "pst", (128, 16), 2, dtype=F32, scope=psB)
    wait("tensor", S.act, v_inv)
    for tile in range(16):
        pt = pp_t.begin()
        bd.raw("tensor", lambda e, pt=pt, tile=tile: e.transpose(
            pt[:, 0:8], stB[0:8, tile * 128:(tile + 1) * 128],
            bd.idf32[0:8, 0:8]))
        bd.raw("tensor", lambda e, pt=pt, tile=tile: e.transpose(
            pt[:, 8:12], stB[32:36, tile * 128:(tile + 1) * 128],
            bd.idf32[32:36, 32:36], tile_position=(32, 0)))
        pp_t.produced()
        pp_t.consume("vector", lambda e, pt, tile=tile: e.tensor_copy(
            sttm[:, tile, 0:12], pt[:, 0:12]), writes=[sttm[:, :, :]])
    bd.ve(lambda e: e.tensor_tensor(sttm[:, :, 12:16], sttm[:, :, 8:12],
                                    sttm[:, :, 4:8], ALU.mult),
          reads=[sttm[:, :, :]], writes=[sttm[:, :, :]])
    v_tmr = mark("vector", S.vec)

    wait("sync", S.act, v_inv)
    v_stw = bd.qdma("sync", S.stW, d["h_stat"][0:8, :], stB[0:8, :])
    wait("gpsimd", S.stW, v_stw)

    def bcast4(dst, src_row0):
        src = dataclasses.replace(
            d["h_stat"][src_row0:src_row0 + 4, :],
            ap=[[0, 64], [2048, 4], [1, 2048]])
        bd.qdma("gpsimd", S.brd, dst[0:64, :, :], src)
        bd.qdma("gpsimd", S.brd, dst[64:128, :, :], src)
        return S.brd.count

    v_tabq = bcast4(tab4, 0)
    wait("vector", S.act, q_silu)
    wait("vector", S.brd, v_tabq)
    for cc in range(8):
        bd.ve(lambda e, cc=cc: e.tensor_tensor(
            qh[:, cc, :], qh[:, cc, :], tab4[:, cc // 2, :], ALU.mult),
            reads=[qh[:, 0, 0:1], tab4[:, 0, 0:1]], writes=[qh[:, 0, 0:1]])
    v_qhat = mark("vector", S.vec)

    wait("gpsimd", S.vec, v_qhat)
    v_tabk = bcast4(tab4, 4)

    wait("vector", S.brd, v_tabk)
    wait("vector", S.act, k_silu)
    for cc in range(8):
        bd.ve(lambda e, cc=cc: e.tensor_tensor(
            prj[:, cc, 0:2048], qh[:, cc, :], kr[:, cc, :], ALU.mult),
            reads=[qh[:, 0, 0:1], kr[:, 0, 0:1]], writes=[prj[:, 0, 0:1]])
        bd.ve(lambda e, cc=cc: e.tensor_tensor(
            prj[:, cc, 0:2048], prj[:, cc, 0:2048], tab4[:, cc // 2, :],
            ALU.mult),
            reads=[prj[:, 0, 0:1], tab4[:, 0, 0:1]], writes=[prj[:, 0, 0:1]])
    v_pqk = mark("vector", S.vec)
    pp_qk = PsumPipe(bd, "psqk", (1, 512), 2, scope=psB)
    wait("tensor", S.vec, v_pqk)
    for (t0, tn) in STRIPS:
        pt = pp_qk.begin()
        for cc in range(8):
            bd.raw("tensor", lambda e, pt=pt, cc=cc, t0=t0, tn=tn:
                   e.matmul(pt[:, 0:tn], sm["c_ones"][:, 0:1],
                            prj[:, cc, t0:t0 + tn],
                            start=(cc == 0), stop=(cc == 7)))
        pp_qk.produced()
        pp_qk.consume("vector", lambda e, pt, t0=t0, tn=tn: e.tensor_copy(
            stB[0:1, t0:t0 + tn], pt[0:1, 0:tn]),
            writes=[stB[:, :]])
    v_smrow = mark("vector", S.vec)
    wait("sync", S.vec, v_smrow)
    v_smw = bd.qdma("sync", S.stW, d["h_stat"][12:13, :], stB[0:1, :])

    bd.barrier()
    idbf = sm["c_idbf"]
    pp_tt = PsumPipe(bd, "pstt", (128, 128), 4, dtype=BF16, scope=psB)
    for cc in range(8):
        for tt in range(16):
            pt = pp_tt.begin()
            bd.raw("tensor", lambda e, pt=pt, cc=cc, tt=tt: e.transpose(
                pt[:, :], vv[:, cc, tt * 128:(tt + 1) * 128], idbf[...]))
            pp_tt.produced()
            pp_tt.consume("vector", lambda e, pt, cc=cc, tt=tt: e.tensor_copy(
                vbt[:, tt, cc * 128:(cc + 1) * 128], pt[:, :]),
                writes=[vbt[:, 0, 0:1]])
    for tt in range(16):
        for h in range(4):
            bd.ve(lambda e, tt=tt, h=h: e.tensor_scalar(
                vbt[:, tt, h * 256:(h + 1) * 256],
                vbt[:, tt, h * 256:(h + 1) * 256],
                sttm[:, tt, 8 + h:9 + h], None, ALU.mult),
                reads=[vbt[:, 0, 0:1], sttm[:, 0, 0:1]],
                writes=[vbt[:, 0, 0:1]])
    v_vbt = mark("vector", S.vec)
    wait("tensor", S.vec, v_vbt)
    for cc in range(8):
        for tt in range(16):
            pt = pp_tt.begin()
            bd.raw("tensor", lambda e, pt=pt, cc=cc, tt=tt: e.transpose(
                pt[:, :], kr[:, cc, tt * 128:(tt + 1) * 128], idbf[...]))
            pp_tt.produced()
            pp_tt.consume("vector", lambda e, pt, cc=cc, tt=tt: e.tensor_copy(
                kbt[:, tt, cc * 128:(cc + 1) * 128], pt[:, :]),
                writes=[kbt[:, 0, 0:1]])
    for tt in range(16):
        for h in range(4):
            bd.ve(lambda e, tt=tt, h=h: e.tensor_scalar(
                kbt[:, tt, h * 256:(h + 1) * 256],
                kbt[:, tt, h * 256:(h + 1) * 256],
                sttm[:, tt, 12 + h:13 + h], None, ALU.mult),
                reads=[kbt[:, 0, 0:1], sttm[:, 0, 0:1]],
                writes=[kbt[:, 0, 0:1]])
    v_kbt = mark("vector", S.vec)

    bd.barrier()
    psB.close()
    ps1 = contextlib.ExitStack()
    pp_g = PsumPipe(bd, "pg", (128, 128), 2, scope=ps1)
    pp_y = PsumPipe(bd, "py", (128, 64), 2, scope=ps1)
    pp_x = PsumPipe(bd, "px", (128, 128), 2, scope=ps1)
    pp_wa = PsumPipe(bd, "pwa", (128, 512), 1, scope=ps1)
    pp_wb = PsumPipe(bd, "pwb", (128, 512), 1, scope=ps1)

    n_sl = nscr[:, 0, :]; n_slt = nscr[:, 1, :]
    n_x = nscr[:, 2, :]; n_xt = nscr[:, 3, :]; n_z = nscr[:, 4, :]
    n_mt = nscr[:, 5, :]
    loc = contextlib.ExitStack()

    def lsb(name, shape, dtype=BF16):
        return loc.enter_context(bd.nc.sbuf_tensor(name, list(shape), dtype))
    gsc = lsb("gsc", (128, 128), F32)
    apair = lsb("apair", (128, 64), F32)
    eye2 = lsb("eye2", (128, 64))
    twoi = lsb("twoi", (128, 64))
    lm2 = lsb("lm2", (128, 64), F32)
    ksc = lsb("ksc", (128, 8, 128))
    puw = lsb("puw", (128, 8, 64))
    usc = lsb("usc", (128, 8, 64))
    d0r = lsb("d0r", (128, 24), F32)

    for (dst, src) in ((eye2, sm["c_eye64"]), (lm2, sm["c_lmask64"])):
        for hh2 in range(2):
            bd.qdma("sync", S.tp0, dst[hh2 * 64:(hh2 + 1) * 64, :],
                    src[0:64, :])
    wait("vector", S.tp0, S.tp0.count)
    bd.ve(lambda e: e.tensor_scalar(twoi[:, :], eye2[:, :], 2.0, None, ALU.mult),
          reads=[eye2[:, 0:1]], writes=[twoi[:, 0:1]])
    v_eye = mark("vector", S.vec)
    wait("tensor", S.vec, v_eye)

    gsc2 = lsb("gsc2", (128, 128), F32)
    apair2 = lsb("apair2", (128, 64), F32)

    NEWT = 5
    GQ = (gsc, gsc2)
    AQ = (apair, apair2)
    for p in range(8):
        ms = (2 * p, 2 * p + 1)
        for q, m in enumerate(ms):
            tsl = slice(m * 128, (m + 1) * 128)
            gq, aq = GQ[q], AQ[q]
            for h in range(4):
                pt = pp_g.begin()
                for j, cc in enumerate((2 * h, 2 * h + 1)):
                    bd.raw("tensor", lambda e, pt=pt, cc=cc, tsl=tsl, j=j:
                           e.matmul(pt[:, :], kr[:, cc, tsl], kr[:, cc, tsl],
                                    start=(j == 0), stop=(j == 1)))
                pp_g.produced()

                def evac_g(e, pt, h=h, m=m, tsl=tsl, gq=gq):
                    return e.scalar_tensor_tensor(
                        gq[:, :], pt[:, :], sttm[:, m, 12 + h:13 + h],
                        tab4[:, h, tsl], ALU.mult, ALU.mult)
                pp_g.consume("vector", evac_g,
                             reads=[sttm[:, 0, 0:1], tab4[:, 0, 0:1]],
                             writes=[gq[:, 0:1]])
                if h == 0:
                    bd.ve(lambda e, gq=gq, aq=aq: e.tensor_copy(
                        aq[0:64, :], gq[0:64, 0:64]),
                        reads=[gq[:, 0:1]], writes=[aq[:, 0:1]])
                    bd.ve(lambda e, gq=gq, aq=aq: e.tensor_copy(
                        aq[64:128, :], gq[64:128, 64:128]),
                        reads=[gq[:, 0:1]], writes=[aq[:, 0:1]])
                else:
                    bd.ve(lambda e, gq=gq, aq=aq: e.tensor_tensor(
                        aq[0:64, :], aq[0:64, :], gq[0:64, 0:64], ALU.add),
                        reads=[gq[:, 0:1], aq[:, 0:1]], writes=[aq[:, 0:1]])
                    bd.ve(lambda e, gq=gq, aq=aq: e.tensor_tensor(
                        aq[64:128, :], aq[64:128, :], gq[64:128, 64:128],
                        ALU.add),
                        reads=[gq[:, 0:1], aq[:, 0:1]], writes=[aq[:, 0:1]])
        for q in range(2):
            gq, aq = GQ[q], AQ[q]
            bd.ve(lambda e, gq=gq, aq=aq: e.tensor_tensor(
                gq[:, 0:64], aq[:, :], lm2[:, :], ALU.mult),
                reads=[aq[:, 0:1], lm2[:, 0:1]], writes=[gq[:, 0:1]])
            bd.ve(lambda e, gq=gq, q=q: e.tensor_copy(
                nscr[:, 6 * q + 0, :], gq[:, 0:64]),
                reads=[gq[:, 0:1]], writes=[nscr[:, 6 * q, 0:1]])
        v_sl = mark("vector", S.vec)
        wait("tensor", S.vec, v_sl)
        for q in range(2):
            gq = GQ[q]
            pt = pp_y.begin()
            bd.raw("tensor", lambda e, pt=pt, gq=gq: e.transpose(
                pt[0:64, :], gq[0:64, 0:64], bd.idf32[0:64, 0:64]))
            bd.raw("tensor", lambda e, pt=pt, gq=gq: e.matmul(
                pt[64:128, :], gq[64:128, 0:64], bd.idf32[64:128, 64:128],
                start=True, stop=True, tile_position=(64, 64)))
            pp_y.produced()
            pp_y.consume("vector", lambda e, pt, q=q: e.tensor_copy(
                nscr[:, 6 * q + 1, :], pt[:, :]),
                writes=[nscr[:, 6 * q + 1, 0:1]])
        for q in range(2):
            b0 = 6 * q
            bd.ve(lambda e, b0=b0: e.tensor_tensor(
                nscr[:, b0 + 2, :], eye2[:, :], nscr[:, b0 + 0, :],
                ALU.subtract),
                reads=[nscr[:, b0, 0:1], eye2[:, 0:1]],
                writes=[nscr[:, b0 + 2, 0:1]])
            bd.ve(lambda e, b0=b0: e.tensor_tensor(
                nscr[:, b0 + 3, :], eye2[:, :], nscr[:, b0 + 1, :],
                ALU.subtract),
                reads=[nscr[:, b0 + 1, 0:1], eye2[:, 0:1]],
                writes=[nscr[:, b0 + 3, 0:1]])
            bd.ve(lambda e, b0=b0: e.tensor_tensor(
                nscr[:, b0 + 5, :], eye2[:, :], nscr[:, b0 + 1, :], ALU.add),
                reads=[nscr[:, b0 + 1, 0:1], eye2[:, 0:1]],
                writes=[nscr[:, b0 + 5, 0:1]])
        v_x0 = mark("vector", S.vec)
        wait("tensor", S.vec, v_x0)
        for it in range(NEWT):
            for q in range(2):
                b0 = 6 * q
                pt = pp_y.begin()
                for (rg, sl2) in ((0, slice(0, 64)), (64, slice(64, 128))):
                    bd.raw("tensor", lambda e, pt=pt, rg=rg, sl2=sl2, b0=b0:
                           e.matmul(pt[sl2, :], nscr[sl2, b0 + 5, :],
                                    nscr[sl2, b0 + 2, :],
                                    start=True, stop=True,
                                    tile_position=(rg, rg)))
                pp_y.produced()
                pp_y.consume("vector", lambda e, pt, b0=b0: e.tensor_tensor(
                    nscr[:, b0 + 4, :], twoi[:, :], pt[:, :], ALU.subtract),
                    reads=[twoi[:, 0:1]], writes=[nscr[:, b0 + 4, 0:1]])
            v_z = mark("vector", S.vec)
            wait("tensor", S.vec, v_z)
            for q in range(2):
                b0 = 6 * q
                pt = pp_x.begin()
                for (rg, sl2) in ((0, slice(0, 64)), (64, slice(64, 128))):
                    bd.raw("tensor", lambda e, pt=pt, rg=rg, sl2=sl2, b0=b0:
                           e.matmul(pt[sl2, 0:64], nscr[sl2, b0 + 3, :],
                                    nscr[sl2, b0 + 4, :],
                                    start=True, stop=True,
                                    tile_position=(rg, rg)))
                    bd.raw("tensor", lambda e, pt=pt, rg=rg, sl2=sl2, b0=b0:
                           e.matmul(pt[sl2, 64:128], nscr[sl2, b0 + 4, :],
                                    nscr[sl2, b0 + 3, :],
                                    start=True, stop=True,
                                    tile_position=(rg, rg)))
                pp_x.produced()
                pp_x.consume("vector", lambda e, pt, b0=b0: (
                    e.tensor_copy(nscr[:, b0 + 2, :], pt[:, 0:64]),
                    e.tensor_copy(nscr[:, b0 + 3, :], pt[:, 64:128])),
                    writes=[nscr[:, b0 + 2, 0:1], nscr[:, b0 + 3, 0:1]])
            v_it = mark("vector", S.vec)
            wait("tensor", S.vec, v_it)
        for q, m in enumerate(ms):
            bd.ve(lambda e, m=m, b0=6 * q: e.scalar_tensor_tensor(
                tts[:, m, :], nscr[:, b0 + 3, :], 3.0, twoi[:, :],
                ALU.mult, ALU.subtract),
                reads=[nscr[:, 6 * q + 3, 0:1], twoi[:, 0:1]],
                writes=[tts[:, 0, 0:1]])
        v_tt = mark("vector", S.vec)
        wait("tensor", S.vec, v_tt)
        for q, m in enumerate(ms):
            tsl = slice(m * 128, (m + 1) * 128)
            bd.ve(lambda e, tsl=tsl: e.tensor_tensor(
                ksc[:, 0:8:2, :], kr[:, 0:8:2, tsl], tab4[:, :, tsl],
                ALU.mult),
                reads=[kr[:, 0, 0:1], tab4[:, 0, 0:1]], writes=[ksc[:, 0, 0:1]])
            bd.ve(lambda e, tsl=tsl: e.tensor_tensor(
                ksc[:, 1:8:2, :], kr[:, 1:8:2, tsl], tab4[:, :, tsl],
                ALU.mult),
                reads=[kr[:, 0, 0:1], tab4[:, 0, 0:1]], writes=[ksc[:, 0, 0:1]])
            for (pipe, rg, sl2) in ((pp_wa, 0, slice(0, 64)),
                                    (pp_wb, 64, slice(64, 128))):
                ptw = pipe.begin()
                for cc in range(8):
                    bd.raw("tensor", lambda e, ptw=ptw, cc=cc, m=m, rg=rg,
                           sl2=sl2:
                           e.matmul(ptw[:, cc * 64:(cc + 1) * 64],
                                    kbt[sl2, m, cc * 128:(cc + 1) * 128],
                                    tts[sl2, m, :], start=True, stop=True,
                                    tile_position=(rg, 0)))
                pipe.produced()

            def cons_ab(pipe, dst, n, io):
                pipe.consume("vector", lambda e, pt, io=io: e.tensor_tensor(
                    puw[:, :, :], ksc[:, :, io:io + 64],
                    pt[:, :].rearrange("p (c i) -> p c i", c=8), ALU.mult),
                    reads=[ksc[:, 0, 0:1]], writes=[puw[:, 0, 0:1]])
                bd.ve(lambda e, n=n, dst=dst: e.tensor_reduce(
                    dst[:, :, n:n + 1], puw[:, :, :], AX.X, ALU.add),
                    reads=[puw[:, 0, 0:1]], writes=[dst[:, 0, 0:1]])
            cons_ab(pp_wa, a_sb, 2 * m, 0)
            for (pipe, rg, sl2) in ((pp_wa, 0, slice(0, 64)),
                                    (pp_wb, 64, slice(64, 128))):
                ptu = pipe.begin()
                for cc in range(8):
                    bd.raw("tensor", lambda e, ptu=ptu, cc=cc, m=m, rg=rg,
                           sl2=sl2:
                           e.matmul(ptu[:, cc * 64:(cc + 1) * 64],
                                    vbt[sl2, m, cc * 128:(cc + 1) * 128],
                                    tts[sl2, m, :], start=True, stop=True,
                                    tile_position=(rg, 0)))
                pipe.produced()
            cons_ab(pp_wb, a_sb, 2 * m + 1, 64)
            cons_ab(pp_wa, b_sb, 2 * m, 0)
            cons_ab(pp_wb, b_sb, 2 * m + 1, 64)

    bd.ve(lambda e: e.tensor_scalar(a_sb[:, :, :], a_sb[:, :, :], -1.0, 1.0,
                                    ALU.mult, ALU.add),
          reads=[a_sb[:, 0, 0:1]], writes=[a_sb[:, 0, 0:1]])
    for cc in range(8):
        bd.ve(lambda e, cc=cc: e.tensor_tensor_scan(
            sd[:, cc, 1:33], a_sb[:, cc, :], b_sb[:, cc, :], 0.0,
            ALU.mult, ALU.add),
            reads=[a_sb[:, 0, 0:1], b_sb[:, 0, 0:1]], writes=[sd[:, 0, 0:1]])
    v_scanA = mark("vector", S.vec)
    wait("sync", S.vec, v_scanA)
    v_d0w = bd.qdma("sync", S.stW, d["d0_in"], sd[:, :, 32:33],
                    allow_slow_non_contiguous=True)
    wait("gpsimd", S.stW, v_d0w)
    S.coll.count += 1
    bd.on("gpsimd", lambda e: e.collective_compute(
        "AllGather", ALU.bypass,
        replica_groups=[[0, 1], [2, 3], [4, 5], [6, 7]],
        ins=[d["d0_in"]], outs=[d["d0_out"]]).then_inc(S.coll.h, 1))
    v_cc0 = S.coll.count
    wait("gpsimd", S.coll, v_cc0)
    bd.qdma("gpsimd", S.gdma, d0r[:, 0:8], d["d0_out"][0])
    v_d0r = bd.qdma("gpsimd", S.gdma, d0r[:, 8:16], d["d0_out"][1])
    wait("vector", S.gdma, v_d0r)
    bd.ve(lambda e: e.tensor_tensor(d0r[:, 16:24], d0r[:, 0:8], d0r[:, 8:16],
                                    ALU.add),
          reads=[d0r[:, 0:1]], writes=[d0r[:, 0:1]])
    bd.ve(lambda e: e.tensor_tensor(d0r[:, 16:24], d0r[:, 16:24],
                                    sd[:, :, 32:33].rearrange("p a b -> p (a b)"),
                                    ALU.subtract),
          reads=[d0r[:, 0:1], sd[:, 0, 0:1]], writes=[d0r[:, 0:1]])
    import dataclasses as _dc
    om = sm["c_oddmask"][:, 0:1]
    om8 = _dc.replace(om, ap=[[1, 128], [0, 8]])
    bd.ve(lambda e, om8=om8: e.tensor_tensor(
        sd[:, :, 0:1].rearrange("p a b -> p (a b)"), d0r[:, 16:24], om8,
        ALU.mult),
        reads=[d0r[:, 0:1], om[0:1, 0:1]], writes=[sd[:, 0, 0:1]])
    for cc in range(8):
        bd.ve(lambda e, cc=cc: e.tensor_tensor_scan(
            sd[:, cc, 1:33], a_sb[:, cc, :], b_sb[:, cc, :], sd[:, cc, 0:1],
            ALU.mult, ALU.add),
            reads=[a_sb[:, 0, 0:1], b_sb[:, 0, 0:1]], writes=[sd[:, 0, 0:1]])
    bd.ve(lambda e: e.tensor_scalar(a_sb[:, :, :], sd[:, :, 0:32], -1.0, None,
                                    ALU.mult),
          reads=[sd[:, 0, 0:1]], writes=[a_sb[:, 0, 0:1]])
    v_scanB = mark("vector", S.vec)

    wait("gpsimd", S.stW, v_smw)
    wait("gpsimd", S.vec, v_scanB)
    src_sm = _dc.replace(d["h_stat"][12:13, :], ap=[[0, 128], [1, 2048]])
    v_smb = bd.qdma("gpsimd", S.brd, tab4[:, 0, :], src_sm)
    wait("vector", S.brd, v_smb)

    for m in range(16):
        for (pipe, rg, sl2) in ((pp_wa, 0, slice(0, 64)),
                                (pp_wb, 64, slice(64, 128))):
            ptw = pipe.begin()
            for cc in range(8):
                bd.raw("tensor", lambda e, ptw=ptw, cc=cc, m=m, rg=rg, sl2=sl2:
                       e.matmul(ptw[:, cc * 64:(cc + 1) * 64],
                                kbt[sl2, m, cc * 128:(cc + 1) * 128],
                                tts[sl2, m, :], start=True, stop=True,
                                tile_position=(rg, 0)))
            pipe.produced()
        for (pipe, rg, sl2) in ((pp_wa, 0, slice(0, 64)),
                                (pp_wb, 64, slice(64, 128))):
            ptu = pipe.begin()
            for cc in range(8):
                bd.raw("tensor", lambda e, ptu=ptu, cc=cc, m=m, rg=rg, sl2=sl2:
                       e.matmul(ptu[:, cc * 64:(cc + 1) * 64],
                                vbt[sl2, m, cc * 128:(cc + 1) * 128],
                                tts[sl2, m, :], start=True, stop=True,
                                tile_position=(rg, 0)))
            pipe.produced()
        for half, (pipe_w, pipe_u) in enumerate(((pp_wa, pp_wa),
                                                 (pp_wb, pp_wb))):
            n = 2 * m + half
            sln = slice(n * CC, (n + 1) * CC)
            nsdv = _dc.replace(a_sb[:, :, n:n + 1],
                               ap=[[8 * 33 if False else a_sb.shape[1] * a_sb.shape[2], 128],
                                   [a_sb.shape[2], 8], [0, 64]])
            sdv = _dc.replace(sd[:, :, n:n + 1],
                              ap=[[sd.shape[1] * sd.shape[2], 128],
                                  [sd.shape[2], 8], [0, 64]])
            pipe_w.consume("vector", lambda e, pt, nsdv=nsdv: e.tensor_tensor(
                usc[:, :, :], pt[:, :].rearrange("p (c i) -> p c i", c=8),
                nsdv, ALU.mult),
                reads=[a_sb[:, 0, 0:1]], writes=[usc[:, 0, 0:1]])
            pipe_u.consume("vector", lambda e, pt: e.tensor_tensor(
                usc[:, :, :], usc[:, :, :],
                pt[:, :].rearrange("p (c i) -> p c i", c=8), ALU.add),
                reads=[usc[:, 0, 0:1]], writes=[usc[:, 0, 0:1]])
            bd.ve(lambda e, sln=sln: e.tensor_tensor(
                usc[:, :, :], usc[:, :, :],
                _dc.replace(tab4[:, 0, sln], ap=[[4 * 2048, 128], [0, 8], [1, 64]]),
                ALU.mult),
                reads=[usc[:, 0, 0:1], tab4[:, 0, 0:1]], writes=[usc[:, 0, 0:1]])
            bd.ve(lambda e: e.tensor_tensor(
                usc[:, :, :], usc[:, :, :],
                _dc.replace(sm["c_cmask2"][:, 0:64], ap=[[64, 128], [0, 8], [1, 64]]),
                ALU.mult),
                reads=[usc[:, 0, 0:1], sm["c_cmask2"][:, 0:1]],
                writes=[usc[:, 0, 0:1]])
            bd.ve(lambda e, sln=sln, sdv=sdv: e.tensor_tensor(
                puw[:, :, :], qh[:, :, sln], sdv, ALU.mult),
                reads=[qh[:, 0, 0:1], sd[:, 0, 0:1]], writes=[puw[:, 0, 0:1]])
            bd.ve(lambda e, sln=sln: e.tensor_tensor(
                qh[:, :, sln], usc[:, :, :], puw[:, :, :], ALU.add),
                reads=[usc[:, 0, 0:1], puw[:, 0, 0:1]], writes=[qh[:, 0, 0:1]])
    v_pass2 = mark("vector", S.vec)
    bd.barrier()
    ps1.close()

    ps2 = contextlib.ExitStack()
    sqv = vw2(arA, 0, 8, 2048)
    x1p = vw2(arD, 0, 8, 2048)
    sq2 = vw2(arB, 0, 8, 2048)
    xres = vw2(arC, 0, 8, 2049)

    bd.se(lambda e: e.activation(sqv[:, :, :], qh[:, :, 0:2048], AF.Square),
          reads=[qh[:, 0, 0:1]], writes=[sqv[:, 0, 0:1]])
    v_osq = mark("scalar", S.act)
    pp_rms = PsumPipe(bd, "rms", (8, 512), 2, scope=ps2)
    wait("tensor", S.act, v_osq)
    for si, (t0, tn) in enumerate(STRIPS):
        pt = pp_rms.begin()
        for cc in range(8):
            bd.raw("tensor", lambda e, pt=pt, cc=cc, t0=t0, tn=tn:
                   e.matmul(pt[:, 0:tn], sm["c_dselq"][:, cc, 0:8],
                            sqv[:, cc, t0:t0 + tn],
                            start=(cc == 0), stop=(cc == 7)))
        pp_rms.produced()
        pp_rms.consume("vector", lambda e, pt, t0=t0, tn=tn: e.tensor_copy(
            stB[0:4, t0:t0 + tn], pt[0:4, 0:tn]), writes=[stB[:, :]])
    v_rmse = mark("vector", S.vec)
    wait("scalar", S.vec, v_rmse)
    bd.se(lambda e: e.activation(stB[0:4, :], stB[0:4, :], AF.Sqrt,
                                 bias=1e-5, scale=1.0 / HD),
          reads=[stB[:, :]], writes=[stB[:, :]])
    v_rmsq = mark("scalar", S.act)
    wait("vector", S.act, v_rmsq)
    bd.ve(lambda e: e.reciprocal(stB[0:4, :], stB[0:4, :]),
          reads=[stB[:, :]], writes=[stB[:, :]])
    v_rmsi = mark("vector", S.vec)
    wait("sync", S.vec, v_rmsi)
    v_rmw = bd.qdma("sync", S.stW, d["h_stat"][8:12, :], stB[0:4, :])
    wait("gpsimd", S.stW, v_rmw)
    v_tabr = bcast4(tab4, 8)
    wait("vector", S.brd, v_tabr)
    for cc in range(8):
        bd.ve(lambda e, cc=cc: e.scalar_tensor_tensor(
            qh[:, cc, :], qh[:, cc, :], sm["g_col"][:, cc % 2:cc % 2 + 1],
            tab4[:, cc // 2, :], ALU.mult, ALU.mult),
            reads=[qh[:, 0, 0:1], tab4[:, 0, 0:1], sm["g_col"][:, 0:1]],
            writes=[qh[:, 0, 0:1]])
    v_og = mark("vector", S.vec)
    wait("gpsimd", S.vec, v_og)
    v_xr = bd.qdma("gpsimd", S.xin, sqv[:, :, :], d["x_loc"][:, :, 3:])
    pp_wo = PsumPipe(bd, "pwo", (128, 512), 2, scope=ps2)
    wait("tensor", S.vec, v_og)
    wait("tensor", S.wg1, v_wo)
    wait("tensor", S.xin, v_xr)
    for f in range(8):
        for (t0, tn) in STRIPS:
            pt = pp_wo.begin()
            for k in range(8):
                bd.raw("tensor", lambda e, pt=pt, f=f, k=k, t0=t0, tn=tn:
                       e.matmul(pt[:, 0:tn], wB[:, k, f * 128:(f + 1) * 128],
                                qh[:, k, t0:t0 + tn],
                                start=(k == 0), stop=(k == 7)))
            pp_wo.produced()
            pp_wo.consume("vector", lambda e, pt, f=f, t0=t0, tn=tn:
                          e.scalar_tensor_tensor(
                              x1p[:, f, t0:t0 + tn], pt[:, 0:tn], 1.0,
                              sqv[:, f, t0:t0 + tn], ALU.mult, ALU.add),
                          reads=[sqv[:, 0, 0:1]], writes=[x1p[:, 0, 0:1]])
    v_x1p = mark("vector", S.vec)

    wait("scalar", S.vec, v_x1p)
    bd.se(lambda e: e.activation(sq2[:, :, :], x1p[:, :, :], AF.Square),
          reads=[x1p[:, 0, 0:1]], writes=[sq2[:, 0, 0:1]])
    v_x1sq = mark("scalar", S.act)
    pp_s1 = PsumPipe(bd, "ps1", (1, 512), 2, scope=ps2)
    pp_s2 = PsumPipe(bd, "ps2", (64, 512), 2, scope=ps2)
    wait("tensor", S.act, v_x1sq)
    for (t0, tn) in STRIPS:
        pt1 = pp_s1.begin()
        for cc in range(8):
            bd.raw("tensor", lambda e, pt=pt1, cc=cc, t0=t0, tn=tn:
                   e.matmul(pt[:, 0:tn], sm["c_ones"][:, 0:1],
                            x1p[:, cc, t0:t0 + tn],
                            start=(cc == 0), stop=(cc == 7)))
        pp_s1.produced()
        pp_s1.consume("vector", lambda e, pt, t0=t0, tn=tn: e.tensor_copy(
            stB[0:1, t0:t0 + tn], pt[0:1, 0:tn]), writes=[stB[:, :]])
        pt2 = pp_s2.begin()
        for cc in range(8):
            bd.raw("tensor", lambda e, pt=pt2, cc=cc, t0=t0, tn=tn:
                   e.matmul(pt[32:33, 0:tn], sm["c_ones"][:, 0:1],
                            sq2[:, cc, t0:t0 + tn],
                            start=(cc == 0), stop=(cc == 7),
                            tile_position=(0, 32)))
        pp_s2.produced()
        pp_s2.consume("vector", lambda e, pt, t0=t0, tn=tn: e.tensor_copy(
            stB[32:33, t0:t0 + tn], pt[32:33, 0:tn]), writes=[stB[:, :]])
    v_ln1 = mark("vector", S.vec)
    wait("sync", S.vec, v_ln1)
    bd.qdma("sync", S.stW, d["h_stat"][14:15, :], stB[0:1, :])
    v_lnw = bd.qdma("sync", S.stW, d["h_stat"][15:16, :], stB[32:33, :])
    wait("gpsimd", S.stW, v_lnw)
    import dataclasses as _dc2
    for r in range(2):
        src = _dc2.replace(d["h_stat"][14 + r:15 + r, :], ap=[[0, 128], [1, 2048]])
        bd.qdma("gpsimd", S.brd, tab4[:, r, :], src)
    v_mro = S.brd.count
    wait("vector", S.brd, v_mro)
    bd.ve(lambda e: e.tensor_scalar(tab4[:, 0, :], tab4[:, 0, :], 1.0 / D,
                                    None, ALU.mult),
          reads=[tab4[:, 0, 0:1]], writes=[tab4[:, 0, 0:1]])
    bd.ve(lambda e: e.tensor_tensor(tab4[:, 2, :], tab4[:, 0, :],
                                    tab4[:, 0, :], ALU.mult),
          reads=[tab4[:, 0, 0:1]], writes=[tab4[:, 0, 0:1]])
    bd.ve(lambda e: e.scalar_tensor_tensor(tab4[:, 1, :], tab4[:, 1, :],
                                           1.0 / D, tab4[:, 2, :],
                                           ALU.mult, ALU.subtract),
          reads=[tab4[:, 0, 0:1]], writes=[tab4[:, 0, 0:1]])
    v_var = mark("vector", S.vec)
    wait("scalar", S.vec, v_var)
    bd.se(lambda e: e.activation(tab4[:, 1, :], tab4[:, 1, :], AF.Sqrt,
                                 bias=1e-5),
          reads=[tab4[:, 0, 0:1]], writes=[tab4[:, 0, 0:1]])
    v_sd2 = mark("scalar", S.act)
    wait("vector", S.act, v_sd2)
    def _rec_t4(e):
        with bd.nc.allow_low_precision("bf16 LN rstd table"):
            e.reciprocal(tab4[:, 1, :], tab4[:, 1, :])
    bd.ve(_rec_t4, reads=[tab4[:, 0, 0:1]], writes=[tab4[:, 0, 0:1]])
    bd.ve(lambda e: e.tensor_tensor(tab4[:, 0, :], tab4[:, 0, :],
                                    tab4[:, 1, :], ALU.mult),
          reads=[tab4[:, 0, 0:1]], writes=[tab4[:, 0, 0:1]])
    v_tabready = mark("vector", S.vec)
    for cc in range(8):
        bd.ve(lambda e, cc=cc: e.tensor_tensor(
            xres[:, cc, 1:2049], x1p[:, cc, :], tab4[:, 1, :], ALU.mult),
            reads=[x1p[:, 0, 0:1], tab4[:, 0, 0:1]], writes=[xres[:, 0, 0:1]])
        bd.ve(lambda e, cc=cc: e.tensor_tensor(
            xres[:, cc, 1:2049], xres[:, cc, 1:2049], tab4[:, 0, :],
            ALU.subtract),
            reads=[xres[:, 0, 0:1], tab4[:, 0, 0:1]], writes=[xres[:, 0, 0:1]])
    v_x0 = mark("vector", S.vec)
    wait("gpsimd", S.vec, v_x0)
    bd.qdma("gpsimd", S.bx0, d["h_x0"].rearrange("p (a b) -> p a b", a=8),
            xres[:, :, 1:2049])
    bd.barrier()
    ps2.close()
    loc.close()


def bcast4(bd, tab_dst, src_row0, S):
    import dataclasses as _dc
    src = _dc.replace(bd.d["h_stat"][src_row0:src_row0 + 4, :],
                      ap=[[0, 64], [2048, 4], [1, 2048]])
    bd.qdma("gpsimd", S.brd, tab_dst[0:64, :, :], src)
    bd.qdma("gpsimd", S.brd, tab_dst[64:128, :, :], src)
    return S.brd.count


def emit_ffn(bd, arA, arB, arC, arD, wgt, tab4):
    import dataclasses as _dc
    nc, d, S, sm = bd.nc, bd.d, bd.S, bd.small

    def mark(eng, sem):
        return bd.release(eng, sem, 1)

    def wait(eng, sem, v):
        bd.raw(eng, lambda e, s=sem.h, v=v: e.wait_ge(s, v))

    xres = vw2(arC, 0, 8, 2049)
    qrot = vw2(arA, 0, 4, 2048)
    otm = vw2(arA, 4 * 2048, 16, 512)
    krot = vw2(arD, 0, 4, 2560)
    vtm = vw2(arB, 0, 20, VCOLS)
    prb = vw2(arD, 10240, 2, 640)
    prbT = vw2(arD, 10240 + 2 * 640, 2, 640)
    ofm = vw2(arD, 10240 + 4 * 640, 4, 512)
    xcolstage = vw2(arD, 10240 + 4 * 640 + 4 * 512, 8, 1)
    xhat = vw2(arA, 0, 8, 2048)
    sqf = vw2(arB, 0, 8, 2049)
    hgs = vw2(arB, 0, 22, 512)
    wqkv_v = vw2(wgt, 0, 8, 1536)
    wo_v = vw2(wgt, 12288, 4, 1024)
    w1bufs = [vw2(wgt, 0, 8, 256), vw2(wgt, 8 * 256, 8, 256)]
    w2W = vw2(wgt, 4096, 6, 1024)
    sttm = bd.sb["sttm"]
    stB = bd.sb["st16"]
    fsc = contextlib.ExitStack()
    bd.ffn_scope = fsc
    ikb = fsc.enter_context(bd.nc.sbuf_tensor("ikb", [128, 2, 2048], BF16))
    rcv = fsc.enter_context(bd.nc.sbuf_tensor("rcv", [128, 2, 520], BF16))
    halo2x = fsc.enter_context(bd.nc.sbuf_tensor("halo", [128, 8, 2], BF16))
    mro = vw2(arB, 22 * 512, 2, 2048)
    t_qc, t_qs, t_kc, t_ks = (tab4[:, 0, :], tab4[:, 1, :],
                              tab4[:, 2, :], tab4[:, 3, :])
    S = bd.S
    for i, nm in enumerate(("t_qc", "t_qs", "t_kc", "t_ks")):
        bd.qdma("sync", S.tload, tab4[:, i, :], bd.d[nm])
    v_rt = S.tload.count
    halo2 = halo2x
    idbf = sm["c_idbf"]

    rsc0 = vw2(arD, 14880, 1, 512)
    rsc = rsc0[:, 0, :]
    aw = vw2(arB, 15360, 2, 512)

    CH = []
    m = 0
    while m * 128 < HID:
        sz = min(128, HID - m * 128)
        CH.append(sz)
        m += 1
    NPAIR = len(CH)
    acol = []
    c0 = 0
    for m in range(NPAIR):
        acol.append(c0)
        c0 += 2 * CH[m]
    PIECES = []
    m = 0
    while m < NPAIR:
        PIECES.append(list(range(m, min(m + 2, NPAIR))))
        m += 2
    w2row = []
    r0 = 0
    for m in range(NPAIR):
        w2row.append(r0)
        r0 += CH[m]

    bd.ffn_sems = dict(vb=bd.sem("avb"), it=[1],
                       seV={}, tsV={}, oevV={}, avmV={},
                       dt3=bd.sem("adt3"), dt4=bd.sem("adt4"),
                       dt3V={}, dt4V={})
    bd.raw("vector", lambda e, v=S.bx0.count: e.wait_ge(S.bx0.h, v))
    for layer in range(2):
        ps = contextlib.ExitStack()
        v_wq = wload(bd, f"wqkv{layer}", wqkv_v[:, :, :],
                     d["wqkv"][layer].rearrange("(k p) j -> p k j", p=128),
                     S.wg0)
        v_wo = wload(bd, f"wo{layer}", wo_v[:, :, :],
                     d["wo_f"][layer].rearrange("(k p) j -> p k j", p=128),
                     S.wg1)

        pp_qk = PsumPipe(bd, f"fqk{layer}", (128, 512), 2, scope=ps, reg_key="fqk")
        pp_rot = PsumPipe(bd, f"frot{layer}", (128, 512), 2, scope=ps, reg_key="frot")
        kraw = vw2(arA, 4 * 2048, 4, 2048)

        def qkv_strips(strips, xcol_off, _ph=[0]):
            qsc = contextlib.ExitStack()
            _ph[0] += 1
            pp_ss = PsumPipe(bd, f"fss{layer}_{_ph[0]}", (16, 512), 1,
                             scope=qsc, reg_key="fss")
            pp_tp = PsumPipe(bd, f"ftp{layer}_{_ph[0]}", (128, 16), 1,
                             scope=qsc, reg_key="ftp")
            for f in range(8):
                dst = qrot if f < 4 else kraw
                dcc = f if f < 4 else f - 4
                for (t0, tn) in strips:
                    pt = pp_qk.begin()
                    for k in range(8):
                        src_off = (1 if k < 4 else 0) + t0
                        bd.raw("tensor", lambda e, pt=pt, f=f, k=k, t0=t0,
                               tn=tn, so=src_off:
                               e.matmul(pt[:, 0:tn], wqkv_v[:, k, f * 128:(f + 1) * 128],
                                        xres[:, k, so:so + tn],
                                        start=(k == 0), stop=(k == 7)))
                    pp_qk.produced()
                    pp_qk.consume("scalar", lambda e, pt, dst=dst, dcc=dcc,
                                  t0=t0, tn=tn:
                                  e.activation(dst[:, dcc, t0:t0 + tn],
                                               pt[:, 0:tn], AF.Copy),
                                  writes=[dst[:, 0, 0:1]])
            for (t0, tn) in strips:
                for tt in range(t0 // 128, (t0 + tn) // 128):
                    pt = pp_qk.begin()
                    for k in range(8):
                        so = (1 if k < 4 else 0) + tt * 128
                        bd.raw("tensor", lambda e, pt=pt, k=k, tt=tt, so=so:
                               e.matmul(pt[:, 0:512],
                                        xres[:, k, so:so + 128],
                                        wqkv_v[:, k, 1024:1536],
                                        start=(k == 0), stop=(k == 7)))
                    pp_qk.produced()
                    def ev_v(e, pt, tt=tt):
                        va = vtm[:, 4 + tt, :].rearrange(
                            "p (h dd) -> p h dd", dd=65)[:, :, 0:64]
                        e.activation(va, pt[:, 0:512].rearrange(
                            "p (h dd) -> p h dd", h=8), AF.Copy)
                    pp_qk.consume("scalar", ev_v, writes=[vtm[:, 0, 0:1]])
            for (t0, tn) in strips:
                pt = pp_ss.begin()
                for j in range(8):
                    srcv = (qrot[:, j, t0:t0 + tn] if j < 4
                            else kraw[:, j - 4, t0:t0 + tn])
                    slot = j % 2
                    if S.ev2.count >= 1 and (j >= 2 or S.ev2.count > 1):
                        pass
                    if S.ev1.count >= 2:
                        wait("scalar", S.ev2, S.ev2.count - 1)
                    bd.se(lambda e, srcv=srcv, slot=slot: e.activation(
                        aw[:, slot, 0:512], srcv, AF.Square),
                        reads=[srcv], writes=[aw[:, 0, 0:1]])
                    va = mark("scalar", S.ev1)
                    wait("tensor", S.ev1, va)
                    sel = sm["c_selq"] if j < 4 else sm["c_selk"]
                    cc = j % 4
                    bd.raw("tensor", lambda e, pt=pt, sel=sel, cc=cc, tn=tn,
                           slot=slot, j=j:
                           e.matmul(pt[:, 0:tn], sel[:, cc, :],
                                    aw[:, slot, 0:tn],
                                    start=(j == 0), stop=(j == 7)))
                    mark("tensor", S.ev2)
                pp_ss.produced()
                pp_ss.consume("vector", lambda e, pt, t0=t0, tn=tn:
                              e.tensor_copy(stB[0:16, t0:t0 + tn], pt[:, 0:tn]),
                              writes=[stB[:, :]])
            cr = slice(strips[0][0], strips[-1][0] + strips[-1][1])
            bd.ve(lambda e, cr=cr: e.reciprocal(stB[0:16, cr], stB[0:16, cr]),
                  reads=[stB[:, :]], writes=[stB[:, :]])
            v1 = mark("vector", S.vec)
            wait("scalar", S.vec, v1)
            bd.se(lambda e, cr=cr: e.activation(stB[0:16, cr], stB[0:16, cr],
                                                AF.Sqrt),
                  reads=[stB[:, :]], writes=[stB[:, :]])
            v2 = mark("scalar", S.act)
            wait("vector", S.act, v2)
            bd.ve(lambda e, cr=cr: e.tensor_scalar(
                stB[0:8, cr], stB[0:8, cr], 8.0, None, ALU.mult),
                reads=[stB[:, :]], writes=[stB[:, :]])
            v3 = mark("vector", S.vec)
            wait("tensor", S.vec, v3)
            for r in range(cr.start // 128, cr.stop // 128):
                pt = pp_tp.begin()
                bd.raw("tensor", lambda e, pt=pt, r=r: e.transpose(
                    pt[:, 0:8], stB[0:8, r * 128:(r + 1) * 128],
                    bd.idf32[0:8, 0:8]))
                pp_tp.produced()
                pp_tp.consume("vector", lambda e, pt, r=r: e.tensor_copy(
                    sttm[:, r, 0:8], pt[:, 0:8]), writes=[sttm[:, 0, 0:1]])
            wait("sync", S.vec, v3)
            vW = bd.qdma("sync", S.stW, d["h_stat"][0:8, cr], stB[8:16, cr])
            wait("gpsimd", S.stW, vW)
            n = cr.stop - cr.start
            wait("vector", S.tload, v_rt)
            for cc in range(4):
                for (t0, tn) in strips:
                    bd.ve(lambda e, cc=cc, t0=t0, tn=tn: e.tensor_scalar(
                        qrot[:, cc, t0:t0 + tn], qrot[:, cc, t0:t0 + tn],
                        sm["qs_col"][:, layer, 0:1], None, ALU.mult),
                        reads=[qrot[:, 0, 0:1]], writes=[qrot[:, 0, 0:1]])
            vq2 = mark("vector", S.vec)
            wait("tensor", S.vec, vq2)
            for cc in range(4):
                for (t0, tn) in strips:
                    pt = pp_rot.begin()
                    bd.raw("tensor", lambda e, pt=pt, cc=cc, t0=t0, tn=tn:
                           e.matmul(pt[:, 0:tn], sm["c_r2"][:, :],
                                    qrot[:, cc, t0:t0 + tn],
                                    start=True, stop=True))
                    pp_rot.produced()

                    def rotq_fn(e, pt, cc=cc, t0=t0, tn=tn):
                        e.tensor_tensor(rsc[:, 0:tn], pt[:, 0:tn],
                                        t_qs[:, t0:t0 + tn], ALU.mult)
                        i2 = e.tensor_tensor(qrot[:, cc, t0:t0 + tn],
                                             qrot[:, cc, t0:t0 + tn],
                                             t_qc[:, t0:t0 + tn], ALU.mult)
                        bd.fence_after(e, "vector", i2)
                        return e.tensor_tensor(qrot[:, cc, t0:t0 + tn],
                                               qrot[:, cc, t0:t0 + tn],
                                               rsc[:, 0:tn], ALU.add)
                    pp_rot.consume("vector", rotq_fn,
                        reads=[qrot[:, 0, 0:1]], writes=[qrot[:, 0, 0:1]])
            vqr = mark("vector", S.vec)
            for half in range(2):
                base = 2 * half
                if getattr(bd, "ikb_guard", None) is not None:
                    wait("gpsimd", S.vec, bd.ikb_guard)
                for (p0, row0) in ((0, 2 * base), (64, 2 * base + 1)):
                    srcx = _dc.replace(
                        d["h_stat"][row0:row0 + 1, cr],
                        ap=[[0, 64], [2 * 2048, 2], [1, n]])
                    bd.qdma("gpsimd", S.brd,
                            ikb[p0:p0 + 64, :, cr.start:cr.stop], srcx)
                vB = S.brd.count
                wait("vector", S.brd, vB)
                for cc in (base, base + 1):
                    for (t0, tn) in strips:
                        bd.ve(lambda e, cc=cc, t0=t0, tn=tn, base=base:
                              e.tensor_tensor(
                            kraw[:, cc, t0:t0 + tn], kraw[:, cc, t0:t0 + tn],
                            ikb[:, cc - base, t0:t0 + tn], ALU.mult),
                            reads=[kraw[:, 0, 0:1], ikb[:, 0, 0:1]],
                            writes=[kraw[:, 0, 0:1]])
                        bd.ve(lambda e, cc=cc, t0=t0, tn=tn: e.tensor_scalar(
                            kraw[:, cc, t0:t0 + tn], kraw[:, cc, t0:t0 + tn],
                            sm["ks_col"][:, layer, 0:1], None, ALU.mult),
                            reads=[kraw[:, 0, 0:1]], writes=[kraw[:, 0, 0:1]])
                bd.ikb_guard = mark("vector", S.vec)
            vk2 = mark("vector", S.vec)
            wait("tensor", S.vec, vk2)
            for cc in range(4):
                for (t0, tn) in strips:
                    pt = pp_rot.begin()
                    bd.raw("tensor", lambda e, pt=pt, cc=cc, t0=t0, tn=tn:
                           e.matmul(pt[:, 0:tn], sm["c_r2"][:, :],
                                    kraw[:, cc, t0:t0 + tn],
                                    start=True, stop=True))
                    pp_rot.produced()
                    def rotk_fn(e, pt, cc=cc, t0=t0, tn=tn):
                        e.tensor_tensor(rsc[:, 0:tn], pt[:, 0:tn],
                                        t_ks[:, t0:t0 + tn], ALU.mult)
                        i2 = e.tensor_tensor(krot[:, cc, 512 + t0:512 + t0 + tn],
                                             kraw[:, cc, t0:t0 + tn],
                                             t_kc[:, t0:t0 + tn], ALU.mult)
                        bd.fence_after(e, "vector", i2)
                        return e.tensor_tensor(
                            krot[:, cc, 512 + t0:512 + t0 + tn],
                            krot[:, cc, 512 + t0:512 + t0 + tn],
                            rsc[:, 0:tn], ALU.add)
                    pp_rot.consume("vector", rotk_fn,
                        reads=[kraw[:, 0, 0:1], krot[:, 0, 0:1]],
                        writes=[krot[:, 0, 0:1]])
            vkr = mark("vector", S.vec)
            qsc.close()
            return vqr, vkr

        pipes = dict(
            sim=pp_qk, simd=pp_rot, wo=pp_qk, layer=layer,
            vb=bd.ffn_sems["vb"], it=bd.ffn_sems["it"][0],
        )

        for h in range(8):
            bd.ve(lambda e, h=h: e.memset(vtm[:, :, h * 65 + 64:h * 65 + 65],
                                          1.0),
                  writes=[vtm[:, 0, 0:1]])
        v_ones = mark("vector", S.vec)

        wait("tensor", S.wg0, v_wq)
        vqr1, vkr1 = qkv_strips(STRIPS[1:], 1)
        bd.ve(lambda e: e.tensor_copy(xcolstage[:, :, 0:1], xres[:, :, 2048:2049]),
              reads=[xres[:, 0, 0:1]], writes=[xcolstage[:, 0, 0:1]])
        v_xcol = mark("vector", S.vec)
        wait("tensor", S.vec, max(vkr1, v_ones))
        emit_attention(bd, layer, range(8, 16), qrot, krot, vtm, otm, prb,
                       prbT, sttm, pipes)
        emit_wo_strips(bd, layer, (2, 3), otm, ofm, wo_v, xres, v_wo, pipes)
        emit_exchange(bd, layer, krot, vtm, xres, xcolstage, halo2, rcv)
        vqr2, vkr2 = qkv_strips(STRIPS[0:1], 1)
        wait("tensor", S.vec, vkr2)
        emit_attention(bd, layer, range(0, 8), qrot, krot, vtm, otm, prb,
                       prbT, sttm, pipes)
        emit_wo_strips(bd, layer, (0, 1), otm, ofm, wo_v, xres, v_wo, pipes)
        bd.barrier()
        bd.ve(lambda e: e.tensor_copy(xres[:, :, 0:1], halo2[:, :, 1:2]),
              reads=[halo2[:, 0, 0:1]], writes=[xres[:, 0, 0:1]])
        bd.barrier()
        w2D = vw2(arD, 0, 16, 1024)
        emit_mlp(bd, layer, xres, xhat, sqf, hgs, w1bufs, (w2D, w2W), mro, ps,
                 acol, CH, NPAIR, PIECES, w2row, aw)
        bd.barrier()
        ps.close()
    fsc.close()


def emit_attention(bd, layer, qtiles, qrot, krot, vtm, otm, prb, prbT, sttm,
                   pipes):
    S, sm = bd.S, bd.small
    asc = contextlib.ExitStack()
    if not hasattr(bd, "_attn_ph"):
        bd._attn_ph = 0
    bd._attn_ph += 1
    pp_s, pp_d = pipes["sim"], pipes["simd"]
    pp_t = PsumPipe(bd, f"fptr_{bd._attn_ph}", (128, 128), 2, dtype=BF16,
                    scope=asc, reg_key="fptr")
    pp_av = PsumPipe(bd, f"favp_{bd._attn_ph}", (128, 72), 1,
                     scope=asc, reg_key="favp")
    s_vb = pipes["vb"]
    seV, tsV = bd.ffn_sems["seV"], bd.ffn_sems["tsV"]
    oevV, avmV = bd.ffn_sems["oevV"], bd.ffn_sems["avmV"]
    dt3, dt4 = bd.ffn_sems["dt3"], bd.ffn_sems["dt4"]
    dt3V, dt4V = bd.ffn_sems["dt3V"], bd.ffn_sems["dt4V"]
    stc = sm["stcol"]

    def head(it, r, h, w, hb, ch, par, slot):
        ptS = pp_s.begin()
        bd.raw("tensor", lambda e, ptS=ptS, r=r, hb=hb, ch=ch:
               e.matmul(ptS[:, 0:512],
                        qrot[hb:hb + 64, ch, r * 128:(r + 1) * 128],
                        krot[hb:hb + 64, ch, r * 128:r * 128 + 512],
                        start=True, stop=False, tile_position=(hb, 0)))
        if w == 0:
            for kc in range(4):
                pat = sm["m_pfar0"] if kc == 0 else sm["m_pmid0"]
                bd.raw("tensor", lambda e, ptS=ptS, kc=kc, pat=pat:
                       e.matmul(ptS[:, kc * 128:(kc + 1) * 128],
                                sm["c_idbf"][:, :], pat[:, :],
                                start=False, stop=(kc == 3)))
        else:
            bd.raw("tensor", lambda e, ptS=ptS:
                   e.matmul(ptS[:, 0:128], sm["c_idbf"][:, :],
                            sm["m_far"][:, :], start=False, stop=True))
        pp_s.produced()
        ptD = pp_d.begin()
        bd.raw("tensor", lambda e, ptD=ptD, r=r, hb=hb, ch=ch:
               e.matmul(ptD[:, 0:128],
                        qrot[hb:hb + 64, ch, r * 128:(r + 1) * 128],
                        krot[hb:hb + 64, ch, r * 128 + 512:r * 128 + 640],
                        start=True, stop=False, tile_position=(hb, 0)))
        bd.raw("tensor", lambda e, ptD=ptD:
               e.matmul(ptD[:, 0:128], sm["c_idbf"][:, :],
                        sm["m_diag"][:, :], start=False, stop=True))
        pp_d.produced()

        if seV.get(it - 1):
            bd.raw("vector", lambda e, s=pp_d.cons.h, v=seV[it - 1]:
                   e.wait_ge(s, v))
        pp_s.wait_in("vector")
        pp_d.wait_in("vector")

        def vb_ops(e, ptS=ptS, ptD=ptD, par=par, r=r, h=h):
            e.tensor_reduce(stc[:, par:par + 1], ptS[:, 0:512], AX.X,
                            ALU.max, negate=True)
            i2 = e.tensor_reduce(stc[:, par + 1:par + 2], ptD[:, 0:128],
                                 AX.X, ALU.max, negate=True)
            bd.fence_after(e, "vector", i2)
            i3 = e.tensor_tensor(stc[:, par:par + 1], stc[:, par:par + 1],
                                 stc[:, par + 1:par + 2], ALU.min)
            bd.fence_after(e, "vector", i3)
            return e.tensor_tensor(stc[:, par + 2:par + 3],
                                   stc[:, par:par + 1],
                                   sttm[:, r, h:h + 1], ALU.mult)
        bd.ve(vb_ops, reads=[sttm[:, 0, 0:1]], writes=[stc[:, 0:8]])
        bd.release("vector", s_vb, 1)

        bd.raw("scalar", lambda e, v=s_vb.count: e.wait_ge(s_vb.h, v))
        if tsV.get(it - 2):
            bd.raw("scalar", lambda e, s=pp_t.prod.h, v=tsV[it - 2]:
                   e.wait_ge(s, v))
            bd.raw("scalar", lambda e, s=dt3.h, v=dt3V[it - 2]:
                   e.wait_ge(s, v))
            bd.raw("scalar", lambda e, s=dt4.h, v=dt4V[it - 2]:
                   e.wait_ge(s, v))
        pp_s.cons.count += 1
        pp_d.cons.count += 1

        def exp_ops(e, ptS=ptS, ptD=ptD, slot=slot, par=par, r=r, h=h,
                    v1=pp_s.cons.count, v2=pp_d.cons.count):
            i1 = e.activation(prb[:, slot, 0:512], ptS[:, 0:512], AF.Exp,
                              bias=stc[:, par + 2:par + 3],
                              scale=sttm[:, r, h:h + 1])
            i1.then_inc(pp_s.cons.h, 1)
            bd._upd[id(i1)] = (i1, pp_s.cons.h, v1)
            i2 = e.activation(prb[:, slot, 512:640], ptD[:, 0:128], AF.Exp,
                              bias=stc[:, par + 2:par + 3],
                              scale=sttm[:, r, h:h + 1])
            i2.then_inc(pp_d.cons.h, 1)
            bd._upd[id(i2)] = (i2, pp_d.cons.h, v2)
            return i2
        bd.se(exp_ops, reads=[stc[:, 0:8], sttm[:, 0, 0:1]],
              writes=[prb[:, 0, 0:1]])
        seV[it] = pp_d.cons.count

        for (eng, sem, vdict, kc) in (("scalar", dt3, dt3V, 3),
                                      ("scalar", dt4, dt4V, 4)):
            prior = sem.count
            sem.count += 16
            vdict[it] = sem.count

            def trig(e, prior=prior, s=sem.h, kc=kc, slot=slot,
                     sev=seV[it], war=avmV.get(it - 2)):
                if prior:
                    e.wait_ge(s, prior)
                e.wait_ge(pp_d.cons.h, sev)
                if war:
                    e.wait_ge(pp_av.prod.h, war)
                e.dma_start_transpose(
                    prbT[:, slot, kc * 128:(kc + 1) * 128],
                    prb[:, slot, kc * 128:(kc + 1) * 128]).then_inc(s, 16)
            bd.raw(eng, trig)

    def tail(it, r, h, par, slot):
        bd.raw("tensor", lambda e, s=pp_d.cons.h, v=seV[it]:
               e.wait_ge(s, v))
        if avmV.get(it - 1):
            bd.raw("vector", lambda e, s=pp_av.prod.h, v=avmV[it - 1]:
                   e.wait_ge(s, v))
        for kc in range(3):
            ptT = pp_t.begin()
            bd.raw("tensor", lambda e, ptT=ptT, kc=kc, slot=slot:
                   e.transpose(ptT[:, :],
                               prb[:, slot, kc * 128:(kc + 1) * 128],
                               sm["c_idbf"][:, :]))
            pp_t.produced()
            pp_t.consume("vector", lambda e, pt, slot=slot, kc=kc:
                         e.tensor_copy(prbT[:, slot,
                                            kc * 128:(kc + 1) * 128],
                                       pt[:, :]),
                         writes=[prbT[:, 0, 0:1]])
        tsV[it] = pp_t.prod.count
        oevV[it] = pp_t.cons.count

        bd.raw("tensor", lambda e, s=pp_t.cons.h, v=oevV[it]:
               e.wait_ge(s, v))
        bd.raw("tensor", lambda e, s=dt3.h, v=dt3V[it]: e.wait_ge(s, v))
        bd.raw("tensor", lambda e, s=dt4.h, v=dt4V[it]: e.wait_ge(s, v))
        ptA = pp_av.begin()
        for kc in range(5):
            bd.raw("tensor", lambda e, ptA=ptA, kc=kc, slot=slot, r=r, h=h:
                   e.matmul(ptA[:, 0:65],
                            prbT[:, slot, kc * 128:(kc + 1) * 128],
                            vtm[:, r + kc, h * 65:(h + 1) * 65],
                            start=(kc == 0), stop=(kc == 4)))
        pp_av.produced()
        avmV[it] = pp_av.prod.count

        def av_ops(e, pt, par=par, r=r, h=h):
            i1 = e.reciprocal(stc[:, par + 3:par + 4], pt[:, 64:65])
            bd.fence_after(e, "vector", i1)
            return e.tensor_scalar(otm[:, r, h * 64:(h + 1) * 64],
                                   pt[:, 0:64],
                                   stc[:, par + 3:par + 4], None, ALU.mult)
        pp_av.consume("vector", av_ops, reads=[stc[:, 0:8]],
                      writes=[otm[:, 0, 0:1], stc[:, 0:8]])

    pend = [None]
    for r in qtiles:
        w = r // 4
        for h in range(8):
            hb = 64 * (h % 2)
            ch = h // 2
            it = bd.ffn_sems["it"][0]
            bd.ffn_sems["it"][0] = it + 1
            par = 4 * (it % 2)
            slot = it % 2
            head(it, r, h, w, hb, ch, par, slot)
            if pend[0] is not None:
                pend[0]()
            pend[0] = (lambda it=it, r=r, h=h, par=par, slot=slot:
                       tail(it, r, h, par, slot))
    pend[0]()
    bd.raw("tensor", lambda e, s=pp_av.cons.h, v=pp_av.cons.count:
           e.wait_ge(s, v))
    bd.raw("tensor", lambda e, s=pp_t.cons.h, v=pp_t.cons.count:
           e.wait_ge(s, v))
    asc.close()

def emit_wo_strips(bd, layer, strips_idx, otm, ofm, wo_v, xres, v_wo, pipes):
    S, sm = bd.S, bd.small
    wsc = contextlib.ExitStack()
    if not hasattr(bd, "_wo_ph"):
        bd._wo_ph = 0
    bd._wo_ph += 1
    pp_ot = PsumPipe(bd, f"fot_{bd._wo_ph}", (128, 128), 2, dtype=BF16,
                     scope=wsc, reg_key="fptr")
    pp_wo = pipes["wo"]

    def mark(eng, sem):
        return bd.release(eng, sem, 1)

    def wait(eng, sem, v):
        bd.raw(eng, lambda e, s=sem.h, v=v: e.wait_ge(s, v))

    wait("tensor", S.wg1, v_wo)
    for si in strips_idx:
        t0 = si * 512
        for tt in range(4):
            for fc in range(4):
                pt = pp_ot.begin()
                bd.raw("tensor", lambda e, pt=pt, tt=tt, fc=fc, si=si:
                       e.transpose(pt[:, :],
                                   otm[:, si * 4 + tt, fc * 128:(fc + 1) * 128],
                                   sm["c_idbf"][:, :]))
                pp_ot.produced()
                pp_ot.consume("vector", lambda e, pt, tt=tt, fc=fc:
                              e.tensor_copy(ofm[:, fc, tt * 128:(tt + 1) * 128],
                                            pt[:, :]),
                              writes=[ofm[:, 0, 0:1]])
        v_of = mark("vector", S.vec)
        wait("tensor", S.vec, v_of)
        for f in range(8):
            pt = pp_wo.begin()
            for k in range(4):
                bd.raw("tensor", lambda e, pt=pt, f=f, k=k:
                       e.matmul(pt[:, 0:512], wo_v[:, k, f * 128:(f + 1) * 128],
                                ofm[:, k, :], start=(k == 0), stop=(k == 3)))
            pp_wo.produced()
            pp_wo.consume("vector", lambda e, pt, f=f, t0=t0:
                          e.scalar_tensor_tensor(
                              xres[:, f, 1 + t0:1 + t0 + 512], pt[:, 0:512],
                              1.0, xres[:, f, 1 + t0:1 + t0 + 512],
                              ALU.mult, ALU.add),
                          reads=[xres[:, 0, 0:1]], writes=[xres[:, 0, 0:1]])
    wsc.close()


def emit_exchange(bd, layer, krot, vtm, xres, xcolstage, halo2, rcv):
    import dataclasses as _dc
    S, sm = bd.S, bd.small
    d = bd.d

    def mark(eng, sem):
        return bd.release(eng, sem, 1)

    def wait(eng, sem, v):
        bd.raw(eng, lambda e, s=sem.h, v=v: e.wait_ge(s, v))

    bd.barrier()
    bd.qdma("sync", S.tp1,
            d["e_in"][:, 0:2048].rearrange("p (a b) -> p a b", a=4),
            krot[:, :, 2048:2560])
    bd.qdma("sync", S.tp1,
            d["e_in"][:, 2048:4128].rearrange("p (a b) -> p a b", a=4),
            vtm[:, 16:20, :])
    bd.qdma("sync", S.tp1,
            d["e_in"][:, 4128:4136].rearrange("p (a b) -> p a b", a=8),
            xcolstage[:, :, :], allow_slow_non_contiguous=True)
    v_snd = bd.qdma("sync", S.tp1,
                    d["e_in"][:, 4136:4144].rearrange("p (a b) -> p a b", a=8),
                    xres[:, :, 2048:2049], allow_slow_non_contiguous=True)
    wait("gpsimd", S.tp1, v_snd)
    S.coll.count += 1
    bd.on("gpsimd", lambda e: e.collective_compute(
        "AllGather", ALU.bypass,
        replica_groups=[[0, 1], [2, 3], [4, 5], [6, 7]],
        ins=[d["e_in"]], outs=[d["e_out"]]).then_inc(S.coll.h, 1))
    v_cc = S.coll.count
    wait("gpsimd", S.coll, v_cc)
    pieces = []
    for ch in range(4):
        pieces.append((ch * 512, 512, krot[:, ch, 0:512],
                       krot[:, ch, 2048:2560]))
    for gg in range(4):
        pieces.append((2048 + gg * VCOLS, VCOLS, vtm[:, gg, :],
                       vtm[:, 16 + gg, :]))
    pieces.append((4128, 16, halo2[:, :, :].rearrange("p a b -> p (a b)"),
                   None))
    for (off, n, dst, mine) in pieces:
        if getattr(bd, "rcv_guard", None) is not None:
            wait("gpsimd", S.vec, bd.rcv_guard)
        bd.qdma("gpsimd", S.gdma, rcv[:, 0, 0:n],
                d["e_out"][0][:, off:off + n])
        vg = bd.qdma("gpsimd", S.gdma, rcv[:, 1, 0:n],
                     d["e_out"][1][:, off:off + n])
        wait("vector", S.gdma, vg)
        if mine is not None:
            bd.ve(lambda e, n=n: e.tensor_tensor(
                rcv[:, 0, 0:n], rcv[:, 0, 0:n], rcv[:, 1, 0:n], ALU.add),
                reads=[rcv[:, 0, 0:1]], writes=[rcv[:, 0, 0:1]])
            bd.ve(lambda e, n=n, dst=dst, mine=mine: e.tensor_tensor(
                dst, rcv[:, 0, 0:n], mine, ALU.subtract),
                reads=[rcv[:, 0, 0:1]], writes=[dst])
        else:
            bd.ve(lambda e, n=n: e.tensor_tensor(
                rcv[:, 0, 0:n], rcv[:, 0, 0:n], rcv[:, 1, 0:n], ALU.add),
                reads=[rcv[:, 0, 0:1]], writes=[rcv[:, 0, 0:1]])
            bd.ve(lambda e: e.tensor_tensor(
                rcv[:, 0, 0:8], rcv[:, 0, 0:8],
                xcolstage[:, :, 0].rearrange("p a -> p a"), ALU.subtract),
                reads=[rcv[:, 0, 0:1]], writes=[rcv[:, 0, 0:1]])
            bd.ve(lambda e: e.tensor_tensor(
                rcv[:, 0, 8:16], rcv[:, 0, 8:16],
                xres[:, :, 2048:2049].rearrange("p a b -> p (a b)"),
                ALU.subtract),
                reads=[rcv[:, 0, 0:1], xres[:, 0, 0:1]],
                writes=[rcv[:, 0, 0:1]])
            om8 = _dc.replace(sm["c_oddmask"][:, 0:1], ap=[[1, 128], [0, 16]])
            bd.ve(lambda e, om8=om8: e.tensor_tensor(
                halo2[:, :, :].rearrange("p a b -> p (a b)"), rcv[:, 0, 0:16],
                om8, ALU.mult),
                reads=[rcv[:, 0, 0:1], sm["c_oddmask"][0:1, 0:1]],
                writes=[halo2[:, 0, 0:1]])
            bd.ve(lambda e: e.tensor_copy(
                xres[:, :, 0:1], halo2[:, :, 0:1]),
                reads=[halo2[:, 0, 0:1]], writes=[xres[:, 0, 0:1]])
        bd.rcv_guard = mark("vector", S.vec)
    v_unp = mark("vector", S.vec)
    bd.barrier()


def emit_mlp(bd, layer, xres, xhat, sqf, hgs, w1bufs, w2parts, mro, ps,
             acol, CH, NPAIR, PIECES, w2row, aw):
    import dataclasses as _dc
    nc, d, S, sm = bd.nc, bd.d, bd.S, bd.small
    stB = bd.sb["st16"]

    def mark(eng, sem):
        return bd.release(eng, sem, 1)

    def wait(eng, sem, v):
        bd.raw(eng, lambda e, s=sem.h, v=v: e.wait_ge(s, v))

    ps.close()
    mps = contextlib.ExitStack()
    sps = contextlib.ExitStack()

    w2D, w2W = w2parts
    for m in range(NPAIR):
        dst = w2D[:, m, :] if m < 16 else w2W[:, m - 16, :]
        r0, nr = w2row[m], CH[m]
        src = d["w2"][layer][r0:r0 + nr, :]
        bd.qdma("gpsimd", S.sdma, dst[0:nr, :], src)
    v_w2 = S.sdma.count

    bd.se(lambda e: e.activation(sqf[:, :, :], xres[:, :, 0:2049], AF.Square),
          reads=[xres[:, 0, 0:1]], writes=[sqf[:, 0, 0:1]])
    v_sq = mark("scalar", S.act)
    pp_s1 = PsumPipe(bd, f"ms1_{layer}", (1, 512), 2, scope=sps, reg_key="ms1")
    pp_s2 = PsumPipe(bd, f"ms2_{layer}", (64, 512), 2, scope=sps, reg_key="ms2")
    wait("tensor", S.act, v_sq)
    for (t0, tn) in STRIPS:
        pt1 = pp_s1.begin()
        for cc in range(8):
            so = (1 if cc < 4 else 0) + t0
            bd.raw("tensor", lambda e, pt=pt1, cc=cc, so=so, tn=tn:
                   e.matmul(pt[:, 0:tn], sm["c_ones"][:, 0:1],
                            xres[:, cc, so:so + tn],
                            start=(cc == 0), stop=(cc == 7)))
        pp_s1.produced()
        pp_s1.consume("vector", lambda e, pt, t0=t0, tn=tn: e.tensor_copy(
            stB[0:1, t0:t0 + tn], pt[0:1, 0:tn]), writes=[stB[:, :]])
        pt2 = pp_s2.begin()
        for cc in range(8):
            so = (1 if cc < 4 else 0) + t0
            bd.raw("tensor", lambda e, pt=pt2, cc=cc, so=so, tn=tn:
                   e.matmul(pt[32:33, 0:tn], sm["c_ones"][:, 0:1],
                            sqf[:, cc, so:so + tn],
                            start=(cc == 0), stop=(cc == 7),
                            tile_position=(0, 32)))
        pp_s2.produced()
        pp_s2.consume("vector", lambda e, pt, t0=t0, tn=tn: e.tensor_copy(
            stB[32:33, t0:t0 + tn], pt[32:33, 0:tn]), writes=[stB[:, :]])
    v_ln = mark("vector", S.vec)
    wait("sync", S.vec, v_ln)
    bd.qdma("sync", S.stW, d["h_stat"][14:15, :], stB[0:1, :])
    v_lnw = bd.qdma("sync", S.stW, d["h_stat"][15:16, :], stB[32:33, :])
    wait("gpsimd", S.stW, v_lnw)
    for r in range(2):
        src = _dc.replace(d["h_stat"][14 + r:15 + r, :],
                          ap=[[0, 128], [1, 2048]])
        bd.qdma("gpsimd", S.brd, mro[:, r, :], src)
    v_mro = S.brd.count
    wait("vector", S.brd, v_mro)
    msc = mro
    mscr = vw2(mro.tensor if False else sqf.tensor, 0, 1, 2048)
    bd.ve(lambda e: e.tensor_scalar(mro[:, 0, :], mro[:, 0, :], 1.0 / D,
                                    None, ALU.mult),
          reads=[mro[:, 0, 0:1]], writes=[mro[:, 0, 0:1]])
    bd.ve(lambda e: e.tensor_tensor(mscr[:, 0, :], mro[:, 0, :],
                                    mro[:, 0, :], ALU.mult),
          reads=[mro[:, 0, 0:1]], writes=[mscr[:, 0, 0:1]])
    bd.ve(lambda e: e.scalar_tensor_tensor(mro[:, 1, :], mro[:, 1, :],
                                           1.0 / D, mscr[:, 0, :],
                                           ALU.mult, ALU.subtract),
          reads=[mro[:, 0, 0:1], mscr[:, 0, 0:1]], writes=[mro[:, 0, 0:1]])
    v_var = mark("vector", S.vec)
    wait("scalar", S.vec, v_var)
    bd.se(lambda e: e.activation(mro[:, 1, :], mro[:, 1, :], AF.Sqrt,
                                 bias=1e-5),
          reads=[mro[:, 0, 0:1]], writes=[mro[:, 0, 0:1]])
    v_sd = mark("scalar", S.act)
    wait("vector", S.act, v_sd)
    def _rec_mro(e):
        with bd.nc.allow_low_precision("bf16 LN rstd table"):
            e.reciprocal(mro[:, 1, :], mro[:, 1, :])
    bd.ve(_rec_mro, reads=[mro[:, 0, 0:1]], writes=[mro[:, 0, 0:1]])
    bd.ve(lambda e: e.tensor_tensor(mro[:, 0, :], mro[:, 0, :],
                                    mro[:, 1, :], ALU.mult),
          reads=[mro[:, 0, 0:1]], writes=[mro[:, 0, 0:1]])
    v_tabr = mark("vector", S.vec)
    for cc in range(8):
        so = 1 if cc < 4 else 0
        bd.ve(lambda e, cc=cc, so=so: e.tensor_tensor(
            xhat[:, cc, :], xres[:, cc, so:so + 2048], mro[:, 1, :], ALU.mult),
            reads=[xres[:, 0, 0:1], mro[:, 0, 0:1]], writes=[xhat[:, 0, 0:1]])
        bd.ve(lambda e, cc=cc: e.tensor_tensor(
            xhat[:, cc, :], xhat[:, cc, :], mro[:, 0, :], ALU.subtract),
            reads=[xhat[:, 0, 0:1], mro[:, 0, 0:1]], writes=[xhat[:, 0, 0:1]])
        bd.ve(lambda e, cc=cc: e.tensor_scalar(
            xhat[:, cc, :], xhat[:, cc, :],
            sm["ln_g_col"][:, layer, cc:cc + 1],
            sm["ln_b_col"][:, layer, cc:cc + 1], ALU.mult, ALU.add),
            reads=[xhat[:, 0, 0:1]], writes=[xhat[:, 0, 0:1]])
    v_xh = mark("vector", S.vec)

    bd.barrier()
    sps.close()
    pp_a = PsumPipe(bd, f"mpa{layer}", (128, 512), 2, scope=mps, reg_key="mpa")
    pp_g = PsumPipe(bd, f"mpg{layer}", (128, 512), 2, scope=mps, reg_key="mpg")
    pp_o = PsumPipe(bd, f"mpo{layer}", (128, 512), 2, scope=mps, reg_key="mpo")
    if not hasattr(bd, "mlp_sems"):
        bd.mlp_sems = (bd.sem("hg"), bd.sem("w1a"), bd.sem("w1b"))
    hsem, w1a, w1b = bd.mlp_sems
    wsem = [w1a, w1b]
    wait("tensor", S.vec, v_xh)
    wait("tensor", S.sdma, v_w2)

    seq = [(si, m) for si in range(4) for m in range(NPAIR)]
    hs_base = hsem.count
    mm2V = {}
    for idx, (si, m) in enumerate(seq):
        t0 = si * 512
        buf = idx % 2
        nch = CH[m]
        c0 = acol[m]
        if idx >= 2:
            bd.raw("gpsimd", lambda e, s=pp_g.prod.h, v=mm2V[idx - 2]:
                   e.wait_ge(s, v))
        v_l = wload(bd, f"w1_{layer}_{idx}", w1bufs[buf][:, :, 0:2 * nch],
                    d["w1p"][layer][:, c0:c0 + 2 * nch]
                    .rearrange("(k p) j -> p k j", p=128), wsem[buf])
        wait("tensor", wsem[buf], v_l)
        ptA = pp_a.begin()
        for k in range(8):
            bd.raw("tensor", lambda e, pt=ptA, k=k, nch=nch, buf=buf, t0=t0:
                   e.matmul(pt[0:nch, 0:512], w1bufs[buf][:, k, 0:nch],
                            xhat[:, k, t0:t0 + 512],
                            start=(k == 0), stop=(k == 7)))
        pp_a.produced()
        ptG = pp_g.begin()
        for k in range(8):
            bd.raw("tensor", lambda e, pt=ptG, k=k, nch=nch, buf=buf, t0=t0:
                   e.matmul(pt[0:nch, 0:512], w1bufs[buf][:, k, nch:2 * nch],
                            xhat[:, k, t0:t0 + 512],
                            start=(k == 0), stop=(k == 7)))
        pp_g.produced()
        mm2V[idx] = pp_g.prod.count
        if idx >= 1:
            wait("scalar", hsem, hs_base + idx)
        pp_a.consume("scalar", lambda e, pt, m=m, nch=nch:
                     e.activation(aw[0:nch, 0, :], pt[0:nch, 0:512],
                                  AF.Identity,
                                  bias=sm["b1p_col"][0:nch, layer, 2 * m:2 * m + 1]),
                     writes=[aw[:, 0, 0:1]])
        pp_g.consume("scalar", lambda e, pt, m=m, nch=nch:
                     e.activation(aw[0:nch, 1, :], pt[0:nch, 0:512],
                                  AF.Gelu,
                                  bias=sm["b1p_col"][0:nch, layer, 2 * m + 1:2 * m + 2]),
                     writes=[aw[:, 0, 0:1]])
        sv = mark("scalar", S.act)
        wait("vector", S.act, sv)
        bd.ve(lambda e, m=m, nch=nch: e.tensor_tensor(
            hgs[0:nch, m, :], aw[0:nch, 0, :], aw[0:nch, 1, :],
            ALU.mult),
            reads=[aw[:, 0, 0:1]], writes=[hgs[:, 0, 0:1]])
        bd.release("vector", hsem, 1)

        if m == NPAIR - 1:
            wait("tensor", hsem, hsem.count)
            for f in range(8):
                ptO = pp_o.begin()
                for mm in range(NPAIR):
                    w2v = w2parts[0][:, mm, :] if mm < 16 else \
                        w2parts[1][:, mm - 16, :]
                    bd.raw("tensor", lambda e, pt=ptO, mm=mm, f=f, w2v=w2v,
                           nck=CH[mm]:
                           e.matmul(pt[:, 0:512],
                                    w2v[0:nck, f * 128:(f + 1) * 128],
                                    hgs[0:nck, mm, :],
                                    start=(mm == 0), stop=(mm == NPAIR - 1)))
                pp_o.produced()
                pp_o.consume("vector", lambda e, pt, f=f, t0=t0:
                             e.scalar_tensor_tensor(
                                 xres[:, f, 1 + t0:1 + t0 + 512], pt[:, 0:512],
                                 sm["b2_col"][:, layer, f:f + 1],
                                 xres[:, f, 1 + t0:1 + t0 + 512],
                                 ALU.add, ALU.add),
                             reads=[xres[:, 0, 0:1], sm["b2_col"][:, 0, 0:1]],
                             writes=[xres[:, 0, 0:1]])
    mps.close()


def emit_final(bd, arA, arB, arC, arD):
    import dataclasses as _dc
    nc, d, S, sm = bd.nc, bd.d, bd.S, bd.small

    def mark(eng, sem):
        return bd.release(eng, sem, 1)

    def wait(eng, sem, v):
        bd.raw(eng, lambda e, s=sem.h, v=v: e.wait_ge(s, v))

    fps = contextlib.ExitStack()
    xres = vw2(arC, 0, 8, 2049)
    x0v = vw2(arA, 0, 8, 2048)
    zv = vw2(arD, 0, 8, 2048)
    ztm = vw2(arB, 0, 16, 1024)
    fst = bd.sbuf("fst", (128, 2, 1024), F32)
    stB = bd.sb["st16"]

    bd.barrier()
    v_x0 = bd.qdma("gpsimd", S.xin, x0v[:, :, :],
                   d["h_x0"].rearrange("p (a b) -> p a b", a=8))
    wait("vector", S.xin, v_x0)
    bd.ve(lambda e: e.tensor_tensor(zv[:, :, :], x0v[:, :, :],
                                    xres[:, :, 1:2049], ALU.add),
          reads=[x0v[:, 0, 0:1], xres[:, 0, 0:1]], writes=[zv[:, 0, 0:1]])
    v_z = mark("vector", S.vec)
    pp_ft = PsumPipe(bd, "fin_t", (128, 128), 4, dtype=BF16, scope=fps)
    wait("tensor", S.vec, v_z)
    for cc in range(8):
        for tt in range(16):
            pt = pp_ft.begin()
            bd.raw("tensor", lambda e, pt=pt, cc=cc, tt=tt: e.transpose(
                pt[:, :], zv[:, cc, tt * 128:(tt + 1) * 128],
                sm["c_idbf"][:, :]))
            pp_ft.produced()
            pp_ft.consume("vector", lambda e, pt, cc=cc, tt=tt: e.tensor_copy(
                ztm[:, tt, cc * 128:(cc + 1) * 128], pt[:, :]),
                writes=[ztm[:, 0, 0:1]])
    v_ztm = mark("vector", S.vec)
    wait("scalar", S.vec, v_ztm)
    sdv = bd.sb["sd_sb"]
    for tt in range(16):
        par = tt % 2
        bd.ve(lambda e, tt=tt, par=par: e.scalar_tensor_tensor(
            fst[:, par, :], ztm[:, tt, :], 1.0, ztm[:, tt, :],
            ALU.mult, ALU.mult,
            accum_out=sdv[:, 0, tt:tt + 1]),
            reads=[ztm[:, 0, 0:1]], writes=[fst[:, 0, 0:1]])
        bd.ve(lambda e, tt=tt: e.tensor_reduce(
            sdv[:, 1, tt:tt + 1], ztm[:, tt, :], AX.X, ALU.add),
            reads=[ztm[:, 0, 0:1]], writes=[bd.sb["sd_sb"][:, 0, 0:1]])
    bd.ve(lambda e: e.tensor_scalar(
        sdv[:, 1, 0:16], sdv[:, 1, 0:16], 1.0 / D, None, ALU.mult),
        reads=[sdv[:, 0, 0:1]], writes=[sdv[:, 0, 0:1]])
    bd.ve(lambda e: e.tensor_scalar(
        sdv[:, 0, 0:16], sdv[:, 0, 0:16], 1.0 / D, None, ALU.mult),
        reads=[sdv[:, 0, 0:1]], writes=[sdv[:, 0, 0:1]])
    bd.ve(lambda e: e.tensor_tensor(
        sdv[:, 2, 0:16], sdv[:, 1, 0:16], sdv[:, 1, 0:16], ALU.mult),
        reads=[sdv[:, 0, 0:1]], writes=[sdv[:, 0, 0:1]])
    bd.ve(lambda e: e.tensor_tensor(
        sdv[:, 0, 0:16], sdv[:, 0, 0:16], sdv[:, 2, 0:16], ALU.subtract),
        reads=[sdv[:, 0, 0:1]], writes=[sdv[:, 0, 0:1]])
    v_v = mark("vector", S.vec)
    wait("scalar", S.vec, v_v)
    bd.se(lambda e: e.activation(
        sdv[:, 0, 0:16], sdv[:, 0, 0:16], AF.Sqrt, bias=1e-5),
        reads=[sdv[:, 0, 0:1]], writes=[sdv[:, 0, 0:1]])
    v_s = mark("scalar", S.act)
    wait("vector", S.act, v_s)
    bd.ve(lambda e: e.reciprocal(sdv[:, 0, 0:16], sdv[:, 0, 0:16]),
          reads=[sdv[:, 0, 0:1]], writes=[sdv[:, 0, 0:1]])
    bd.ve(lambda e: e.tensor_tensor(
        sdv[:, 2, 0:16], sdv[:, 1, 0:16], sdv[:, 0, 0:16], ALU.mult),
        reads=[sdv[:, 0, 0:1]], writes=[sdv[:, 0, 0:1]])
    bd.ve(lambda e: e.tensor_scalar(
        sdv[:, 2, 0:16], sdv[:, 2, 0:16], -1.0, None, ALU.mult),
        reads=[sdv[:, 0, 0:1]], writes=[sdv[:, 0, 0:1]])
    v_r = mark("vector", S.vec)
    wait("scalar", S.vec, v_r)
    for tt in range(16):
        par = tt % 2
        if tt >= 2:
            osem = S.outs if par == 0 else S.out2
            wait("scalar", osem, 16 * (tt // 2))
        bd.se(lambda e, tt=tt, par=par: e.activation(
            fst[:, par, :], ztm[:, tt, :], AF.Identity,
            bias=sdv[:, 2, tt:tt + 1], scale=sdv[:, 0, tt:tt + 1]),
            reads=[ztm[:, 0, 0:1], sdv[:, 0, 0:1]], writes=[fst[:, 0, 0:1]])
        v_n = mark("scalar", S.act)
        if par == 0:
            wait("sync", S.act, v_n)
            bd.qdma("sync", S.outs, bd.d["out"][tt * 128:(tt + 1) * 128, :],
                    fst[:, par, :])
        else:
            wait("gpsimd", S.act, v_n)
            bd.qdma("gpsimd", S.out2,
                    bd.d["out"][tt * 128:(tt + 1) * 128, :], fst[:, par, :])
    bd.raw("sync", lambda e, v=S.outs.count: e.wait_ge(S.outs.h, v))
    bd.raw("sync", lambda e, v=S.out2.count: e.wait_ge(S.out2.h, v))
    bd.raw("gpsimd", lambda e, v=S.out2.count: e.wait_ge(S.out2.h, v))


_CACHE = {}


def _pack_weights(inputs):
    w = {}
    w["Wq"] = inputs["L0_attn_Wq"]
    w["Wk"] = inputs["L0_attn_Wk"]
    w["Wv"] = inputs["L0_attn_Wv"]
    w["Wb"] = inputs["L0_attn_Wb"]
    w["convs"] = np.ascontiguousarray(np.stack([
        inputs["L0_attn_conv_q"][:, 0, :],
        inputs["L0_attn_conv_k"][:, 0, :],
        inputs["L0_attn_conv_v"][:, 0, :]]))
    g = np.asarray(inputs["L0_attn_g"])
    w["g_col"] = np.ascontiguousarray(g.reshape(2, 128).T)
    w["Wo"] = inputs["L0_attn_Wo"]
    w["wqkv"] = inputs["L0_ffn_wqkv"]
    w["qs_col"] = np.ascontiguousarray(
        np.tile(np.asarray(inputs["L0_ffn_q_scale"]), (1, 2)).T[:, :, None])
    w["ks_col"] = np.ascontiguousarray(
        np.tile(np.asarray(inputs["L0_ffn_k_scale"]), (1, 2)).T[:, :, None])
    w["wo_f"] = inputs["L0_ffn_wo"]
    w["ln_g_col"] = np.ascontiguousarray(
        np.asarray(inputs["L0_ffn_ln_g"]).reshape(2, 8, 128).transpose(2, 0, 1))
    w["ln_b_col"] = np.ascontiguousarray(
        np.asarray(inputs["L0_ffn_ln_b"]).reshape(2, 8, 128).transpose(2, 0, 1))
    w1p, b1p = permute_w1(np.asarray(inputs["L0_ffn_w1"]),
                          np.asarray(inputs["L0_ffn_b1"]))
    w["w1p"] = np.ascontiguousarray(w1p)
    CHl = []
    m = 0
    while m * 128 < HID:
        CHl.append(min(128, HID - m * 128))
        m += 1
    b1c = np.zeros((128, 2, 44), np.float32)
    c0 = 0
    for m, nch in enumerate(CHl):
        b1c[0:nch, :, 2 * m] = b1p[:, c0:c0 + nch].T
        b1c[0:nch, :, 2 * m + 1] = b1p[:, c0 + nch:c0 + 2 * nch].T
        c0 += 2 * nch
    w["b1p_col"] = b1c
    w["w2"] = inputs["L0_ffn_w2"]
    w["b2_col"] = np.ascontiguousarray(
        np.asarray(inputs["L0_ffn_b2"]).reshape(2, 8, 128).transpose(2, 0, 1))
    return w


def kernel(**inputs):
    if "bd" not in _CACHE:
        _CACHE["bd"] = build_program()
        _CACHE["tabs"] = host_tables()
    bd = _CACHE["bd"]
    tabs = _CACHE["tabs"]

    x = np.asarray(inputs["L0_attn_x"], dtype=np.float32)
    w = _pack_weights(inputs)
    w = {k: np.ascontiguousarray(np.asarray(v, dtype=np.float32))
         if np.asarray(v).dtype == np.float32 or
         np.asarray(v).dtype == np.float64 else v for k, v in w.items()}

    in_names = [a.memorylocations[0].name
                for a in bd.nc.m.functions[0].allocations
                if getattr(a, "kind", None) == "ExternalInput"]
    in_maps = []
    for c in range(NCORES):
        bb, hh = c // 2, c % 2
        s = hh * T
        xl = np.zeros((T + 3, D), np.float32)
        lo = max(0, s - 3)
        xl[3 - (s - lo):3] = x[bb, lo:s]
        xl[3:] = x[bb, s:s + T]
        feed = dict(w)
        feed["x_loc"] = np.ascontiguousarray(
            xl.reshape(T + 3, 8, 128).transpose(2, 1, 0))
        feed.update(tabs[c])
        m = {}
        for nm in in_names:
            if nm == "partition_id":
                continue
            assert nm in feed, f"missing input {nm}"
            m[nm] = np.ascontiguousarray(feed[nm])
        in_maps.append(m)

    res = run_bass_kernel_spmd(bd.nc, in_maps, core_ids=list(range(NCORES)))
    out = np.zeros((B, L, D), np.float32)
    for c in range(NCORES):
        bb, hh = c // 2, c % 2
        out[bb, hh * T:(hh + 1) * T] = np.asarray(res.results[c]["out"])
    return out


# version 15
# speedup vs baseline: 1.0314x; 1.0020x over previous
import contextlib

import numpy as np
import ml_dtypes

import concourse.bass as bass
import concourse.mybir as mybir
from concourse.bass_utils import run_bass_kernel_spmd

F32 = mybir.dt.float32
BF16 = mybir.dt.bfloat16
AF = mybir.ActivationFunctionType
ALU = mybir.AluOpType
AX = mybir.AxisListType

BF = ml_dtypes.bfloat16

B, L, D = 4, 4096, 1024
T = 2048
NCORES = 8
NH, HD = 4, 256
CC = 64
NCH = T // CC
HEADS, DH = 8, 64
WIN = 512
HID = 2730
NEG = -1.0e30

VCOLS = 8 * 65
PAY = 4 * 512 + 4 * VCOLS + 8 + 8

DEBUG = 0


def host_tables():
    half = np.arange(0, DH, 2, dtype=np.float64)
    inv_freq = 1.0 / 10000.0 ** (half / DH)
    base = (half + 0.4 * DH) / (1.4 * DH)

    def xpos(positions):
        t = positions.astype(np.float64)
        freqs = t[:, None] * inv_freq[None, :]
        freqs = np.concatenate([freqs, freqs], axis=-1)
        power = (t - L // 2) / float(WIN // 2)
        scale = base[None, :] ** power[:, None]
        scale = np.concatenate([scale, scale], axis=-1)
        return np.cos(freqs), np.sin(freqs), scale

    R = np.zeros((DH, DH), np.float32)
    for dd in range(32):
        R[dd, dd + 32] = -1.0
        R[dd + 32, dd] = 1.0
    R2 = np.zeros((128, 128), np.float32)
    R2[:64, :64] = R
    R2[64:, 64:] = R
    R2_lhsT = np.ascontiguousarray(R2.T)

    a = np.arange(128)
    m_diag = np.where(a[:, None] >= a[None, :], 0.0, NEG).astype(np.float32)
    m_far = np.where(a[:, None] <= a[None, :], 0.0, NEG).astype(np.float32)
    zeros128 = np.zeros((128, 128), np.float32)
    full128 = np.full((128, 128), NEG, np.float32)

    selq = np.zeros((4, 128, 16), np.float32)
    selk = np.zeros((4, 128, 16), np.float32)
    for c in range(4):
        selq[c, 0:64, 2 * c] = 1.0
        selq[c, 64:128, 2 * c + 1] = 1.0
        selk[c, 0:64, 8 + 2 * c] = 1.0
        selk[c, 64:128, 8 + 2 * c + 1] = 1.0
    dselq = np.zeros((8, 128, 8), np.float32)
    dselk = np.zeros((8, 128, 8), np.float32)
    for c in range(8):
        dselq[c, :, c // 2] = 1.0
        dselk[c, :, 4 + c // 2] = 1.0
    ones128 = np.ones((128, 1), np.float32)

    eye64 = np.eye(64, dtype=np.float32)
    lmask64 = np.tril(np.ones((64, 64), np.float32), -1)
    umask64 = np.triu(np.ones((64, 64), np.float32), 1)

    per_core = []
    for c in range(NCORES):
        bb, hh = c // 2, c % 2
        s = hh * T
        cs, sn, sc = xpos(np.arange(s, s + T))

        def fm(tab):
            t2 = tab.T.astype(np.float32)
            return np.concatenate([t2, t2], axis=0)

        cm64 = (np.arange(CC) <= bb).astype(np.float32)
        cmask2 = np.broadcast_to(cm64[None, :], (128, CC)).copy()
        per_core.append(dict(
            t_qc=fm(cs * sc).astype(BF), t_qs=fm(sn * sc).astype(BF),
            t_kc=fm(cs / sc).astype(BF), t_ks=fm(sn / sc).astype(BF),
            m_diag=m_diag.astype(BF), m_far=m_far.astype(BF),
            m_pfar0=(full128 if hh == 0 else m_far).astype(BF),
            m_pmid0=(full128 if hh == 0 else zeros128).astype(BF),
            c_r2=R2_lhsT.astype(BF),
            c_selq=selq.transpose(1, 0, 2).astype(BF),
            c_selk=selk.transpose(1, 0, 2).astype(BF),
            c_dselq=dselq.transpose(1, 0, 2).astype(BF),
            c_dselk=dselk.transpose(1, 0, 2).astype(BF),
            c_ones=ones128.astype(BF),
            c_eye64=eye64.astype(BF),
            c_idbf=np.eye(128, dtype=np.float32).astype(BF),
            c_idf32=np.eye(128, dtype=np.float32),
            c_lmask64=lmask64,
            c_cmask2=cmask2.astype(BF),
            c_oddmask=np.full((128, 1), float(hh), np.float32),
        ))
    return per_core


def permute_w1(w1, b1):
    order = []
    m = 0
    while m * 128 < HID:
        lo, hi = m * 128, min((m + 1) * 128, HID)
        order.extend(range(lo, hi))
        order.extend(range(HID + lo, HID + hi))
        m += 1
    order = np.array(order)
    return w1[:, :, order], b1[:, order]


class Sem:
    def __init__(self, handle, name):
        self.h = handle
        self.name = name
        self.count = 0


class Builder:
    ENGS = ("sync", "gpsimd", "vector", "scalar", "tensor")

    def __init__(self):
        self.nc = bass.Bass("TRN2", target_bir_lowering=False)
        self.ctx = contextlib.ExitStack()
        self.streams = {k: [] for k in self.ENGS}
        self._semno = 0
        self.d = {}
        self.sb = {}
        self.dbg = {}

    def sem(self, name=None):
        self._semno += 1
        nm = name or f"s{self._semno}"
        h = self.ctx.enter_context(self.nc.semaphore(nm))
        return Sem(h, nm)

    def on(self, eng, fn):
        self.streams[eng].append(fn)

    def raw(self, eng, call):
        self.on(eng, call)

    def _names(self, aps):
        out = set()
        for ap in aps:
            if ap is None or isinstance(ap, (int, float)):
                continue
            try:
                out.add(ap.tensor.name)
            except AttributeError:
                pass
        return out

    _NOFENCE = ("EventSemaphore", "NoOp", "Drain", "DMACopy",
                "CollectiveCompute")

    def _track(self, eng, ret):
        li = ret[-1] if isinstance(ret, tuple) and ret else ret
        if li is None or not hasattr(li, "concise"):
            self._lastok[eng] = False
            return
        try:
            op = li.concise().split()[1]
        except Exception:
            self._lastok[eng] = False
            return
        if op == "EventSemaphore":
            return
        if op in self._NOFENCE:
            self._lastok[eng] = False
            return
        self._last[eng] = li
        self._lastok[eng] = True

    def _stat(self, key, rode):
        st = self._relstat.setdefault(key, [0, 0, 0])
        st[0 if rode else 1] += 1

    def _stat2(self, key):
        st = self._relstat.setdefault(key, [0, 0, 0])
        st[2] += 1

    def _relinc(self, e, eng, semh, k, nm="?", v=None):
        li = self._last.get(eng)
        if self._lastok.get(eng) and li is not None:
            prev = self._upd.get(id(li), ())
            if prev == ():
                li.then_inc(semh, k)
                self._upd[id(li)] = (li, semh, v) if v is not None else (li,)
                self._stat((eng, nm), True)
                return li
            if len(prev) == 3:
                e.wait_ge(prev[1], prev[2])
                e.nop().then_inc(semh, k)
                self._stat2((eng, nm))
                return li
        self._stat((eng, nm), False)
        e.drain()
        e.nop().then_inc(semh, k)
        return None

    def _fence(self, e, eng, ln=0):
        hz = self._hz[eng]
        li = self._last.get(eng)
        if self._lastok.get(eng) and li is not None:
            prev = self._upd.get(id(li), ())
            if prev == ():
                li.then_inc(hz.h, 1)
                hz.count += 1
                self._upd[id(li)] = (li, hz.h, hz.count)
                e.wait_ge(hz.h, hz.count)
                self._stat((eng, f"<fence>@{ln}"), True)
                return
            if len(prev) == 3:
                e.wait_ge(prev[1], prev[2])
                self._stat2((eng, f"<fence>@{ln}"))
                return
        self._stat((eng, f"<fence>@{ln}"), False)
        e.drain()

    def release(self, eng, sem, k=1):
        import sys
        fr = sys._getframe(1)
        while fr.f_back is not None and fr.f_code.co_name in (
                "release", "produced", "mark", "release_from"):
            fr = fr.f_back
        nm = f"{sem.name}@{fr.f_lineno}"
        sem.count += k
        v = sem.count
        self.on(eng, lambda e, eng=eng, s=sem.h, k=k, nm=nm, v=v:
                self._relinc(e, eng, s, k, nm, v))
        return v

    def fence_after(self, e, eng, inst):
        hz = self._hz[eng]
        hz.count += 1
        inst.then_inc(hz.h, 1)
        self._upd[id(inst)] = (inst, hz.h, hz.count)
        e.wait_ge(hz.h, hz.count)

    def ve(self, call, reads=(), writes=(), eng="vector"):
        import sys
        ln = sys._getframe(1).f_lineno
        r = self._names(reads) | self._names(writes)
        w = self._names(writes)

        def f(e, r=r, w=w, eng=eng, call=call, ln=ln):
            if (r & self._dirty[eng]) or (w & self._reads[eng]):
                self._fence(e, eng, ln)
                self._dirty[eng] = set()
                self._reads[eng] = set()
            ret = call(e)
            self._dirty[eng] |= w
            self._reads[eng] |= r
            return ret
        self.on(eng, f)

    def se(self, call, reads=(), writes=()):
        self.ve(call, reads, writes, eng="scalar")

    def barrier(self, engs=None):
        engs = engs or self.ENGS
        if not hasattr(self, "_barsem"):
            self._barsem = self.sem("bar")
        s = self._barsem
        s.count += len(engs)
        target = s.count

        def mk(eng):
            def f(e):
                self._relinc(e, eng, s.h, 1, "<barrier>", None)
                e.wait_ge(s.h, target)
                self._dirty[eng] = set()
                self._reads[eng] = set()
            return f
        for eng in engs:
            self.on(eng, mk(eng))

    def replay(self):
        self._dirty = {k: set() for k in self.ENGS}
        self._reads = {k: set() for k in self.ENGS}
        self._last = {k: None for k in self.ENGS}
        self._lastok = {k: False for k in self.ENGS}
        self._upd = {}
        self._relstat = {}
        self._hz = {k: self.sem("hz_" + k) for k in
                    ("vector", "scalar", "tensor", "gpsimd")}
        blk = self.ctx.enter_context(self.nc.Block())
        st = self.streams

        def run(e, eng):
            for f in st[eng]:
                self._track(eng, f(e))

        @blk.sync
        def _(e):
            run(e, "sync")

        @blk.gpsimd
        def _(e):
            run(e, "gpsimd")

        @blk.vector
        def _(e):
            run(e, "vector")

        @blk.scalar
        def _(e):
            run(e, "scalar")

        @blk.tensor
        def _(e):
            run(e, "tensor")

    def din(self, name, shape, dtype=F32):
        ap = self.nc.dram_tensor(name, list(shape), dtype, kind="ExternalInput").ap()
        self.d[name] = ap
        return ap

    def dout(self, name, shape, dtype=F32):
        ap = self.nc.dram_tensor(name, list(shape), dtype, kind="ExternalOutput").ap()
        self.d[name] = ap
        return ap

    def dram(self, name, shape, dtype=F32):
        ap = self.nc.dram_tensor(name, list(shape), dtype).ap()
        self.d[name] = ap
        return ap

    def sbuf(self, name, shape, dtype=BF16):
        t = self.ctx.enter_context(self.nc.sbuf_tensor(name, list(shape), dtype))
        self.sb[name] = t
        return t

    def psum(self, name, shape, dtype=F32):
        return self.ctx.enter_context(self.nc.psum_tensor(name, list(shape), dtype))

    def qdma(self, eng, sem, out_ap, in_ap, **kw):
        prior = sem.count
        sem.count += 16
        v = sem.count

        def f(e, out_ap=out_ap, in_ap=in_ap, s=sem.h, prior=prior, kw=kw):
            if prior:
                e.wait_ge(s, prior)
            e.dma_start(out=out_ap, in_=in_ap, **kw).then_inc(s, 16)
        self.on(eng, f)
        return v

    def dbg_out(self, name, src_ap, shape, dtype=F32, wait_sems=()):
        if not DEBUG:
            return
        o = self.dout("dbg_" + name, shape, dtype)
        s = self.sem()

        def f(e, o=o, src=src_ap, s=s):
            e.dma_start(out=o, in_=src).then_inc(s.h, 16)
            e.wait_ge(s.h, 16)
        self.on("sync", f)
        s.count += 16


def build_program():
    bd = Builder()
    nc, d = bd.nc, bd.d

    bd.din("x_loc", (128, 8, T + 3))
    bd.din("Wq", (D, D)); bd.din("Wk", (D, D)); bd.din("Wv", (D, D))
    bd.din("Wb", (D, NH))
    bd.din("convs", (3, D, 4))
    bd.din("g_col", (128, 2))
    bd.din("Wo", (D, D))
    bd.din("wqkv", (2, D, 3 * WIN))
    bd.din("qs_col", (128, 2, 1)); bd.din("ks_col", (128, 2, 1))
    bd.din("wo_f", (2, WIN, D))
    bd.din("ln_g_col", (128, 2, 8)); bd.din("ln_b_col", (128, 2, 8))
    bd.din("w1p", (2, D, 2 * HID)); bd.din("b1p_col", (128, 2, 44))
    bd.din("w2", (2, HID, D)); bd.din("b2_col", (128, 2, 8))
    for nm, shp, dt in [
        ("t_qc", (128, T), BF16), ("t_qs", (128, T), BF16),
        ("t_kc", (128, T), BF16), ("t_ks", (128, T), BF16),
        ("m_diag", (128, 128), BF16), ("m_far", (128, 128), BF16),
        ("m_pfar0", (128, 128), BF16), ("m_pmid0", (128, 128), BF16),
        ("c_r2", (128, 128), BF16),
        ("c_selq", (128, 4, 16), BF16), ("c_selk", (128, 4, 16), BF16),
        ("c_dselq", (128, 8, 8), BF16), ("c_dselk", (128, 8, 8), BF16),
        ("c_ones", (128, 1), BF16), ("c_eye64", (64, 64), BF16),
        ("c_idbf", (128, 128), BF16),
        ("c_lmask64", (64, 64), F32),
        ("c_cmask2", (128, 64), BF16), ("c_oddmask", (128, 1), F32),
    ]:
        bd.din(nm, shp, dt)

    out = bd.dout("out", (T, D), F32)

    h_stat = bd.dram("h_stat", (16, T))
    h_sd = bd.dram("h_sd", (128, 8 * NCH))
    h_x0 = bd.dram("h_x0", (128, 8 * T), BF16)
    e_in = bd.dram("e_in", (128, PAY), BF16)
    e_out = bd.dram("e_out", (2, 128, PAY), BF16)
    d0_in = bd.dram("d0_in", (128, 8))
    d0_out = bd.dram("d0_out", (2, 128, 8))

    FREE_AR = 16424
    arA = bd.sbuf("arA", (128, FREE_AR))
    arB = bd.sbuf("arB", (128, FREE_AR))
    arC = bd.sbuf("arC", (128, FREE_AR))
    arD = bd.sbuf("arD", (128, FREE_AR))
    wgt = bd.sbuf("wgt", (128, 16416))
    tab4 = bd.sbuf("tab4", (128, 4, 2048))
    st16 = bd.sbuf("st16", (128, 2048), F32)
    sttm = bd.sbuf("sttm", (128, 16, 16), F32)
    sd_sb = bd.sbuf("sd_sb", (128, 8, NCH + 1), F32)
    a_sb = bd.sbuf("a_sb", (128, 8, NCH), F32)
    b_sb = bd.sbuf("b_sb", (128, 8, NCH), F32)
    tt_sb = bd.sbuf("tt_sb", (128, NCH // 2, CC))
    nsc = bd.sbuf("nsc", (128, 12, 64))
    small = {}
    for nm, shp, dt in [
        ("m_diag", (128, 128), BF16), ("m_far", (128, 128), BF16),
        ("m_pfar0", (128, 128), BF16), ("m_pmid0", (128, 128), BF16),
        ("c_r2", (128, 128), BF16),
        ("c_selq", (128, 4, 16), BF16), ("c_selk", (128, 4, 16), BF16),
        ("c_dselq", (128, 8, 8), BF16), ("c_dselk", (128, 8, 8), BF16),
        ("c_ones", (128, 1), BF16), ("c_eye64", (64, 64), BF16),
        ("c_idbf", (128, 128), BF16),
        ("c_lmask64", (64, 64), F32),
        ("c_cmask2", (128, 64), BF16),
        ("c_oddmask", (128, 1), F32),
        ("g_col", (128, 2), F32),
        ("qs_col", (128, 2, 1), F32), ("ks_col", (128, 2, 1), F32),
        ("ln_g_col", (128, 2, 8), F32), ("ln_b_col", (128, 2, 8), F32),
        ("b1p_col", (128, 2, 44), F32), ("b2_col", (128, 2, 8), F32),
        ("stcol", (128, 8), F32),
    ]:
        small[nm] = bd.sbuf("s_" + nm, shp, dt)
    bd.small = small

    from types import SimpleNamespace
    S = SimpleNamespace(**{nm: bd.sem(nm) for nm in
         ("xin", "wg0", "wg1", "tload", "stW", "stB", "prj", "cnv", "act",
          "ten", "vec", "mm2", "ev1", "ev2", "coll", "gdma", "sdma",
          "tp0", "tp1", "outs", "out2", "bx0", "brd")})
    bd.S = S

    cst = bd.sbuf("cst", (128, 1), F32)
    nc.const_aps.register(F32, 1e-5, cst[:, 0:1])
    bd.on("vector", lambda e: e.memset(cst[:, 0:1], 1e-5))
    bd.release("vector", S.vec, 1)

    emit_load_consts(bd)
    bd.raw("sync", lambda e, v=S.tload.count: e.wait_ge(S.tload.h, v))
    bd.barrier()
    emit_deltanet(bd, arA, arB, arC, arD, wgt, tab4)
    emit_ffn(bd, arA, arB, arC, arD, wgt, tab4)
    emit_final(bd, arA, arB, arC, arD)

    bd.replay()
    return bd


def vw2(ar, off, n0, n1):
    return ar[:, off:off + n0 * n1].rearrange("p (a b) -> p a b", a=n0)


class PsumPipe:
    def __init__(self, bd, name, shape, n=2, dtype=F32, scope=None,
                 reg_key=None):
        alloc = scope.enter_context if scope is not None else bd.ctx.enter_context
        self.ts = [alloc(bd.nc.psum_tensor(f"{name}{i}", list(shape), dtype))
                   for i in range(n)]
        self.bd = bd
        self.n = n
        if not hasattr(bd, "pipereg"):
            bd.pipereg = {}
        key = reg_key or name
        if key in bd.pipereg:
            self.prod, self.cons, self.st = bd.pipereg[key]
            assert self.st["n"] == n, (key, n, self.st)
        else:
            self.prod = bd.sem(key + "_p")
            self.cons = bd.sem(key + "_c")
            self.st = {"i": self.prod_i0 if False else 0, "consumed": 0, "n": n}
            bd.pipereg[key] = (self.prod, self.cons, self.st)

    @property
    def i(self):
        return self.st["i"]

    @i.setter
    def i(self, v):
        self.st["i"] = v

    @property
    def consumed(self):
        return self.st["consumed"]

    @consumed.setter
    def consumed(self, v):
        self.st["consumed"] = v

    def begin(self):
        t = self.ts[self.i % self.n]
        if self.i >= self.n:
            need = self.i - self.n + 1
            self.bd.raw("tensor", lambda e, s=self.cons.h, v=need: e.wait_ge(s, v))
        self.i += 1
        return t

    def produced(self):
        self.bd.release("tensor", self.prod, 1)

    def consume(self, eng, fn, reads=(), writes=(), release=True):
        self.consumed += 1
        v = self.consumed
        t = self.ts[(self.consumed - 1) % self.n]
        s = self.prod
        c = self.cons
        if release:
            c.count += 1

        cv = c.count

        def f(e, v=v, t=t, fn=fn, release=release, cv=cv):
            e.wait_ge(s.h, v)
            ret = fn(e, t)
            if release:
                self.bd._track(eng, ret)
                self.bd._relinc(e, eng, c.h, 1, "<cons>", cv)
            return ret
        rr = self.bd._names(list(reads) + [t[:, :]]) | self.bd._names(writes)
        ww = self.bd._names(writes)

        def g(e, rr=rr, ww=ww, eng=eng, f=f):
            if (rr & self.bd._dirty[eng]) or (ww & self.bd._reads[eng]):
                self.bd._fence(e, eng)
                self.bd._dirty[eng] = set()
                self.bd._reads[eng] = set()
            ret = f(e)
            self.bd._dirty[eng] |= ww
            self.bd._reads[eng] |= rr
            return ret
        self.bd.on(eng, g)

    def release_from(self, eng):
        self.bd.release(eng, self.cons, 1)

    def peek(self):
        return self.ts[self.consumed % self.n]

    def wait_in(self, eng):
        self.consumed += 1
        v = self.consumed
        t = self.ts[(self.consumed - 1) % self.n]
        self.bd.on(eng, lambda e, s=self.prod.h, v=v: e.wait_ge(s, v))
        return t


PSTRIPS = [(0, 512), (512, 512), (1024, 512), (1536, 512), (2048, 3)]
STRIPS = [(0, 512), (512, 512), (1024, 512), (1536, 512)]


def emit_load_consts(bd):
    d, S, sm = bd.d, bd.S, bd.small
    def ld(name, dst_ap=None):
        src = d[name]
        dst = (sm[name][...] if dst_ap is None else dst_ap)
        prior = S.tload.count
        S.tload.count += 16

        def f(e, dst=dst, src=src, prior=prior):
            if prior:
                e.wait_ge(S.tload.h, prior)
            e.dma_start(out=dst, in_=src).then_inc(S.tload.h, 16)
        bd.on("sync", f)

    for nm in ("m_diag", "m_far", "m_pfar0", "m_pmid0", "c_r2",
               "c_selq", "c_selk", "c_dselq", "c_dselk", "c_ones",
               "c_eye64", "c_idbf", "c_lmask64", "c_cmask2", "c_oddmask",
               "g_col", "qs_col", "ks_col", "ln_g_col", "ln_b_col",
               "b1p_col", "b2_col"):
        ld(nm)
    convs = bd.sbuf("convs_sb", (128, 3, 8, 4), F32)
    bd.convs_sb = convs
    src = d["convs"].rearrange("w (k p) t -> p w k t", p=128)
    bd.qdma("sync", S.tload, convs[...], src)
    idf = bd.sbuf("idf32", (128, 128), F32)
    bd.idf32 = idf
    eye = bd.din("c_idf32", (128, 128))
    bd.qdma("sync", S.tload, idf[...], eye)


def wload(bd, name, dst_view, src_ap, sem):
    prior = sem.count
    sem.count += 16
    v = sem.count

    def f(e, dst=dst_view, src=src_ap, s=sem.h, prior=prior):
        if prior:
            e.wait_ge(s, prior)
        e.dma_start(out=dst, in_=src).then_inc(s, 16)
    bd.on("gpsimd", f)
    return v


def emit_deltanet(bd, arA, arB, arC, arD, wgt, tab4):
    import dataclasses
    nc, d, S, sm = bd.nc, bd.d, bd.S, bd.small
    ps = contextlib.ExitStack()

    xv = vw2(arA, 0, 8, 2051)
    prj = vw2(arB, 0, 8, 2051)
    qh = vw2(arC, 0, 8, 2048)
    kr = vw2(arD, 0, 8, 2048)
    vv = vw2(arA, 0, 8, 2048)
    vbt = vw2(arB, 0, 16, 1024)
    kbt = vw2(arA, 0, 16, 1024)
    wA = vw2(wgt, 0, 8, 1024)
    wB = vw2(wgt, 8 * 1024, 8, 1024)
    wb_sb = vw2(wgt, 2 * 8 * 1024, 8, 4)

    stB = bd.sb["st16"]
    sttm = bd.sb["sttm"]
    sd = bd.sb["sd_sb"]
    a_sb, b_sb = bd.sb["a_sb"], bd.sb["b_sb"]
    tts = bd.sb["tt_sb"]
    nscr = bd.sb["nsc"]

    def wsrc(nm):
        return d[nm].rearrange("(k p) j -> p k j", p=128)
    v_wb = wload(bd, "Wb", wb_sb[:, :, :], wsrc("Wb"), S.wg1)
    bd.qdma("gpsimd", S.xin, xv[:, :, :], d["x_loc"])
    v_wq = wload(bd, "Wq", wA[:, :, :], wsrc("Wq"), S.wg0)
    v_wk = wload(bd, "Wk", wB[:, :, :], wsrc("Wk"), S.wg1)

    def mark(eng, sem):
        return bd.release(eng, sem, 1)

    def wait(eng, sem, v):
        bd.raw(eng, lambda e, s=sem.h, v=v: e.wait_ge(s, v))

    pp_b = PsumPipe(bd, "psb", (64, 512), 2, scope=ps)
    wait("tensor", S.xin, 16)
    wait("tensor", S.wg1, v_wb)
    for (t0, tn) in STRIPS:
        pt = pp_b.begin()
        for k in range(8):
            bd.raw("tensor", lambda e, pt=pt, k=k, t0=t0, tn=tn:
                   e.matmul(pt[32:36, 0:tn], wb_sb[:, k, :],
                            xv[:, k, 3 + t0:3 + t0 + tn],
                            start=(k == 0), stop=(k == 7),
                            tile_position=(0, 32)))
        pp_b.produced()
        pp_b.consume("scalar", lambda e, pt, t0=t0, tn=tn: e.activation(
            stB[32:36, t0:t0 + tn], pt[32:36, 0:tn], AF.Sigmoid),
            writes=[stB[:, :]])

    pp_p = PsumPipe(bd, "psp", (128, 512), 2, scope=ps)
    convs = bd.convs_sb

    def do_proj(widx, wbuf, wv, wsem, dst):
        wait("tensor", wsem, wv)
        for f in range(8):
            for (t0, tn) in PSTRIPS:
                pt = pp_p.begin()
                for k in range(8):
                    bd.raw("tensor", lambda e, pt=pt, f=f, k=k, t0=t0, tn=tn,
                           wbuf=wbuf:
                           e.matmul(pt[:, 0:tn],
                                    wbuf[:, k, f * 128:(f + 1) * 128],
                                    xv[:, k, t0:t0 + tn],
                                    start=(k == 0), stop=(k == 7)))
                pp_p.produced()
                pp_p.consume("scalar", lambda e, pt, f=f, t0=t0, tn=tn:
                             e.activation(prj[:, f, t0:t0 + tn], pt[:, 0:tn],
                                          AF.Copy),
                             writes=[prj[:, :, :]])
        av = mark("scalar", S.act)
        wait("vector", S.act, av)
        for cc in range(8):
            w0 = convs[:, widx, cc, 0:1]
            bd.ve(lambda e, cc=cc, w0=w0, dst=dst: e.tensor_scalar(
                dst[:, cc, :], prj[:, cc, 0:2048], w0, None, ALU.mult),
                reads=[prj[:, 0, 0:1]], writes=[dst[:, 0, 0:1]])
            for tau in range(1, 4):
                wc = convs[:, widx, cc, tau:tau + 1]
                bd.ve(lambda e, cc=cc, wc=wc, tau=tau, dst=dst:
                      e.scalar_tensor_tensor(
                          dst[:, cc, :], prj[:, cc, tau:tau + 2048], wc,
                          dst[:, cc, :], ALU.mult, ALU.add),
                      reads=[prj[:, 0, 0:1], dst[:, 0, 0:1]],
                      writes=[dst[:, 0, 0:1]])
        vv_ = mark("vector", S.vec)
        wait("scalar", S.vec, vv_)
        bd.se(lambda e, dst=dst: e.activation(dst[:, :, :], dst[:, :, :], AF.Silu),
              reads=[dst[:, 0, 0:1]], writes=[dst[:, 0, 0:1]])
        return mark("scalar", S.act)

    q_silu = do_proj(0, wA, v_wq, S.wg0, qh)
    tq = mark("tensor", S.ten)
    wait("gpsimd", S.ten, tq)
    v_wv = wload(bd, "Wv", wA[:, :, :], wsrc("Wv"), S.wg0)

    ps_ss4 = [ps.enter_context(nc.psum_tensor(f"ss4_{i}", [8, 512], F32))
              for i in range(4)]

    def do_square(src):
        bd.se(lambda e, src=src: e.activation(prj[:, :, 0:2048], src[:, :, :],
                                              AF.Square),
              reads=[src[:, 0, 0:1]], writes=[prj[:, 0, 0:1]])
        return mark("scalar", S.act)

    def sumsq_mms(sel, start):
        for si, (t0, tn) in enumerate(STRIPS):
            for cc in range(8):
                bd.raw("tensor", lambda e, si=si, cc=cc, t0=t0, tn=tn,
                       sel=sel, st=(start and cc == 0),
                       sp=((not start) and cc == 7):
                       e.matmul(ps_ss4[si][:, 0:tn], sel[:, cc, 0:8],
                                prj[:, cc, t0:t0 + tn], start=st, stop=sp))

    sq_q = do_square(qh)
    wait("tensor", S.act, sq_q)
    sumsq_mms(sm["c_dselq"], True)
    t_sq = mark("tensor", S.ten)

    wait("scalar", S.ten, t_sq)
    k_silu = do_proj(1, wB, v_wk, S.wg1, kr)
    tk = mark("tensor", S.ten)
    wait("gpsimd", S.ten, tk)
    v_wo = wload(bd, "Wo", wB[:, :, :], wsrc("Wo"), S.wg1)

    sq_k = do_square(kr)
    wait("tensor", S.act, sq_k)
    sumsq_mms(sm["c_dselk"], False)
    t_ssk = mark("tensor", S.ten)

    wait("scalar", S.ten, t_ssk)
    v_silu = do_proj(2, wA, v_wv, S.wg0, vv)

    wait("vector", S.ten, t_ssk)
    psB = contextlib.ExitStack()
    for si, (t0, tn) in enumerate(STRIPS):
        bd.ve(lambda e, si=si, t0=t0, tn=tn: e.tensor_copy(
            stB[0:8, t0:t0 + tn], ps_ss4[si][:, 0:tn]),
            writes=[stB[:, :]])
    bd.ve(lambda e: e.reciprocal(stB[0:8, :], stB[0:8, :]),
          reads=[stB[:, :]], writes=[stB[:, :]])
    v_rec = mark("vector", S.vec)
    wait("scalar", S.vec, v_rec)
    bd.se(lambda e: e.activation(stB[0:8, :], stB[0:8, :], AF.Sqrt),
          reads=[stB[:, :]], writes=[stB[:, :]])
    v_inv = mark("scalar", S.act)

    bd.barrier()
    ps.close()
    pp_t = PsumPipe(bd, "pst", (128, 16), 2, dtype=F32, scope=psB)
    wait("tensor", S.act, v_inv)
    for tile in range(16):
        pt = pp_t.begin()
        bd.raw("tensor", lambda e, pt=pt, tile=tile: e.transpose(
            pt[:, 0:8], stB[0:8, tile * 128:(tile + 1) * 128],
            bd.idf32[0:8, 0:8]))
        bd.raw("tensor", lambda e, pt=pt, tile=tile: e.transpose(
            pt[:, 8:12], stB[32:36, tile * 128:(tile + 1) * 128],
            bd.idf32[32:36, 32:36], tile_position=(32, 0)))
        pp_t.produced()
        pp_t.consume("vector", lambda e, pt, tile=tile: e.tensor_copy(
            sttm[:, tile, 0:12], pt[:, 0:12]), writes=[sttm[:, :, :]])
    bd.ve(lambda e: e.tensor_tensor(sttm[:, :, 12:16], sttm[:, :, 8:12],
                                    sttm[:, :, 4:8], ALU.mult),
          reads=[sttm[:, :, :]], writes=[sttm[:, :, :]])
    v_tmr = mark("vector", S.vec)

    wait("sync", S.act, v_inv)
    v_stw = bd.qdma("sync", S.stW, d["h_stat"][0:8, :], stB[0:8, :])
    wait("gpsimd", S.stW, v_stw)

    def bcast4(dst, src_row0):
        src = dataclasses.replace(
            d["h_stat"][src_row0:src_row0 + 4, :],
            ap=[[0, 64], [2048, 4], [1, 2048]])
        bd.qdma("gpsimd", S.brd, dst[0:64, :, :], src)
        bd.qdma("gpsimd", S.brd, dst[64:128, :, :], src)
        return S.brd.count

    v_tabq = bcast4(tab4, 0)
    wait("vector", S.act, q_silu)
    wait("vector", S.brd, v_tabq)
    for cc in range(8):
        bd.ve(lambda e, cc=cc: e.tensor_tensor(
            qh[:, cc, :], qh[:, cc, :], tab4[:, cc // 2, :], ALU.mult),
            reads=[qh[:, 0, 0:1], tab4[:, 0, 0:1]], writes=[qh[:, 0, 0:1]])
    v_qhat = mark("vector", S.vec)

    wait("gpsimd", S.vec, v_qhat)
    v_tabk = bcast4(tab4, 4)

    wait("vector", S.brd, v_tabk)
    wait("vector", S.act, k_silu)
    for cc in range(8):
        bd.ve(lambda e, cc=cc: e.tensor_tensor(
            prj[:, cc, 0:2048], qh[:, cc, :], kr[:, cc, :], ALU.mult),
            reads=[qh[:, 0, 0:1], kr[:, 0, 0:1]], writes=[prj[:, 0, 0:1]])
        bd.ve(lambda e, cc=cc: e.tensor_tensor(
            prj[:, cc, 0:2048], prj[:, cc, 0:2048], tab4[:, cc // 2, :],
            ALU.mult),
            reads=[prj[:, 0, 0:1], tab4[:, 0, 0:1]], writes=[prj[:, 0, 0:1]])
    v_pqk = mark("vector", S.vec)
    pp_qk = PsumPipe(bd, "psqk", (1, 512), 2, scope=psB)
    wait("tensor", S.vec, v_pqk)
    for (t0, tn) in STRIPS:
        pt = pp_qk.begin()
        for cc in range(8):
            bd.raw("tensor", lambda e, pt=pt, cc=cc, t0=t0, tn=tn:
                   e.matmul(pt[:, 0:tn], sm["c_ones"][:, 0:1],
                            prj[:, cc, t0:t0 + tn],
                            start=(cc == 0), stop=(cc == 7)))
        pp_qk.produced()
        pp_qk.consume("vector", lambda e, pt, t0=t0, tn=tn: e.tensor_copy(
            stB[0:1, t0:t0 + tn], pt[0:1, 0:tn]),
            writes=[stB[:, :]])
    v_smrow = mark("vector", S.vec)
    wait("sync", S.vec, v_smrow)
    v_smw = bd.qdma("sync", S.stW, d["h_stat"][12:13, :], stB[0:1, :])

    bd.barrier()
    idbf = sm["c_idbf"]
    pp_tt = PsumPipe(bd, "pstt", (128, 128), 4, dtype=BF16, scope=psB)
    for cc in range(8):
        for tt in range(16):
            pt = pp_tt.begin()
            bd.raw("tensor", lambda e, pt=pt, cc=cc, tt=tt: e.transpose(
                pt[:, :], vv[:, cc, tt * 128:(tt + 1) * 128], idbf[...]))
            pp_tt.produced()
            pp_tt.consume("vector", lambda e, pt, cc=cc, tt=tt: e.tensor_copy(
                vbt[:, tt, cc * 128:(cc + 1) * 128], pt[:, :]),
                writes=[vbt[:, 0, 0:1]])
    for tt in range(16):
        for h in range(4):
            bd.ve(lambda e, tt=tt, h=h: e.tensor_scalar(
                vbt[:, tt, h * 256:(h + 1) * 256],
                vbt[:, tt, h * 256:(h + 1) * 256],
                sttm[:, tt, 8 + h:9 + h], None, ALU.mult),
                reads=[vbt[:, 0, 0:1], sttm[:, 0, 0:1]],
                writes=[vbt[:, 0, 0:1]])
    v_vbt = mark("vector", S.vec)
    wait("tensor", S.vec, v_vbt)
    for cc in range(8):
        for tt in range(16):
            pt = pp_tt.begin()
            bd.raw("tensor", lambda e, pt=pt, cc=cc, tt=tt: e.transpose(
                pt[:, :], kr[:, cc, tt * 128:(tt + 1) * 128], idbf[...]))
            pp_tt.produced()
            pp_tt.consume("vector", lambda e, pt, cc=cc, tt=tt: e.tensor_copy(
                kbt[:, tt, cc * 128:(cc + 1) * 128], pt[:, :]),
                writes=[kbt[:, 0, 0:1]])
    for tt in range(16):
        for h in range(4):
            bd.ve(lambda e, tt=tt, h=h: e.tensor_scalar(
                kbt[:, tt, h * 256:(h + 1) * 256],
                kbt[:, tt, h * 256:(h + 1) * 256],
                sttm[:, tt, 12 + h:13 + h], None, ALU.mult),
                reads=[kbt[:, 0, 0:1], sttm[:, 0, 0:1]],
                writes=[kbt[:, 0, 0:1]])
    v_kbt = mark("vector", S.vec)

    bd.barrier()
    psB.close()
    ps1 = contextlib.ExitStack()
    pp_g = PsumPipe(bd, "pg", (128, 128), 2, scope=ps1)
    pp_y = PsumPipe(bd, "py", (128, 64), 2, scope=ps1)
    pp_x = PsumPipe(bd, "px", (128, 128), 2, scope=ps1)
    pp_wa = PsumPipe(bd, "pwa", (128, 512), 1, scope=ps1)
    pp_wb = PsumPipe(bd, "pwb", (128, 512), 1, scope=ps1)

    n_sl = nscr[:, 0, :]; n_slt = nscr[:, 1, :]
    n_x = nscr[:, 2, :]; n_xt = nscr[:, 3, :]; n_z = nscr[:, 4, :]
    n_mt = nscr[:, 5, :]
    loc = contextlib.ExitStack()

    def lsb(name, shape, dtype=BF16):
        return loc.enter_context(bd.nc.sbuf_tensor(name, list(shape), dtype))
    gsc = lsb("gsc", (128, 128), F32)
    apair = lsb("apair", (128, 64), F32)
    eye2 = lsb("eye2", (128, 64))
    twoi = lsb("twoi", (128, 64))
    lm2 = lsb("lm2", (128, 64), F32)
    ksc = lsb("ksc", (128, 8, 128))
    puw = lsb("puw", (128, 8, 64))
    usc = lsb("usc", (128, 8, 64))
    d0r = lsb("d0r", (128, 24), F32)

    for (dst, src) in ((eye2, sm["c_eye64"]), (lm2, sm["c_lmask64"])):
        for hh2 in range(2):
            bd.qdma("sync", S.tp0, dst[hh2 * 64:(hh2 + 1) * 64, :],
                    src[0:64, :])
    wait("vector", S.tp0, S.tp0.count)
    bd.ve(lambda e: e.tensor_scalar(twoi[:, :], eye2[:, :], 2.0, None, ALU.mult),
          reads=[eye2[:, 0:1]], writes=[twoi[:, 0:1]])
    v_eye = mark("vector", S.vec)
    wait("tensor", S.vec, v_eye)

    gsc2 = lsb("gsc2", (128, 128), F32)
    apair2 = lsb("apair2", (128, 64), F32)

    NEWT = 5
    GQ = (gsc, gsc2)
    AQ = (apair, apair2)
    for p in range(8):
        ms = (2 * p, 2 * p + 1)
        for q, m in enumerate(ms):
            tsl = slice(m * 128, (m + 1) * 128)
            gq, aq = GQ[q], AQ[q]
            for h in range(4):
                pt = pp_g.begin()
                for j, cc in enumerate((2 * h, 2 * h + 1)):
                    bd.raw("tensor", lambda e, pt=pt, cc=cc, tsl=tsl, j=j:
                           e.matmul(pt[:, :], kr[:, cc, tsl], kr[:, cc, tsl],
                                    start=(j == 0), stop=(j == 1)))
                pp_g.produced()

                def evac_g(e, pt, h=h, m=m, tsl=tsl, gq=gq):
                    return e.scalar_tensor_tensor(
                        gq[:, :], pt[:, :], sttm[:, m, 12 + h:13 + h],
                        tab4[:, h, tsl], ALU.mult, ALU.mult)
                pp_g.consume("vector", evac_g,
                             reads=[sttm[:, 0, 0:1], tab4[:, 0, 0:1]],
                             writes=[gq[:, 0:1]])
                if h == 0:
                    bd.ve(lambda e, gq=gq, aq=aq: e.tensor_copy(
                        aq[0:64, :], gq[0:64, 0:64]),
                        reads=[gq[:, 0:1]], writes=[aq[:, 0:1]])
                    bd.ve(lambda e, gq=gq, aq=aq: e.tensor_copy(
                        aq[64:128, :], gq[64:128, 64:128]),
                        reads=[gq[:, 0:1]], writes=[aq[:, 0:1]])
                else:
                    bd.ve(lambda e, gq=gq, aq=aq: e.tensor_tensor(
                        aq[0:64, :], aq[0:64, :], gq[0:64, 0:64], ALU.add),
                        reads=[gq[:, 0:1], aq[:, 0:1]], writes=[aq[:, 0:1]])
                    bd.ve(lambda e, gq=gq, aq=aq: e.tensor_tensor(
                        aq[64:128, :], aq[64:128, :], gq[64:128, 64:128],
                        ALU.add),
                        reads=[gq[:, 0:1], aq[:, 0:1]], writes=[aq[:, 0:1]])
        for q in range(2):
            gq, aq = GQ[q], AQ[q]
            bd.ve(lambda e, gq=gq, aq=aq: e.tensor_tensor(
                gq[:, 0:64], aq[:, :], lm2[:, :], ALU.mult),
                reads=[aq[:, 0:1], lm2[:, 0:1]], writes=[gq[:, 0:1]])
            bd.ve(lambda e, gq=gq, q=q: e.tensor_copy(
                nscr[:, 6 * q + 0, :], gq[:, 0:64]),
                reads=[gq[:, 0:1]], writes=[nscr[:, 6 * q, 0:1]])
        v_sl = mark("vector", S.vec)
        wait("tensor", S.vec, v_sl)
        for q in range(2):
            gq = GQ[q]
            pt = pp_y.begin()
            bd.raw("tensor", lambda e, pt=pt, gq=gq: e.transpose(
                pt[0:64, :], gq[0:64, 0:64], bd.idf32[0:64, 0:64]))
            bd.raw("tensor", lambda e, pt=pt, gq=gq: e.matmul(
                pt[64:128, :], gq[64:128, 0:64], bd.idf32[64:128, 64:128],
                start=True, stop=True, tile_position=(64, 64)))
            pp_y.produced()
            pp_y.consume("vector", lambda e, pt, q=q: e.tensor_copy(
                nscr[:, 6 * q + 1, :], pt[:, :]),
                writes=[nscr[:, 6 * q + 1, 0:1]])
        for q in range(2):
            b0 = 6 * q
            bd.ve(lambda e, b0=b0: e.tensor_tensor(
                nscr[:, b0 + 2, :], eye2[:, :], nscr[:, b0 + 0, :],
                ALU.subtract),
                reads=[nscr[:, b0, 0:1], eye2[:, 0:1]],
                writes=[nscr[:, b0 + 2, 0:1]])
            bd.ve(lambda e, b0=b0: e.tensor_tensor(
                nscr[:, b0 + 3, :], eye2[:, :], nscr[:, b0 + 1, :],
                ALU.subtract),
                reads=[nscr[:, b0 + 1, 0:1], eye2[:, 0:1]],
                writes=[nscr[:, b0 + 3, 0:1]])
            bd.ve(lambda e, b0=b0: e.tensor_tensor(
                nscr[:, b0 + 5, :], eye2[:, :], nscr[:, b0 + 1, :], ALU.add),
                reads=[nscr[:, b0 + 1, 0:1], eye2[:, 0:1]],
                writes=[nscr[:, b0 + 5, 0:1]])
        v_x0 = mark("vector", S.vec)
        wait("tensor", S.vec, v_x0)
        for it in range(NEWT):
            for q in range(2):
                b0 = 6 * q
                pt = pp_y.begin()
                for (rg, sl2) in ((0, slice(0, 64)), (64, slice(64, 128))):
                    bd.raw("tensor", lambda e, pt=pt, rg=rg, sl2=sl2, b0=b0:
                           e.matmul(pt[sl2, :], nscr[sl2, b0 + 5, :],
                                    nscr[sl2, b0 + 2, :],
                                    start=True, stop=True,
                                    tile_position=(rg, rg)))
                pp_y.produced()
                pp_y.consume("vector", lambda e, pt, b0=b0: e.tensor_tensor(
                    nscr[:, b0 + 4, :], twoi[:, :], pt[:, :], ALU.subtract),
                    reads=[twoi[:, 0:1]], writes=[nscr[:, b0 + 4, 0:1]])
            v_z = mark("vector", S.vec)
            wait("tensor", S.vec, v_z)
            for q in range(2):
                b0 = 6 * q
                pt = pp_x.begin()
                for (rg, sl2) in ((0, slice(0, 64)), (64, slice(64, 128))):
                    bd.raw("tensor", lambda e, pt=pt, rg=rg, sl2=sl2, b0=b0:
                           e.matmul(pt[sl2, 0:64], nscr[sl2, b0 + 3, :],
                                    nscr[sl2, b0 + 4, :],
                                    start=True, stop=True,
                                    tile_position=(rg, rg)))
                    bd.raw("tensor", lambda e, pt=pt, rg=rg, sl2=sl2, b0=b0:
                           e.matmul(pt[sl2, 64:128], nscr[sl2, b0 + 4, :],
                                    nscr[sl2, b0 + 3, :],
                                    start=True, stop=True,
                                    tile_position=(rg, rg)))
                pp_x.produced()
                pp_x.consume("vector", lambda e, pt, b0=b0: (
                    e.tensor_copy(nscr[:, b0 + 2, :], pt[:, 0:64]),
                    e.tensor_copy(nscr[:, b0 + 3, :], pt[:, 64:128])),
                    writes=[nscr[:, b0 + 2, 0:1], nscr[:, b0 + 3, 0:1]])
            v_it = mark("vector", S.vec)
            wait("tensor", S.vec, v_it)
        for q, m in enumerate(ms):
            bd.ve(lambda e, m=m, b0=6 * q: e.scalar_tensor_tensor(
                tts[:, m, :], nscr[:, b0 + 3, :], 3.0, twoi[:, :],
                ALU.mult, ALU.subtract),
                reads=[nscr[:, 6 * q + 3, 0:1], twoi[:, 0:1]],
                writes=[tts[:, 0, 0:1]])
        v_tt = mark("vector", S.vec)
        wait("tensor", S.vec, v_tt)
        for q, m in enumerate(ms):
            tsl = slice(m * 128, (m + 1) * 128)
            bd.ve(lambda e, tsl=tsl: e.tensor_tensor(
                ksc[:, 0:8:2, :], kr[:, 0:8:2, tsl], tab4[:, :, tsl],
                ALU.mult),
                reads=[kr[:, 0, 0:1], tab4[:, 0, 0:1]], writes=[ksc[:, 0, 0:1]])
            bd.ve(lambda e, tsl=tsl: e.tensor_tensor(
                ksc[:, 1:8:2, :], kr[:, 1:8:2, tsl], tab4[:, :, tsl],
                ALU.mult),
                reads=[kr[:, 0, 0:1], tab4[:, 0, 0:1]], writes=[ksc[:, 0, 0:1]])
            for (pipe, rg, sl2) in ((pp_wa, 0, slice(0, 64)),
                                    (pp_wb, 64, slice(64, 128))):
                ptw = pipe.begin()
                for cc in range(8):
                    bd.raw("tensor", lambda e, ptw=ptw, cc=cc, m=m, rg=rg,
                           sl2=sl2:
                           e.matmul(ptw[:, cc * 64:(cc + 1) * 64],
                                    kbt[sl2, m, cc * 128:(cc + 1) * 128],
                                    tts[sl2, m, :], start=True, stop=True,
                                    tile_position=(rg, 0)))
                pipe.produced()

            def cons_ab(pipe, dst, n, io):
                pipe.consume("vector", lambda e, pt, io=io: e.tensor_tensor(
                    puw[:, :, :], ksc[:, :, io:io + 64],
                    pt[:, :].rearrange("p (c i) -> p c i", c=8), ALU.mult),
                    reads=[ksc[:, 0, 0:1]], writes=[puw[:, 0, 0:1]])
                bd.ve(lambda e, n=n, dst=dst: e.tensor_reduce(
                    dst[:, :, n:n + 1], puw[:, :, :], AX.X, ALU.add),
                    reads=[puw[:, 0, 0:1]], writes=[dst[:, 0, 0:1]])
            cons_ab(pp_wa, a_sb, 2 * m, 0)
            for (pipe, rg, sl2) in ((pp_wa, 0, slice(0, 64)),
                                    (pp_wb, 64, slice(64, 128))):
                ptu = pipe.begin()
                for cc in range(8):
                    bd.raw("tensor", lambda e, ptu=ptu, cc=cc, m=m, rg=rg,
                           sl2=sl2:
                           e.matmul(ptu[:, cc * 64:(cc + 1) * 64],
                                    vbt[sl2, m, cc * 128:(cc + 1) * 128],
                                    tts[sl2, m, :], start=True, stop=True,
                                    tile_position=(rg, 0)))
                pipe.produced()
            cons_ab(pp_wb, a_sb, 2 * m + 1, 64)
            cons_ab(pp_wa, b_sb, 2 * m, 0)
            cons_ab(pp_wb, b_sb, 2 * m + 1, 64)

    bd.ve(lambda e: e.tensor_scalar(a_sb[:, :, :], a_sb[:, :, :], -1.0, 1.0,
                                    ALU.mult, ALU.add),
          reads=[a_sb[:, 0, 0:1]], writes=[a_sb[:, 0, 0:1]])
    for cc in range(8):
        bd.ve(lambda e, cc=cc: e.tensor_tensor_scan(
            sd[:, cc, 1:33], a_sb[:, cc, :], b_sb[:, cc, :], 0.0,
            ALU.mult, ALU.add),
            reads=[a_sb[:, 0, 0:1], b_sb[:, 0, 0:1]], writes=[sd[:, 0, 0:1]])
    v_scanA = mark("vector", S.vec)
    wait("sync", S.vec, v_scanA)
    v_d0w = bd.qdma("sync", S.stW, d["d0_in"], sd[:, :, 32:33],
                    allow_slow_non_contiguous=True)
    wait("gpsimd", S.stW, v_d0w)
    S.coll.count += 1
    bd.on("gpsimd", lambda e: e.collective_compute(
        "AllGather", ALU.bypass,
        replica_groups=[[0, 1], [2, 3], [4, 5], [6, 7]],
        ins=[d["d0_in"]], outs=[d["d0_out"]]).then_inc(S.coll.h, 1))
    v_cc0 = S.coll.count
    wait("gpsimd", S.coll, v_cc0)
    bd.qdma("gpsimd", S.gdma, d0r[:, 0:8], d["d0_out"][0])
    v_d0r = bd.qdma("gpsimd", S.gdma, d0r[:, 8:16], d["d0_out"][1])
    wait("vector", S.gdma, v_d0r)
    bd.ve(lambda e: e.tensor_tensor(d0r[:, 16:24], d0r[:, 0:8], d0r[:, 8:16],
                                    ALU.add),
          reads=[d0r[:, 0:1]], writes=[d0r[:, 0:1]])
    bd.ve(lambda e: e.tensor_tensor(d0r[:, 16:24], d0r[:, 16:24],
                                    sd[:, :, 32:33].rearrange("p a b -> p (a b)"),
                                    ALU.subtract),
          reads=[d0r[:, 0:1], sd[:, 0, 0:1]], writes=[d0r[:, 0:1]])
    import dataclasses as _dc
    om = sm["c_oddmask"][:, 0:1]
    om8 = _dc.replace(om, ap=[[1, 128], [0, 8]])
    bd.ve(lambda e, om8=om8: e.tensor_tensor(
        sd[:, :, 0:1].rearrange("p a b -> p (a b)"), d0r[:, 16:24], om8,
        ALU.mult),
        reads=[d0r[:, 0:1], om[0:1, 0:1]], writes=[sd[:, 0, 0:1]])
    for cc in range(8):
        bd.ve(lambda e, cc=cc: e.tensor_tensor_scan(
            sd[:, cc, 1:33], a_sb[:, cc, :], b_sb[:, cc, :], sd[:, cc, 0:1],
            ALU.mult, ALU.add),
            reads=[a_sb[:, 0, 0:1], b_sb[:, 0, 0:1]], writes=[sd[:, 0, 0:1]])
    bd.ve(lambda e: e.tensor_scalar(a_sb[:, :, :], sd[:, :, 0:32], -1.0, None,
                                    ALU.mult),
          reads=[sd[:, 0, 0:1]], writes=[a_sb[:, 0, 0:1]])
    v_scanB = mark("vector", S.vec)

    wait("gpsimd", S.stW, v_smw)
    wait("gpsimd", S.vec, v_scanB)
    src_sm = _dc.replace(d["h_stat"][12:13, :], ap=[[0, 128], [1, 2048]])
    v_smb = bd.qdma("gpsimd", S.brd, tab4[:, 0, :], src_sm)
    wait("vector", S.brd, v_smb)

    for m in range(16):
        for (pipe, rg, sl2) in ((pp_wa, 0, slice(0, 64)),
                                (pp_wb, 64, slice(64, 128))):
            ptw = pipe.begin()
            for cc in range(8):
                bd.raw("tensor", lambda e, ptw=ptw, cc=cc, m=m, rg=rg, sl2=sl2:
                       e.matmul(ptw[:, cc * 64:(cc + 1) * 64],
                                kbt[sl2, m, cc * 128:(cc + 1) * 128],
                                tts[sl2, m, :], start=True, stop=True,
                                tile_position=(rg, 0)))
            pipe.produced()
        for (pipe, rg, sl2) in ((pp_wa, 0, slice(0, 64)),
                                (pp_wb, 64, slice(64, 128))):
            ptu = pipe.begin()
            for cc in range(8):
                bd.raw("tensor", lambda e, ptu=ptu, cc=cc, m=m, rg=rg, sl2=sl2:
                       e.matmul(ptu[:, cc * 64:(cc + 1) * 64],
                                vbt[sl2, m, cc * 128:(cc + 1) * 128],
                                tts[sl2, m, :], start=True, stop=True,
                                tile_position=(rg, 0)))
            pipe.produced()
        for half, (pipe_w, pipe_u) in enumerate(((pp_wa, pp_wa),
                                                 (pp_wb, pp_wb))):
            n = 2 * m + half
            sln = slice(n * CC, (n + 1) * CC)
            nsdv = _dc.replace(a_sb[:, :, n:n + 1],
                               ap=[[8 * 33 if False else a_sb.shape[1] * a_sb.shape[2], 128],
                                   [a_sb.shape[2], 8], [0, 64]])
            sdv = _dc.replace(sd[:, :, n:n + 1],
                              ap=[[sd.shape[1] * sd.shape[2], 128],
                                  [sd.shape[2], 8], [0, 64]])
            pipe_w.consume("vector", lambda e, pt, nsdv=nsdv: e.tensor_tensor(
                usc[:, :, :], pt[:, :].rearrange("p (c i) -> p c i", c=8),
                nsdv, ALU.mult),
                reads=[a_sb[:, 0, 0:1]], writes=[usc[:, 0, 0:1]])
            pipe_u.consume("vector", lambda e, pt: e.tensor_tensor(
                usc[:, :, :], usc[:, :, :],
                pt[:, :].rearrange("p (c i) -> p c i", c=8), ALU.add),
                reads=[usc[:, 0, 0:1]], writes=[usc[:, 0, 0:1]])
            bd.ve(lambda e, sln=sln: e.tensor_tensor(
                usc[:, :, :], usc[:, :, :],
                _dc.replace(tab4[:, 0, sln], ap=[[4 * 2048, 128], [0, 8], [1, 64]]),
                ALU.mult),
                reads=[usc[:, 0, 0:1], tab4[:, 0, 0:1]], writes=[usc[:, 0, 0:1]])
            bd.ve(lambda e: e.tensor_tensor(
                usc[:, :, :], usc[:, :, :],
                _dc.replace(sm["c_cmask2"][:, 0:64], ap=[[64, 128], [0, 8], [1, 64]]),
                ALU.mult),
                reads=[usc[:, 0, 0:1], sm["c_cmask2"][:, 0:1]],
                writes=[usc[:, 0, 0:1]])
            bd.ve(lambda e, sln=sln, sdv=sdv: e.tensor_tensor(
                puw[:, :, :], qh[:, :, sln], sdv, ALU.mult),
                reads=[qh[:, 0, 0:1], sd[:, 0, 0:1]], writes=[puw[:, 0, 0:1]])
            bd.ve(lambda e, sln=sln: e.tensor_tensor(
                qh[:, :, sln], usc[:, :, :], puw[:, :, :], ALU.add),
                reads=[usc[:, 0, 0:1], puw[:, 0, 0:1]], writes=[qh[:, 0, 0:1]])
    v_pass2 = mark("vector", S.vec)
    bd.barrier()
    ps1.close()

    ps2 = contextlib.ExitStack()
    sqv = vw2(arA, 0, 8, 2048)
    x1p = vw2(arD, 0, 8, 2048)
    sq2 = vw2(arB, 0, 8, 2048)
    xres = vw2(arC, 0, 8, 2049)

    bd.se(lambda e: e.activation(sqv[:, :, :], qh[:, :, 0:2048], AF.Square),
          reads=[qh[:, 0, 0:1]], writes=[sqv[:, 0, 0:1]])
    v_osq = mark("scalar", S.act)
    pp_rms = PsumPipe(bd, "rms", (8, 512), 2, scope=ps2)
    wait("tensor", S.act, v_osq)
    for si, (t0, tn) in enumerate(STRIPS):
        pt = pp_rms.begin()
        for cc in range(8):
            bd.raw("tensor", lambda e, pt=pt, cc=cc, t0=t0, tn=tn:
                   e.matmul(pt[:, 0:tn], sm["c_dselq"][:, cc, 0:8],
                            sqv[:, cc, t0:t0 + tn],
                            start=(cc == 0), stop=(cc == 7)))
        pp_rms.produced()
        pp_rms.consume("vector", lambda e, pt, t0=t0, tn=tn: e.tensor_copy(
            stB[0:4, t0:t0 + tn], pt[0:4, 0:tn]), writes=[stB[:, :]])
    v_rmse = mark("vector", S.vec)
    wait("scalar", S.vec, v_rmse)
    bd.se(lambda e: e.activation(stB[0:4, :], stB[0:4, :], AF.Sqrt,
                                 bias=1e-5, scale=1.0 / HD),
          reads=[stB[:, :]], writes=[stB[:, :]])
    v_rmsq = mark("scalar", S.act)
    wait("vector", S.act, v_rmsq)
    bd.ve(lambda e: e.reciprocal(stB[0:4, :], stB[0:4, :]),
          reads=[stB[:, :]], writes=[stB[:, :]])
    v_rmsi = mark("vector", S.vec)
    wait("sync", S.vec, v_rmsi)
    v_rmw = bd.qdma("sync", S.stW, d["h_stat"][8:12, :], stB[0:4, :])
    wait("gpsimd", S.stW, v_rmw)
    v_tabr = bcast4(tab4, 8)
    wait("vector", S.brd, v_tabr)
    for cc in range(8):
        bd.ve(lambda e, cc=cc: e.scalar_tensor_tensor(
            qh[:, cc, :], qh[:, cc, :], sm["g_col"][:, cc % 2:cc % 2 + 1],
            tab4[:, cc // 2, :], ALU.mult, ALU.mult),
            reads=[qh[:, 0, 0:1], tab4[:, 0, 0:1], sm["g_col"][:, 0:1]],
            writes=[qh[:, 0, 0:1]])
    v_og = mark("vector", S.vec)
    wait("gpsimd", S.vec, v_og)
    v_xr = bd.qdma("gpsimd", S.xin, sqv[:, :, :], d["x_loc"][:, :, 3:])
    pp_wo = PsumPipe(bd, "pwo", (128, 512), 2, scope=ps2)
    wait("tensor", S.vec, v_og)
    wait("tensor", S.wg1, v_wo)
    wait("tensor", S.xin, v_xr)
    for f in range(8):
        for (t0, tn) in STRIPS:
            pt = pp_wo.begin()
            for k in range(8):
                bd.raw("tensor", lambda e, pt=pt, f=f, k=k, t0=t0, tn=tn:
                       e.matmul(pt[:, 0:tn], wB[:, k, f * 128:(f + 1) * 128],
                                qh[:, k, t0:t0 + tn],
                                start=(k == 0), stop=(k == 7)))
            pp_wo.produced()
            pp_wo.consume("vector", lambda e, pt, f=f, t0=t0, tn=tn:
                          e.scalar_tensor_tensor(
                              x1p[:, f, t0:t0 + tn], pt[:, 0:tn], 1.0,
                              sqv[:, f, t0:t0 + tn], ALU.mult, ALU.add),
                          reads=[sqv[:, 0, 0:1]], writes=[x1p[:, 0, 0:1]])
    v_x1p = mark("vector", S.vec)

    wait("scalar", S.vec, v_x1p)
    bd.se(lambda e: e.activation(sq2[:, :, :], x1p[:, :, :], AF.Square),
          reads=[x1p[:, 0, 0:1]], writes=[sq2[:, 0, 0:1]])
    v_x1sq = mark("scalar", S.act)
    pp_s1 = PsumPipe(bd, "ps1", (1, 512), 2, scope=ps2)
    pp_s2 = PsumPipe(bd, "ps2", (64, 512), 2, scope=ps2)
    wait("tensor", S.act, v_x1sq)
    for (t0, tn) in STRIPS:
        pt1 = pp_s1.begin()
        for cc in range(8):
            bd.raw("tensor", lambda e, pt=pt1, cc=cc, t0=t0, tn=tn:
                   e.matmul(pt[:, 0:tn], sm["c_ones"][:, 0:1],
                            x1p[:, cc, t0:t0 + tn],
                            start=(cc == 0), stop=(cc == 7)))
        pp_s1.produced()
        pp_s1.consume("vector", lambda e, pt, t0=t0, tn=tn: e.tensor_copy(
            stB[0:1, t0:t0 + tn], pt[0:1, 0:tn]), writes=[stB[:, :]])
        pt2 = pp_s2.begin()
        for cc in range(8):
            bd.raw("tensor", lambda e, pt=pt2, cc=cc, t0=t0, tn=tn:
                   e.matmul(pt[32:33, 0:tn], sm["c_ones"][:, 0:1],
                            sq2[:, cc, t0:t0 + tn],
                            start=(cc == 0), stop=(cc == 7),
                            tile_position=(0, 32)))
        pp_s2.produced()
        pp_s2.consume("vector", lambda e, pt, t0=t0, tn=tn: e.tensor_copy(
            stB[32:33, t0:t0 + tn], pt[32:33, 0:tn]), writes=[stB[:, :]])
    v_ln1 = mark("vector", S.vec)
    wait("sync", S.vec, v_ln1)
    bd.qdma("sync", S.stW, d["h_stat"][14:15, :], stB[0:1, :])
    v_lnw = bd.qdma("sync", S.stW, d["h_stat"][15:16, :], stB[32:33, :])
    wait("gpsimd", S.stW, v_lnw)
    import dataclasses as _dc2
    for r in range(2):
        src = _dc2.replace(d["h_stat"][14 + r:15 + r, :], ap=[[0, 128], [1, 2048]])
        bd.qdma("gpsimd", S.brd, tab4[:, r, :], src)
    v_mro = S.brd.count
    wait("vector", S.brd, v_mro)
    bd.ve(lambda e: e.tensor_scalar(tab4[:, 0, :], tab4[:, 0, :], 1.0 / D,
                                    None, ALU.mult),
          reads=[tab4[:, 0, 0:1]], writes=[tab4[:, 0, 0:1]])
    bd.ve(lambda e: e.tensor_tensor(tab4[:, 2, :], tab4[:, 0, :],
                                    tab4[:, 0, :], ALU.mult),
          reads=[tab4[:, 0, 0:1]], writes=[tab4[:, 0, 0:1]])
    bd.ve(lambda e: e.scalar_tensor_tensor(tab4[:, 1, :], tab4[:, 1, :],
                                           1.0 / D, tab4[:, 2, :],
                                           ALU.mult, ALU.subtract),
          reads=[tab4[:, 0, 0:1]], writes=[tab4[:, 0, 0:1]])
    v_var = mark("vector", S.vec)
    wait("scalar", S.vec, v_var)
    bd.se(lambda e: e.activation(tab4[:, 1, :], tab4[:, 1, :], AF.Sqrt,
                                 bias=1e-5),
          reads=[tab4[:, 0, 0:1]], writes=[tab4[:, 0, 0:1]])
    v_sd2 = mark("scalar", S.act)
    wait("vector", S.act, v_sd2)
    def _rec_t4(e):
        with bd.nc.allow_low_precision("bf16 LN rstd table"):
            e.reciprocal(tab4[:, 1, :], tab4[:, 1, :])
    bd.ve(_rec_t4, reads=[tab4[:, 0, 0:1]], writes=[tab4[:, 0, 0:1]])
    bd.ve(lambda e: e.tensor_tensor(tab4[:, 0, :], tab4[:, 0, :],
                                    tab4[:, 1, :], ALU.mult),
          reads=[tab4[:, 0, 0:1]], writes=[tab4[:, 0, 0:1]])
    v_tabready = mark("vector", S.vec)
    for cc in range(8):
        bd.ve(lambda e, cc=cc: e.tensor_tensor(
            xres[:, cc, 1:2049], x1p[:, cc, :], tab4[:, 1, :], ALU.mult),
            reads=[x1p[:, 0, 0:1], tab4[:, 0, 0:1]], writes=[xres[:, 0, 0:1]])
        bd.ve(lambda e, cc=cc: e.tensor_tensor(
            xres[:, cc, 1:2049], xres[:, cc, 1:2049], tab4[:, 0, :],
            ALU.subtract),
            reads=[xres[:, 0, 0:1], tab4[:, 0, 0:1]], writes=[xres[:, 0, 0:1]])
    v_x0 = mark("vector", S.vec)
    wait("gpsimd", S.vec, v_x0)
    bd.qdma("gpsimd", S.bx0, d["h_x0"].rearrange("p (a b) -> p a b", a=8),
            xres[:, :, 1:2049])
    bd.barrier()
    ps2.close()
    loc.close()


def bcast4(bd, tab_dst, src_row0, S):
    import dataclasses as _dc
    src = _dc.replace(bd.d["h_stat"][src_row0:src_row0 + 4, :],
                      ap=[[0, 64], [2048, 4], [1, 2048]])
    bd.qdma("gpsimd", S.brd, tab_dst[0:64, :, :], src)
    bd.qdma("gpsimd", S.brd, tab_dst[64:128, :, :], src)
    return S.brd.count


def emit_ffn(bd, arA, arB, arC, arD, wgt, tab4):
    import dataclasses as _dc
    nc, d, S, sm = bd.nc, bd.d, bd.S, bd.small

    def mark(eng, sem):
        return bd.release(eng, sem, 1)

    def wait(eng, sem, v):
        bd.raw(eng, lambda e, s=sem.h, v=v: e.wait_ge(s, v))

    xres = vw2(arC, 0, 8, 2049)
    qrot = vw2(arA, 0, 4, 2048)
    otm = vw2(arA, 4 * 2048, 16, 512)
    krot = vw2(arD, 0, 4, 2560)
    vtm = vw2(arB, 0, 20, VCOLS)
    prb = vw2(arD, 10240, 2, 640)
    prbT = vw2(arD, 10240 + 2 * 640, 2, 640)
    ofm = vw2(arD, 10240 + 4 * 640, 4, 512)
    xcolstage = vw2(arD, 10240 + 4 * 640 + 4 * 512, 8, 1)
    xhat = vw2(arA, 0, 8, 2048)
    sqf = vw2(arB, 0, 8, 2049)
    hgs = vw2(arB, 0, 22, 512)
    wqkv_v = vw2(wgt, 0, 8, 1536)
    wo_v = vw2(wgt, 12288, 4, 1024)
    w1bufs = [vw2(wgt, 0, 8, 256), vw2(wgt, 8 * 256, 8, 256)]
    w2W = vw2(wgt, 4096, 6, 1024)
    sttm = bd.sb["sttm"]
    stB = bd.sb["st16"]
    fsc = contextlib.ExitStack()
    bd.ffn_scope = fsc
    ikb = fsc.enter_context(bd.nc.sbuf_tensor("ikb", [128, 2, 2048], BF16))
    rcv = fsc.enter_context(bd.nc.sbuf_tensor("rcv", [128, 2, 520], BF16))
    halo2x = fsc.enter_context(bd.nc.sbuf_tensor("halo", [128, 8, 2], BF16))
    mro = vw2(arB, 22 * 512, 2, 2048)
    t_qc, t_qs, t_kc, t_ks = (tab4[:, 0, :], tab4[:, 1, :],
                              tab4[:, 2, :], tab4[:, 3, :])
    S = bd.S
    for i, nm in enumerate(("t_qc", "t_qs", "t_kc", "t_ks")):
        bd.qdma("sync", S.tload, tab4[:, i, :], bd.d[nm])
    v_rt = S.tload.count
    halo2 = halo2x
    idbf = sm["c_idbf"]

    rsc0 = vw2(arD, 14880, 1, 512)
    rsc = rsc0[:, 0, :]
    aw = vw2(arB, 15360, 2, 512)

    CH = []
    m = 0
    while m * 128 < HID:
        sz = min(128, HID - m * 128)
        CH.append(sz)
        m += 1
    NPAIR = len(CH)
    acol = []
    c0 = 0
    for m in range(NPAIR):
        acol.append(c0)
        c0 += 2 * CH[m]
    PIECES = []
    m = 0
    while m < NPAIR:
        PIECES.append(list(range(m, min(m + 2, NPAIR))))
        m += 2
    w2row = []
    r0 = 0
    for m in range(NPAIR):
        w2row.append(r0)
        r0 += CH[m]

    bd.ffn_sems = dict(vb=bd.sem("avb"), it=[1],
                       seV={}, tsV={}, oevV={}, avmV={},
                       dt3=bd.sem("adt3"), dt4=bd.sem("adt4"),
                       dt3V={}, dt4V={})
    bd.raw("vector", lambda e, v=S.bx0.count: e.wait_ge(S.bx0.h, v))
    for layer in range(2):
        ps = contextlib.ExitStack()
        v_wq = wload(bd, f"wqkv{layer}", wqkv_v[:, :, :],
                     d["wqkv"][layer].rearrange("(k p) j -> p k j", p=128),
                     S.wg0)
        v_wo = wload(bd, f"wo{layer}", wo_v[:, :, :],
                     d["wo_f"][layer].rearrange("(k p) j -> p k j", p=128),
                     S.wg1)

        pp_qk = PsumPipe(bd, f"fqk{layer}", (128, 512), 2, scope=ps, reg_key="fqk")
        pp_rot = PsumPipe(bd, f"frot{layer}", (128, 512), 2, scope=ps, reg_key="frot")
        kraw = vw2(arA, 4 * 2048, 4, 2048)

        def qkv_strips(strips, xcol_off, _ph=[0]):
            qsc = contextlib.ExitStack()
            _ph[0] += 1
            pp_ss = PsumPipe(bd, f"fss{layer}_{_ph[0]}", (16, 512), 1,
                             scope=qsc, reg_key="fss")
            pp_tp = PsumPipe(bd, f"ftp{layer}_{_ph[0]}", (128, 16), 1,
                             scope=qsc, reg_key="ftp")
            for f in range(8):
                dst = qrot if f < 4 else kraw
                dcc = f if f < 4 else f - 4
                for (t0, tn) in strips:
                    pt = pp_qk.begin()
                    for k in range(8):
                        src_off = (1 if k < 4 else 0) + t0
                        bd.raw("tensor", lambda e, pt=pt, f=f, k=k, t0=t0,
                               tn=tn, so=src_off:
                               e.matmul(pt[:, 0:tn], wqkv_v[:, k, f * 128:(f + 1) * 128],
                                        xres[:, k, so:so + tn],
                                        start=(k == 0), stop=(k == 7)))
                    pp_qk.produced()
                    pp_qk.consume("scalar", lambda e, pt, dst=dst, dcc=dcc,
                                  t0=t0, tn=tn:
                                  e.activation(dst[:, dcc, t0:t0 + tn],
                                               pt[:, 0:tn], AF.Copy),
                                  writes=[dst[:, 0, 0:1]])
            for (t0, tn) in strips:
                for tt in range(t0 // 128, (t0 + tn) // 128):
                    pt = pp_qk.begin()
                    for k in range(8):
                        so = (1 if k < 4 else 0) + tt * 128
                        bd.raw("tensor", lambda e, pt=pt, k=k, tt=tt, so=so:
                               e.matmul(pt[:, 0:512],
                                        xres[:, k, so:so + 128],
                                        wqkv_v[:, k, 1024:1536],
                                        start=(k == 0), stop=(k == 7)))
                    pp_qk.produced()
                    def ev_v(e, pt, tt=tt):
                        va = vtm[:, 4 + tt, :].rearrange(
                            "p (h dd) -> p h dd", dd=65)[:, :, 0:64]
                        e.activation(va, pt[:, 0:512].rearrange(
                            "p (h dd) -> p h dd", h=8), AF.Copy)
                    pp_qk.consume("scalar", ev_v, writes=[vtm[:, 0, 0:1]])
            for (t0, tn) in strips:
                pt = pp_ss.begin()
                for j in range(8):
                    srcv = (qrot[:, j, t0:t0 + tn] if j < 4
                            else kraw[:, j - 4, t0:t0 + tn])
                    slot = j % 2
                    if S.ev2.count >= 1 and (j >= 2 or S.ev2.count > 1):
                        pass
                    if S.ev1.count >= 2:
                        wait("scalar", S.ev2, S.ev2.count - 1)
                    bd.se(lambda e, srcv=srcv, slot=slot: e.activation(
                        aw[:, slot, 0:512], srcv, AF.Square),
                        reads=[srcv], writes=[aw[:, 0, 0:1]])
                    va = mark("scalar", S.ev1)
                    wait("tensor", S.ev1, va)
                    sel = sm["c_selq"] if j < 4 else sm["c_selk"]
                    cc = j % 4
                    bd.raw("tensor", lambda e, pt=pt, sel=sel, cc=cc, tn=tn,
                           slot=slot, j=j:
                           e.matmul(pt[:, 0:tn], sel[:, cc, :],
                                    aw[:, slot, 0:tn],
                                    start=(j == 0), stop=(j == 7)))
                    mark("tensor", S.ev2)
                pp_ss.produced()
                pp_ss.consume("vector", lambda e, pt, t0=t0, tn=tn:
                              e.tensor_copy(stB[0:16, t0:t0 + tn], pt[:, 0:tn]),
                              writes=[stB[:, :]])
            cr = slice(strips[0][0], strips[-1][0] + strips[-1][1])
            bd.ve(lambda e, cr=cr: e.reciprocal(stB[0:16, cr], stB[0:16, cr]),
                  reads=[stB[:, :]], writes=[stB[:, :]])
            v1 = mark("vector", S.vec)
            wait("scalar", S.vec, v1)
            bd.se(lambda e, cr=cr: e.activation(stB[0:16, cr], stB[0:16, cr],
                                                AF.Sqrt),
                  reads=[stB[:, :]], writes=[stB[:, :]])
            v2 = mark("scalar", S.act)
            wait("vector", S.act, v2)
            bd.ve(lambda e, cr=cr: e.tensor_scalar(
                stB[0:8, cr], stB[0:8, cr], 8.0, None, ALU.mult),
                reads=[stB[:, :]], writes=[stB[:, :]])
            v3 = mark("vector", S.vec)
            wait("tensor", S.vec, v3)
            for r in range(cr.start // 128, cr.stop // 128):
                pt = pp_tp.begin()
                bd.raw("tensor", lambda e, pt=pt, r=r: e.transpose(
                    pt[:, 0:8], stB[0:8, r * 128:(r + 1) * 128],
                    bd.idf32[0:8, 0:8]))
                pp_tp.produced()
                pp_tp.consume("vector", lambda e, pt, r=r: e.tensor_copy(
                    sttm[:, r, 0:8], pt[:, 0:8]), writes=[sttm[:, 0, 0:1]])
            wait("sync", S.vec, v3)
            vW = bd.qdma("sync", S.stW, d["h_stat"][0:8, cr], stB[8:16, cr])
            wait("gpsimd", S.stW, vW)
            n = cr.stop - cr.start
            wait("vector", S.tload, v_rt)
            for cc in range(4):
                for (t0, tn) in strips:
                    bd.ve(lambda e, cc=cc, t0=t0, tn=tn: e.tensor_scalar(
                        qrot[:, cc, t0:t0 + tn], qrot[:, cc, t0:t0 + tn],
                        sm["qs_col"][:, layer, 0:1], None, ALU.mult),
                        reads=[qrot[:, 0, 0:1]], writes=[qrot[:, 0, 0:1]])
            vq2 = mark("vector", S.vec)
            wait("tensor", S.vec, vq2)
            for cc in range(4):
                for (t0, tn) in strips:
                    pt = pp_rot.begin()
                    bd.raw("tensor", lambda e, pt=pt, cc=cc, t0=t0, tn=tn:
                           e.matmul(pt[:, 0:tn], sm["c_r2"][:, :],
                                    qrot[:, cc, t0:t0 + tn],
                                    start=True, stop=True))
                    pp_rot.produced()

                    def rotq_fn(e, pt, cc=cc, t0=t0, tn=tn):
                        e.tensor_tensor(rsc[:, 0:tn], pt[:, 0:tn],
                                        t_qs[:, t0:t0 + tn], ALU.mult)
                        i2 = e.tensor_tensor(qrot[:, cc, t0:t0 + tn],
                                             qrot[:, cc, t0:t0 + tn],
                                             t_qc[:, t0:t0 + tn], ALU.mult)
                        bd.fence_after(e, "vector", i2)
                        return e.tensor_tensor(qrot[:, cc, t0:t0 + tn],
                                               qrot[:, cc, t0:t0 + tn],
                                               rsc[:, 0:tn], ALU.add)
                    pp_rot.consume("vector", rotq_fn,
                        reads=[qrot[:, 0, 0:1]], writes=[qrot[:, 0, 0:1]])
            vqr = mark("vector", S.vec)
            for half in range(2):
                base = 2 * half
                if getattr(bd, "ikb_guard", None) is not None:
                    wait("gpsimd", S.vec, bd.ikb_guard)
                for (p0, row0) in ((0, 2 * base), (64, 2 * base + 1)):
                    srcx = _dc.replace(
                        d["h_stat"][row0:row0 + 1, cr],
                        ap=[[0, 64], [2 * 2048, 2], [1, n]])
                    bd.qdma("gpsimd", S.brd,
                            ikb[p0:p0 + 64, :, cr.start:cr.stop], srcx)
                vB = S.brd.count
                wait("vector", S.brd, vB)
                for cc in (base, base + 1):
                    for (t0, tn) in strips:
                        bd.ve(lambda e, cc=cc, t0=t0, tn=tn, base=base:
                              e.tensor_tensor(
                            kraw[:, cc, t0:t0 + tn], kraw[:, cc, t0:t0 + tn],
                            ikb[:, cc - base, t0:t0 + tn], ALU.mult),
                            reads=[kraw[:, 0, 0:1], ikb[:, 0, 0:1]],
                            writes=[kraw[:, 0, 0:1]])
                        bd.ve(lambda e, cc=cc, t0=t0, tn=tn: e.tensor_scalar(
                            kraw[:, cc, t0:t0 + tn], kraw[:, cc, t0:t0 + tn],
                            sm["ks_col"][:, layer, 0:1], None, ALU.mult),
                            reads=[kraw[:, 0, 0:1]], writes=[kraw[:, 0, 0:1]])
                bd.ikb_guard = mark("vector", S.vec)
            vk2 = mark("vector", S.vec)
            wait("tensor", S.vec, vk2)
            for cc in range(4):
                for (t0, tn) in strips:
                    pt = pp_rot.begin()
                    bd.raw("tensor", lambda e, pt=pt, cc=cc, t0=t0, tn=tn:
                           e.matmul(pt[:, 0:tn], sm["c_r2"][:, :],
                                    kraw[:, cc, t0:t0 + tn],
                                    start=True, stop=True))
                    pp_rot.produced()
                    def rotk_fn(e, pt, cc=cc, t0=t0, tn=tn):
                        e.tensor_tensor(rsc[:, 0:tn], pt[:, 0:tn],
                                        t_ks[:, t0:t0 + tn], ALU.mult)
                        i2 = e.tensor_tensor(krot[:, cc, 512 + t0:512 + t0 + tn],
                                             kraw[:, cc, t0:t0 + tn],
                                             t_kc[:, t0:t0 + tn], ALU.mult)
                        bd.fence_after(e, "vector", i2)
                        return e.tensor_tensor(
                            krot[:, cc, 512 + t0:512 + t0 + tn],
                            krot[:, cc, 512 + t0:512 + t0 + tn],
                            rsc[:, 0:tn], ALU.add)
                    pp_rot.consume("vector", rotk_fn,
                        reads=[kraw[:, 0, 0:1], krot[:, 0, 0:1]],
                        writes=[krot[:, 0, 0:1]])
            vkr = mark("vector", S.vec)
            qsc.close()
            return vqr, vkr

        pipes = dict(
            sim=pp_qk, simd=pp_rot, wo=pp_qk, layer=layer,
            vb=bd.ffn_sems["vb"], it=bd.ffn_sems["it"][0],
        )

        for h in range(8):
            bd.ve(lambda e, h=h: e.memset(vtm[:, :, h * 65 + 64:h * 65 + 65],
                                          1.0),
                  writes=[vtm[:, 0, 0:1]])
        v_ones = mark("vector", S.vec)

        wait("tensor", S.wg0, v_wq)
        vqr1, vkr1 = qkv_strips(STRIPS[1:], 1)
        bd.ve(lambda e: e.tensor_copy(xcolstage[:, :, 0:1], xres[:, :, 2048:2049]),
              reads=[xres[:, 0, 0:1]], writes=[xcolstage[:, 0, 0:1]])
        v_xcol = mark("vector", S.vec)
        wait("tensor", S.vec, max(vkr1, v_ones))
        emit_attention(bd, layer, range(8, 16), qrot, krot, vtm, otm, prb,
                       prbT, sttm, pipes)
        emit_wo_strips(bd, layer, (2, 3), otm, ofm, wo_v, xres, v_wo, pipes)
        emit_exchange(bd, layer, krot, vtm, xres, xcolstage, halo2, rcv)
        vqr2, vkr2 = qkv_strips(STRIPS[0:1], 1)
        wait("tensor", S.vec, vkr2)
        emit_attention(bd, layer, range(0, 8), qrot, krot, vtm, otm, prb,
                       prbT, sttm, pipes)
        emit_wo_strips(bd, layer, (0, 1), otm, ofm, wo_v, xres, v_wo, pipes)
        bd.barrier()
        bd.ve(lambda e: e.tensor_copy(xres[:, :, 0:1], halo2[:, :, 1:2]),
              reads=[halo2[:, 0, 0:1]], writes=[xres[:, 0, 0:1]])
        bd.barrier()
        w2D = vw2(arD, 0, 16, 1024)
        emit_mlp(bd, layer, xres, xhat, sqf, hgs, w1bufs, (w2D, w2W), mro, ps,
                 acol, CH, NPAIR, PIECES, w2row, aw)
        bd.barrier()
        ps.close()
    fsc.close()


def emit_attention(bd, layer, qtiles, qrot, krot, vtm, otm, prb, prbT, sttm,
                   pipes):
    S, sm = bd.S, bd.small
    asc = contextlib.ExitStack()
    if not hasattr(bd, "_attn_ph"):
        bd._attn_ph = 0
    bd._attn_ph += 1
    pp_s, pp_d = pipes["sim"], pipes["simd"]
    pp_t = PsumPipe(bd, f"fptr_{bd._attn_ph}", (128, 128), 2, dtype=BF16,
                    scope=asc, reg_key="fptr")
    pp_av = PsumPipe(bd, f"favp_{bd._attn_ph}", (128, 72), 1,
                     scope=asc, reg_key="favp")
    s_vb = pipes["vb"]
    seV, tsV = bd.ffn_sems["seV"], bd.ffn_sems["tsV"]
    oevV, avmV = bd.ffn_sems["oevV"], bd.ffn_sems["avmV"]
    dt3, dt4 = bd.ffn_sems["dt3"], bd.ffn_sems["dt4"]
    dt3V, dt4V = bd.ffn_sems["dt3V"], bd.ffn_sems["dt4V"]
    stc = sm["stcol"]

    def head(it, r, h, w, hb, ch, par, slot):
        ptS = pp_s.begin()
        bd.raw("tensor", lambda e, ptS=ptS, r=r, hb=hb, ch=ch:
               e.matmul(ptS[:, 0:512],
                        qrot[hb:hb + 64, ch, r * 128:(r + 1) * 128],
                        krot[hb:hb + 64, ch, r * 128:r * 128 + 512],
                        start=True, stop=False, tile_position=(hb, 0)))
        if w == 0:
            for kc in range(4):
                pat = sm["m_pfar0"] if kc == 0 else sm["m_pmid0"]
                bd.raw("tensor", lambda e, ptS=ptS, kc=kc, pat=pat:
                       e.matmul(ptS[:, kc * 128:(kc + 1) * 128],
                                sm["c_idbf"][:, :], pat[:, :],
                                start=False, stop=(kc == 3)))
        else:
            bd.raw("tensor", lambda e, ptS=ptS:
                   e.matmul(ptS[:, 0:128], sm["c_idbf"][:, :],
                            sm["m_far"][:, :], start=False, stop=True))
        pp_s.produced()
        ptD = pp_d.begin()
        bd.raw("tensor", lambda e, ptD=ptD, r=r, hb=hb, ch=ch:
               e.matmul(ptD[:, 0:128],
                        qrot[hb:hb + 64, ch, r * 128:(r + 1) * 128],
                        krot[hb:hb + 64, ch, r * 128 + 512:r * 128 + 640],
                        start=True, stop=False, tile_position=(hb, 0)))
        bd.raw("tensor", lambda e, ptD=ptD:
               e.matmul(ptD[:, 0:128], sm["c_idbf"][:, :],
                        sm["m_diag"][:, :], start=False, stop=True))
        pp_d.produced()

        if seV.get(it - 1):
            bd.raw("vector", lambda e, s=pp_d.cons.h, v=seV[it - 1]:
                   e.wait_ge(s, v))
        pp_s.wait_in("vector")
        pp_d.wait_in("vector")

        def vb_ops(e, ptS=ptS, ptD=ptD, par=par, r=r, h=h):
            e.tensor_reduce(stc[:, par:par + 1], ptS[:, 0:512], AX.X,
                            ALU.max, negate=True)
            i2 = e.tensor_reduce(stc[:, par + 1:par + 2], ptD[:, 0:128],
                                 AX.X, ALU.max, negate=True)
            bd.fence_after(e, "vector", i2)
            i3 = e.tensor_tensor(stc[:, par:par + 1], stc[:, par:par + 1],
                                 stc[:, par + 1:par + 2], ALU.min)
            bd.fence_after(e, "vector", i3)
            return e.tensor_tensor(stc[:, par + 2:par + 3],
                                   stc[:, par:par + 1],
                                   sttm[:, r, h:h + 1], ALU.mult)
        bd.ve(vb_ops, reads=[sttm[:, 0, 0:1]], writes=[stc[:, 0:8]])
        bd.release("vector", s_vb, 1)

        bd.raw("scalar", lambda e, v=s_vb.count: e.wait_ge(s_vb.h, v))
        if tsV.get(it - 2):
            bd.raw("scalar", lambda e, s=pp_t.prod.h, v=tsV[it - 2]:
                   e.wait_ge(s, v))
            bd.raw("scalar", lambda e, s=dt3.h, v=dt3V[it - 2]:
                   e.wait_ge(s, v))
            bd.raw("scalar", lambda e, s=dt4.h, v=dt4V[it - 2]:
                   e.wait_ge(s, v))
        pp_s.cons.count += 1
        pp_d.cons.count += 1

        def exp_ops(e, ptS=ptS, ptD=ptD, slot=slot, par=par, r=r, h=h,
                    v1=pp_s.cons.count, v2=pp_d.cons.count):
            i1 = e.activation(prb[:, slot, 0:512], ptS[:, 0:512], AF.Exp,
                              bias=stc[:, par + 2:par + 3],
                              scale=sttm[:, r, h:h + 1])
            i1.then_inc(pp_s.cons.h, 1)
            bd._upd[id(i1)] = (i1, pp_s.cons.h, v1)
            i2 = e.activation(prb[:, slot, 512:640], ptD[:, 0:128], AF.Exp,
                              bias=stc[:, par + 2:par + 3],
                              scale=sttm[:, r, h:h + 1])
            i2.then_inc(pp_d.cons.h, 1)
            bd._upd[id(i2)] = (i2, pp_d.cons.h, v2)
            return i2
        bd.se(exp_ops, reads=[stc[:, 0:8], sttm[:, 0, 0:1]],
              writes=[prb[:, 0, 0:1]])
        seV[it] = pp_d.cons.count

        for (eng, sem, vdict, kc) in (("scalar", dt3, dt3V, 3),
                                      ("scalar", dt4, dt4V, 4)):
            prior = sem.count
            sem.count += 16
            vdict[it] = sem.count

            def trig(e, prior=prior, s=sem.h, kc=kc, slot=slot,
                     sev=seV[it], war=avmV.get(it - 2)):
                if prior:
                    e.wait_ge(s, prior)
                e.wait_ge(pp_d.cons.h, sev)
                if war:
                    e.wait_ge(pp_av.prod.h, war)
                e.dma_start_transpose(
                    prbT[:, slot, kc * 128:(kc + 1) * 128],
                    prb[:, slot, kc * 128:(kc + 1) * 128]).then_inc(s, 16)
            bd.raw(eng, trig)

    def tail(it, r, h, par, slot):
        bd.raw("tensor", lambda e, s=pp_d.cons.h, v=seV[it]:
               e.wait_ge(s, v))
        if avmV.get(it - 1):
            bd.raw("vector", lambda e, s=pp_av.prod.h, v=avmV[it - 1]:
                   e.wait_ge(s, v))
        for kc in range(3):
            ptT = pp_t.begin()
            bd.raw("tensor", lambda e, ptT=ptT, kc=kc, slot=slot:
                   e.transpose(ptT[:, :],
                               prb[:, slot, kc * 128:(kc + 1) * 128],
                               sm["c_idbf"][:, :]))
            pp_t.produced()
            pp_t.consume("vector", lambda e, pt, slot=slot, kc=kc:
                         e.tensor_copy(prbT[:, slot,
                                            kc * 128:(kc + 1) * 128],
                                       pt[:, :]),
                         writes=[prbT[:, 0, 0:1]])
        tsV[it] = pp_t.prod.count
        oevV[it] = pp_t.cons.count

        bd.raw("tensor", lambda e, s=pp_t.cons.h, v=oevV[it]:
               e.wait_ge(s, v))
        bd.raw("tensor", lambda e, s=dt3.h, v=dt3V[it]: e.wait_ge(s, v))
        bd.raw("tensor", lambda e, s=dt4.h, v=dt4V[it]: e.wait_ge(s, v))
        ptA = pp_av.begin()
        for kc in range(5):
            bd.raw("tensor", lambda e, ptA=ptA, kc=kc, slot=slot, r=r, h=h:
                   e.matmul(ptA[:, 0:65],
                            prbT[:, slot, kc * 128:(kc + 1) * 128],
                            vtm[:, r + kc, h * 65:(h + 1) * 65],
                            start=(kc == 0), stop=(kc == 4)))
        pp_av.produced()
        avmV[it] = pp_av.prod.count

        def av_ops(e, pt, par=par, r=r, h=h):
            i1 = e.reciprocal(stc[:, par + 3:par + 4], pt[:, 64:65])
            bd.fence_after(e, "vector", i1)
            return e.tensor_scalar(otm[:, r, h * 64:(h + 1) * 64],
                                   pt[:, 0:64],
                                   stc[:, par + 3:par + 4], None, ALU.mult)
        pp_av.consume("vector", av_ops, reads=[stc[:, 0:8]],
                      writes=[otm[:, 0, 0:1], stc[:, 0:8]])

    pend = [None]
    for r in qtiles:
        w = r // 4
        for h in range(8):
            hb = 64 * (h % 2)
            ch = h // 2
            it = bd.ffn_sems["it"][0]
            bd.ffn_sems["it"][0] = it + 1
            par = 4 * (it % 2)
            slot = it % 2
            head(it, r, h, w, hb, ch, par, slot)
            if pend[0] is not None:
                pend[0]()
            pend[0] = (lambda it=it, r=r, h=h, par=par, slot=slot:
                       tail(it, r, h, par, slot))
    pend[0]()
    bd.raw("tensor", lambda e, s=pp_av.cons.h, v=pp_av.cons.count:
           e.wait_ge(s, v))
    bd.raw("tensor", lambda e, s=pp_t.cons.h, v=pp_t.cons.count:
           e.wait_ge(s, v))
    asc.close()

def emit_wo_strips(bd, layer, strips_idx, otm, ofm, wo_v, xres, v_wo, pipes):
    S, sm = bd.S, bd.small
    wsc = contextlib.ExitStack()
    if not hasattr(bd, "_wo_ph"):
        bd._wo_ph = 0
    bd._wo_ph += 1
    pp_ot = PsumPipe(bd, f"fot_{bd._wo_ph}", (128, 128), 2, dtype=BF16,
                     scope=wsc, reg_key="fptr")
    pp_wo = pipes["wo"]

    def mark(eng, sem):
        return bd.release(eng, sem, 1)

    def wait(eng, sem, v):
        bd.raw(eng, lambda e, s=sem.h, v=v: e.wait_ge(s, v))

    wait("tensor", S.wg1, v_wo)
    for si in strips_idx:
        t0 = si * 512
        for tt in range(4):
            for fc in range(4):
                pt = pp_ot.begin()
                bd.raw("tensor", lambda e, pt=pt, tt=tt, fc=fc, si=si:
                       e.transpose(pt[:, :],
                                   otm[:, si * 4 + tt, fc * 128:(fc + 1) * 128],
                                   sm["c_idbf"][:, :]))
                pp_ot.produced()
                pp_ot.consume("vector", lambda e, pt, tt=tt, fc=fc:
                              e.tensor_copy(ofm[:, fc, tt * 128:(tt + 1) * 128],
                                            pt[:, :]),
                              writes=[ofm[:, 0, 0:1]])
        v_of = mark("vector", S.vec)
        wait("tensor", S.vec, v_of)
        for f in range(8):
            pt = pp_wo.begin()
            for k in range(4):
                bd.raw("tensor", lambda e, pt=pt, f=f, k=k:
                       e.matmul(pt[:, 0:512], wo_v[:, k, f * 128:(f + 1) * 128],
                                ofm[:, k, :], start=(k == 0), stop=(k == 3)))
            pp_wo.produced()
            pp_wo.consume("vector", lambda e, pt, f=f, t0=t0:
                          e.scalar_tensor_tensor(
                              xres[:, f, 1 + t0:1 + t0 + 512], pt[:, 0:512],
                              1.0, xres[:, f, 1 + t0:1 + t0 + 512],
                              ALU.mult, ALU.add),
                          reads=[xres[:, 0, 0:1]], writes=[xres[:, 0, 0:1]])
    wsc.close()


def emit_exchange(bd, layer, krot, vtm, xres, xcolstage, halo2, rcv):
    import dataclasses as _dc
    S, sm = bd.S, bd.small
    d = bd.d

    def mark(eng, sem):
        return bd.release(eng, sem, 1)

    def wait(eng, sem, v):
        bd.raw(eng, lambda e, s=sem.h, v=v: e.wait_ge(s, v))

    bd.barrier()
    bd.qdma("sync", S.tp1,
            d["e_in"][:, 0:2048].rearrange("p (a b) -> p a b", a=4),
            krot[:, :, 2048:2560])
    bd.qdma("sync", S.tp1,
            d["e_in"][:, 2048:4128].rearrange("p (a b) -> p a b", a=4),
            vtm[:, 16:20, :])
    bd.qdma("sync", S.tp1,
            d["e_in"][:, 4128:4136].rearrange("p (a b) -> p a b", a=8),
            xcolstage[:, :, :], allow_slow_non_contiguous=True)
    v_snd = bd.qdma("sync", S.tp1,
                    d["e_in"][:, 4136:4144].rearrange("p (a b) -> p a b", a=8),
                    xres[:, :, 2048:2049], allow_slow_non_contiguous=True)
    wait("gpsimd", S.tp1, v_snd)
    S.coll.count += 1
    bd.on("gpsimd", lambda e: e.collective_compute(
        "AllGather", ALU.bypass,
        replica_groups=[[0, 1], [2, 3], [4, 5], [6, 7]],
        ins=[d["e_in"]], outs=[d["e_out"]]).then_inc(S.coll.h, 1))
    v_cc = S.coll.count
    wait("gpsimd", S.coll, v_cc)
    pieces = []
    for ch in range(4):
        pieces.append((ch * 512, 512, krot[:, ch, 0:512],
                       krot[:, ch, 2048:2560]))
    for gg in range(4):
        pieces.append((2048 + gg * VCOLS, VCOLS, vtm[:, gg, :],
                       vtm[:, 16 + gg, :]))
    pieces.append((4128, 16, halo2[:, :, :].rearrange("p a b -> p (a b)"),
                   None))
    for (off, n, dst, mine) in pieces:
        if getattr(bd, "rcv_guard", None) is not None:
            wait("gpsimd", S.vec, bd.rcv_guard)
        bd.qdma("gpsimd", S.gdma, rcv[:, 0, 0:n],
                d["e_out"][0][:, off:off + n])
        vg = bd.qdma("gpsimd", S.gdma, rcv[:, 1, 0:n],
                     d["e_out"][1][:, off:off + n])
        wait("vector", S.gdma, vg)
        if mine is not None:
            bd.ve(lambda e, n=n: e.tensor_tensor(
                rcv[:, 0, 0:n], rcv[:, 0, 0:n], rcv[:, 1, 0:n], ALU.add),
                reads=[rcv[:, 0, 0:1]], writes=[rcv[:, 0, 0:1]])
            bd.ve(lambda e, n=n, dst=dst, mine=mine: e.tensor_tensor(
                dst, rcv[:, 0, 0:n], mine, ALU.subtract),
                reads=[rcv[:, 0, 0:1]], writes=[dst])
        else:
            bd.ve(lambda e, n=n: e.tensor_tensor(
                rcv[:, 0, 0:n], rcv[:, 0, 0:n], rcv[:, 1, 0:n], ALU.add),
                reads=[rcv[:, 0, 0:1]], writes=[rcv[:, 0, 0:1]])
            bd.ve(lambda e: e.tensor_tensor(
                rcv[:, 0, 0:8], rcv[:, 0, 0:8],
                xcolstage[:, :, 0].rearrange("p a -> p a"), ALU.subtract),
                reads=[rcv[:, 0, 0:1]], writes=[rcv[:, 0, 0:1]])
            bd.ve(lambda e: e.tensor_tensor(
                rcv[:, 0, 8:16], rcv[:, 0, 8:16],
                xres[:, :, 2048:2049].rearrange("p a b -> p (a b)"),
                ALU.subtract),
                reads=[rcv[:, 0, 0:1], xres[:, 0, 0:1]],
                writes=[rcv[:, 0, 0:1]])
            om8 = _dc.replace(sm["c_oddmask"][:, 0:1], ap=[[1, 128], [0, 16]])
            bd.ve(lambda e, om8=om8: e.tensor_tensor(
                halo2[:, :, :].rearrange("p a b -> p (a b)"), rcv[:, 0, 0:16],
                om8, ALU.mult),
                reads=[rcv[:, 0, 0:1], sm["c_oddmask"][0:1, 0:1]],
                writes=[halo2[:, 0, 0:1]])
            bd.ve(lambda e: e.tensor_copy(
                xres[:, :, 0:1], halo2[:, :, 0:1]),
                reads=[halo2[:, 0, 0:1]], writes=[xres[:, 0, 0:1]])
        bd.rcv_guard = mark("vector", S.vec)
    v_unp = mark("vector", S.vec)
    bd.barrier()


def emit_mlp(bd, layer, xres, xhat, sqf, hgs, w1bufs, w2parts, mro, ps,
             acol, CH, NPAIR, PIECES, w2row, aw):
    import dataclasses as _dc
    nc, d, S, sm = bd.nc, bd.d, bd.S, bd.small
    stB = bd.sb["st16"]

    def mark(eng, sem):
        return bd.release(eng, sem, 1)

    def wait(eng, sem, v):
        bd.raw(eng, lambda e, s=sem.h, v=v: e.wait_ge(s, v))

    ps.close()
    mps = contextlib.ExitStack()
    sps = contextlib.ExitStack()

    w2D, w2W = w2parts
    for m in range(NPAIR):
        dst = w2D[:, m, :] if m < 16 else w2W[:, m - 16, :]
        r0, nr = w2row[m], CH[m]
        src = d["w2"][layer][r0:r0 + nr, :]
        bd.qdma("gpsimd", S.sdma, dst[0:nr, :], src)
    v_w2 = S.sdma.count

    bd.se(lambda e: e.activation(sqf[:, :, :], xres[:, :, 0:2049], AF.Square),
          reads=[xres[:, 0, 0:1]], writes=[sqf[:, 0, 0:1]])
    v_sq = mark("scalar", S.act)
    pp_s1 = PsumPipe(bd, f"ms1_{layer}", (1, 512), 2, scope=sps, reg_key="ms1")
    pp_s2 = PsumPipe(bd, f"ms2_{layer}", (64, 512), 2, scope=sps, reg_key="ms2")
    wait("tensor", S.act, v_sq)
    for (t0, tn) in STRIPS:
        pt1 = pp_s1.begin()
        for cc in range(8):
            so = (1 if cc < 4 else 0) + t0
            bd.raw("tensor", lambda e, pt=pt1, cc=cc, so=so, tn=tn:
                   e.matmul(pt[:, 0:tn], sm["c_ones"][:, 0:1],
                            xres[:, cc, so:so + tn],
                            start=(cc == 0), stop=(cc == 7)))
        pp_s1.produced()
        pp_s1.consume("vector", lambda e, pt, t0=t0, tn=tn: e.tensor_copy(
            stB[0:1, t0:t0 + tn], pt[0:1, 0:tn]), writes=[stB[:, :]])
        pt2 = pp_s2.begin()
        for cc in range(8):
            so = (1 if cc < 4 else 0) + t0
            bd.raw("tensor", lambda e, pt=pt2, cc=cc, so=so, tn=tn:
                   e.matmul(pt[32:33, 0:tn], sm["c_ones"][:, 0:1],
                            sqf[:, cc, so:so + tn],
                            start=(cc == 0), stop=(cc == 7),
                            tile_position=(0, 32)))
        pp_s2.produced()
        pp_s2.consume("vector", lambda e, pt, t0=t0, tn=tn: e.tensor_copy(
            stB[32:33, t0:t0 + tn], pt[32:33, 0:tn]), writes=[stB[:, :]])
    v_ln = mark("vector", S.vec)
    wait("sync", S.vec, v_ln)
    bd.qdma("sync", S.stW, d["h_stat"][14:15, :], stB[0:1, :])
    v_lnw = bd.qdma("sync", S.stW, d["h_stat"][15:16, :], stB[32:33, :])
    wait("gpsimd", S.stW, v_lnw)
    for r in range(2):
        src = _dc.replace(d["h_stat"][14 + r:15 + r, :],
                          ap=[[0, 128], [1, 2048]])
        bd.qdma("gpsimd", S.brd, mro[:, r, :], src)
    v_mro = S.brd.count
    wait("vector", S.brd, v_mro)
    msc = mro
    mscr = vw2(mro.tensor if False else sqf.tensor, 0, 1, 2048)
    bd.ve(lambda e: e.tensor_scalar(mro[:, 0, :], mro[:, 0, :], 1.0 / D,
                                    None, ALU.mult),
          reads=[mro[:, 0, 0:1]], writes=[mro[:, 0, 0:1]])
    bd.ve(lambda e: e.tensor_tensor(mscr[:, 0, :], mro[:, 0, :],
                                    mro[:, 0, :], ALU.mult),
          reads=[mro[:, 0, 0:1]], writes=[mscr[:, 0, 0:1]])
    bd.ve(lambda e: e.scalar_tensor_tensor(mro[:, 1, :], mro[:, 1, :],
                                           1.0 / D, mscr[:, 0, :],
                                           ALU.mult, ALU.subtract),
          reads=[mro[:, 0, 0:1], mscr[:, 0, 0:1]], writes=[mro[:, 0, 0:1]])
    v_var = mark("vector", S.vec)
    wait("scalar", S.vec, v_var)
    bd.se(lambda e: e.activation(mro[:, 1, :], mro[:, 1, :], AF.Sqrt,
                                 bias=1e-5),
          reads=[mro[:, 0, 0:1]], writes=[mro[:, 0, 0:1]])
    v_sd = mark("scalar", S.act)
    wait("vector", S.act, v_sd)
    def _rec_mro(e):
        with bd.nc.allow_low_precision("bf16 LN rstd table"):
            e.reciprocal(mro[:, 1, :], mro[:, 1, :])
    bd.ve(_rec_mro, reads=[mro[:, 0, 0:1]], writes=[mro[:, 0, 0:1]])
    bd.ve(lambda e: e.tensor_tensor(mro[:, 0, :], mro[:, 0, :],
                                    mro[:, 1, :], ALU.mult),
          reads=[mro[:, 0, 0:1]], writes=[mro[:, 0, 0:1]])
    v_tabr = mark("vector", S.vec)
    for cc in range(8):
        so = 1 if cc < 4 else 0
        bd.ve(lambda e, cc=cc, so=so: e.tensor_tensor(
            xhat[:, cc, :], xres[:, cc, so:so + 2048], mro[:, 1, :], ALU.mult),
            reads=[xres[:, 0, 0:1], mro[:, 0, 0:1]], writes=[xhat[:, 0, 0:1]])
        bd.ve(lambda e, cc=cc: e.tensor_tensor(
            xhat[:, cc, :], xhat[:, cc, :], mro[:, 0, :], ALU.subtract),
            reads=[xhat[:, 0, 0:1], mro[:, 0, 0:1]], writes=[xhat[:, 0, 0:1]])
        bd.ve(lambda e, cc=cc: e.tensor_scalar(
            xhat[:, cc, :], xhat[:, cc, :],
            sm["ln_g_col"][:, layer, cc:cc + 1],
            sm["ln_b_col"][:, layer, cc:cc + 1], ALU.mult, ALU.add),
            reads=[xhat[:, 0, 0:1]], writes=[xhat[:, 0, 0:1]])
    v_xh = mark("vector", S.vec)

    bd.barrier()
    sps.close()
    pp_a = PsumPipe(bd, f"mpa{layer}", (128, 512), 2, scope=mps, reg_key="mpa")
    pp_g = PsumPipe(bd, f"mpg{layer}", (128, 512), 2, scope=mps, reg_key="mpg")
    pp_o = PsumPipe(bd, f"mpo{layer}", (128, 512), 2, scope=mps, reg_key="mpo")
    if not hasattr(bd, "mlp_sems"):
        bd.mlp_sems = (bd.sem("hg"), bd.sem("w1a"), bd.sem("w1b"))
    hsem, w1a, w1b = bd.mlp_sems
    wsem = [w1a, w1b]
    wait("tensor", S.vec, v_xh)
    wait("tensor", S.sdma, v_w2)

    seq = [(si, m) for si in range(4) for m in range(NPAIR)]
    hs_base = hsem.count
    mm2V = {}
    for idx, (si, m) in enumerate(seq):
        t0 = si * 512
        buf = idx % 2
        nch = CH[m]
        c0 = acol[m]
        if idx >= 2:
            bd.raw("gpsimd", lambda e, s=pp_g.prod.h, v=mm2V[idx - 2]:
                   e.wait_ge(s, v))
        v_l = wload(bd, f"w1_{layer}_{idx}", w1bufs[buf][:, :, 0:2 * nch],
                    d["w1p"][layer][:, c0:c0 + 2 * nch]
                    .rearrange("(k p) j -> p k j", p=128), wsem[buf])
        wait("tensor", wsem[buf], v_l)
        ptA = pp_a.begin()
        for k in range(8):
            bd.raw("tensor", lambda e, pt=ptA, k=k, nch=nch, buf=buf, t0=t0:
                   e.matmul(pt[0:nch, 0:512], w1bufs[buf][:, k, 0:nch],
                            xhat[:, k, t0:t0 + 512],
                            start=(k == 0), stop=(k == 7)))
        pp_a.produced()
        ptG = pp_g.begin()
        for k in range(8):
            bd.raw("tensor", lambda e, pt=ptG, k=k, nch=nch, buf=buf, t0=t0:
                   e.matmul(pt[0:nch, 0:512], w1bufs[buf][:, k, nch:2 * nch],
                            xhat[:, k, t0:t0 + 512],
                            start=(k == 0), stop=(k == 7)))
        pp_g.produced()
        mm2V[idx] = pp_g.prod.count
        if idx >= 1:
            wait("scalar", hsem, hs_base + idx)
        pp_a.consume("scalar", lambda e, pt, m=m, nch=nch:
                     e.activation(aw[0:nch, 0, :], pt[0:nch, 0:512],
                                  AF.Identity,
                                  bias=sm["b1p_col"][0:nch, layer, 2 * m:2 * m + 1]),
                     writes=[aw[:, 0, 0:1]])
        pp_g.consume("scalar", lambda e, pt, m=m, nch=nch:
                     e.activation(aw[0:nch, 1, :], pt[0:nch, 0:512],
                                  AF.Gelu,
                                  bias=sm["b1p_col"][0:nch, layer, 2 * m + 1:2 * m + 2]),
                     writes=[aw[:, 0, 0:1]])
        sv = mark("scalar", S.act)
        wait("vector", S.act, sv)
        bd.ve(lambda e, m=m, nch=nch: e.tensor_tensor(
            hgs[0:nch, m, :], aw[0:nch, 0, :], aw[0:nch, 1, :],
            ALU.mult),
            reads=[aw[:, 0, 0:1]], writes=[hgs[:, 0, 0:1]])
        bd.release("vector", hsem, 1)

        if m == NPAIR - 1:
            wait("tensor", hsem, hsem.count)
            for f in range(8):
                ptO = pp_o.begin()
                for mm in range(NPAIR):
                    w2v = w2parts[0][:, mm, :] if mm < 16 else \
                        w2parts[1][:, mm - 16, :]
                    bd.raw("tensor", lambda e, pt=ptO, mm=mm, f=f, w2v=w2v,
                           nck=CH[mm]:
                           e.matmul(pt[:, 0:512],
                                    w2v[0:nck, f * 128:(f + 1) * 128],
                                    hgs[0:nck, mm, :],
                                    start=(mm == 0), stop=(mm == NPAIR - 1)))
                pp_o.produced()
                pp_o.consume("vector", lambda e, pt, f=f, t0=t0:
                             e.scalar_tensor_tensor(
                                 xres[:, f, 1 + t0:1 + t0 + 512], pt[:, 0:512],
                                 sm["b2_col"][:, layer, f:f + 1],
                                 xres[:, f, 1 + t0:1 + t0 + 512],
                                 ALU.add, ALU.add),
                             reads=[xres[:, 0, 0:1], sm["b2_col"][:, 0, 0:1]],
                             writes=[xres[:, 0, 0:1]])
    mps.close()


def emit_final(bd, arA, arB, arC, arD):
    import dataclasses as _dc
    nc, d, S, sm = bd.nc, bd.d, bd.S, bd.small

    def mark(eng, sem):
        return bd.release(eng, sem, 1)

    def wait(eng, sem, v):
        bd.raw(eng, lambda e, s=sem.h, v=v: e.wait_ge(s, v))

    fps = contextlib.ExitStack()
    xres = vw2(arC, 0, 8, 2049)
    x0v = vw2(arA, 0, 8, 2048)
    zv = vw2(arD, 0, 8, 2048)
    ztm = vw2(arB, 0, 16, 1024)
    fst = bd.sbuf("fst", (128, 2, 1024), F32)
    stB = bd.sb["st16"]

    bd.barrier()
    v_x0 = bd.qdma("gpsimd", S.xin, x0v[:, :, :],
                   d["h_x0"].rearrange("p (a b) -> p a b", a=8))
    wait("vector", S.xin, v_x0)
    bd.ve(lambda e: e.tensor_tensor(zv[:, :, :], x0v[:, :, :],
                                    xres[:, :, 1:2049], ALU.add),
          reads=[x0v[:, 0, 0:1], xres[:, 0, 0:1]], writes=[zv[:, 0, 0:1]])
    v_z = mark("vector", S.vec)
    pp_ft = PsumPipe(bd, "fin_t", (128, 128), 4, dtype=BF16, scope=fps)
    wait("tensor", S.vec, v_z)
    for cc in range(8):
        for tt in range(16):
            pt = pp_ft.begin()
            bd.raw("tensor", lambda e, pt=pt, cc=cc, tt=tt: e.transpose(
                pt[:, :], zv[:, cc, tt * 128:(tt + 1) * 128],
                sm["c_idbf"][:, :]))
            pp_ft.produced()
            pp_ft.consume("vector", lambda e, pt, cc=cc, tt=tt: e.tensor_copy(
                ztm[:, tt, cc * 128:(cc + 1) * 128], pt[:, :]),
                writes=[ztm[:, 0, 0:1]])
    v_ztm = mark("vector", S.vec)
    wait("scalar", S.vec, v_ztm)
    sdv = bd.sb["sd_sb"]
    for tt in range(16):
        par = tt % 2
        bd.ve(lambda e, tt=tt, par=par: e.scalar_tensor_tensor(
            fst[:, par, :], ztm[:, tt, :], 1.0, ztm[:, tt, :],
            ALU.mult, ALU.mult,
            accum_out=sdv[:, 0, tt:tt + 1]),
            reads=[ztm[:, 0, 0:1]], writes=[fst[:, 0, 0:1]])
        bd.ve(lambda e, tt=tt: e.tensor_reduce(
            sdv[:, 1, tt:tt + 1], ztm[:, tt, :], AX.X, ALU.add),
            reads=[ztm[:, 0, 0:1]], writes=[bd.sb["sd_sb"][:, 0, 0:1]])
    bd.ve(lambda e: e.tensor_scalar(
        sdv[:, 1, 0:16], sdv[:, 1, 0:16], 1.0 / D, None, ALU.mult),
        reads=[sdv[:, 0, 0:1]], writes=[sdv[:, 0, 0:1]])
    bd.ve(lambda e: e.tensor_scalar(
        sdv[:, 0, 0:16], sdv[:, 0, 0:16], 1.0 / D, None, ALU.mult),
        reads=[sdv[:, 0, 0:1]], writes=[sdv[:, 0, 0:1]])
    bd.ve(lambda e: e.tensor_tensor(
        sdv[:, 2, 0:16], sdv[:, 1, 0:16], sdv[:, 1, 0:16], ALU.mult),
        reads=[sdv[:, 0, 0:1]], writes=[sdv[:, 0, 0:1]])
    bd.ve(lambda e: e.tensor_tensor(
        sdv[:, 0, 0:16], sdv[:, 0, 0:16], sdv[:, 2, 0:16], ALU.subtract),
        reads=[sdv[:, 0, 0:1]], writes=[sdv[:, 0, 0:1]])
    v_v = mark("vector", S.vec)
    wait("scalar", S.vec, v_v)
    bd.se(lambda e: e.activation(
        sdv[:, 0, 0:16], sdv[:, 0, 0:16], AF.Sqrt, bias=1e-5),
        reads=[sdv[:, 0, 0:1]], writes=[sdv[:, 0, 0:1]])
    v_s = mark("scalar", S.act)
    wait("vector", S.act, v_s)
    bd.ve(lambda e: e.reciprocal(sdv[:, 0, 0:16], sdv[:, 0, 0:16]),
          reads=[sdv[:, 0, 0:1]], writes=[sdv[:, 0, 0:1]])
    bd.ve(lambda e: e.tensor_tensor(
        sdv[:, 2, 0:16], sdv[:, 1, 0:16], sdv[:, 0, 0:16], ALU.mult),
        reads=[sdv[:, 0, 0:1]], writes=[sdv[:, 0, 0:1]])
    bd.ve(lambda e: e.tensor_scalar(
        sdv[:, 2, 0:16], sdv[:, 2, 0:16], -1.0, None, ALU.mult),
        reads=[sdv[:, 0, 0:1]], writes=[sdv[:, 0, 0:1]])
    v_r = mark("vector", S.vec)
    wait("scalar", S.vec, v_r)
    for tt in range(16):
        par = tt % 2
        if tt >= 2:
            osem = S.outs if par == 0 else S.out2
            wait("scalar", osem, 16 * (tt // 2))
        bd.se(lambda e, tt=tt, par=par: e.activation(
            fst[:, par, :], ztm[:, tt, :], AF.Identity,
            bias=sdv[:, 2, tt:tt + 1], scale=sdv[:, 0, tt:tt + 1]),
            reads=[ztm[:, 0, 0:1], sdv[:, 0, 0:1]], writes=[fst[:, 0, 0:1]])
        v_n = mark("scalar", S.act)
        if par == 0:
            wait("sync", S.act, v_n)
            bd.qdma("sync", S.outs, bd.d["out"][tt * 128:(tt + 1) * 128, :],
                    fst[:, par, :])
        else:
            wait("gpsimd", S.act, v_n)
            bd.qdma("gpsimd", S.out2,
                    bd.d["out"][tt * 128:(tt + 1) * 128, :], fst[:, par, :])
    bd.raw("sync", lambda e, v=S.outs.count: e.wait_ge(S.outs.h, v))
    bd.raw("sync", lambda e, v=S.out2.count: e.wait_ge(S.out2.h, v))
    bd.raw("gpsimd", lambda e, v=S.out2.count: e.wait_ge(S.out2.h, v))


_CACHE = {}


def _pack_weights(inputs):
    w = {}
    w["Wq"] = inputs["L0_attn_Wq"]
    w["Wk"] = inputs["L0_attn_Wk"]
    w["Wv"] = inputs["L0_attn_Wv"]
    w["Wb"] = inputs["L0_attn_Wb"]
    w["convs"] = np.ascontiguousarray(np.stack([
        inputs["L0_attn_conv_q"][:, 0, :],
        inputs["L0_attn_conv_k"][:, 0, :],
        inputs["L0_attn_conv_v"][:, 0, :]]))
    g = np.asarray(inputs["L0_attn_g"])
    w["g_col"] = np.ascontiguousarray(g.reshape(2, 128).T)
    w["Wo"] = inputs["L0_attn_Wo"]
    w["wqkv"] = inputs["L0_ffn_wqkv"]
    w["qs_col"] = np.ascontiguousarray(
        np.tile(np.asarray(inputs["L0_ffn_q_scale"]), (1, 2)).T[:, :, None])
    w["ks_col"] = np.ascontiguousarray(
        np.tile(np.asarray(inputs["L0_ffn_k_scale"]), (1, 2)).T[:, :, None])
    w["wo_f"] = inputs["L0_ffn_wo"]
    w["ln_g_col"] = np.ascontiguousarray(
        np.asarray(inputs["L0_ffn_ln_g"]).reshape(2, 8, 128).transpose(2, 0, 1))
    w["ln_b_col"] = np.ascontiguousarray(
        np.asarray(inputs["L0_ffn_ln_b"]).reshape(2, 8, 128).transpose(2, 0, 1))
    w1p, b1p = permute_w1(np.asarray(inputs["L0_ffn_w1"]),
                          np.asarray(inputs["L0_ffn_b1"]))
    w["w1p"] = np.ascontiguousarray(w1p)
    CHl = []
    m = 0
    while m * 128 < HID:
        CHl.append(min(128, HID - m * 128))
        m += 1
    b1c = np.zeros((128, 2, 44), np.float32)
    c0 = 0
    for m, nch in enumerate(CHl):
        b1c[0:nch, :, 2 * m] = b1p[:, c0:c0 + nch].T
        b1c[0:nch, :, 2 * m + 1] = b1p[:, c0 + nch:c0 + 2 * nch].T
        c0 += 2 * nch
    w["b1p_col"] = b1c
    w["w2"] = inputs["L0_ffn_w2"]
    w["b2_col"] = np.ascontiguousarray(
        np.asarray(inputs["L0_ffn_b2"]).reshape(2, 8, 128).transpose(2, 0, 1))
    return w


def kernel(**inputs):
    if "bd" not in _CACHE:
        _CACHE["bd"] = build_program()
        _CACHE["tabs"] = host_tables()
    bd = _CACHE["bd"]
    tabs = _CACHE["tabs"]

    x = np.asarray(inputs["L0_attn_x"], dtype=np.float32)
    w = _pack_weights(inputs)
    w = {k: np.ascontiguousarray(np.asarray(v, dtype=np.float32))
         if np.asarray(v).dtype == np.float32 or
         np.asarray(v).dtype == np.float64 else v for k, v in w.items()}

    in_names = [a.memorylocations[0].name
                for a in bd.nc.m.functions[0].allocations
                if getattr(a, "kind", None) == "ExternalInput"]
    in_maps = []
    for c in range(NCORES):
        bb, hh = c // 2, c % 2
        s = hh * T
        xl = np.zeros((T + 3, D), np.float32)
        lo = max(0, s - 3)
        xl[3 - (s - lo):3] = x[bb, lo:s]
        xl[3:] = x[bb, s:s + T]
        feed = dict(w)
        feed["x_loc"] = np.ascontiguousarray(
            xl.reshape(T + 3, 8, 128).transpose(2, 1, 0))
        feed.update(tabs[c])
        m = {}
        for nm in in_names:
            if nm == "partition_id":
                continue
            assert nm in feed, f"missing input {nm}"
            m[nm] = np.ascontiguousarray(feed[nm])
        in_maps.append(m)

    res = run_bass_kernel_spmd(bd.nc, in_maps, core_ids=list(range(NCORES)))
    out = np.zeros((B, L, D), np.float32)
    for c in range(NCORES):
        bb, hh = c // 2, c % 2
        out[bb, hh * T:(hh + 1) * T] = np.asarray(res.results[c]["out"])
    return out
